# version 08
# baseline (speedup 1.0000x reference)
import numpy as np
from contextlib import ExitStack

import concourse.bass as bass
import concourse.mybir as mybir
from concourse.bass_utils import run_bass_kernel_spmd

F32 = mybir.dt.float32
BF16 = mybir.dt.bfloat16
AF = mybir.ActivationFunctionType
ALU = mybir.AluOpType
AX = mybir.AxisListType.X

D = 2048
T = 4096
TH = 2048
NKC = D // 128
NTB = T // 512
NTBH = TH // 512


class Sem:
    def __init__(self, nc, ctx, name):
        self.h = ctx.enter_context(nc.semaphore(name))
        self.n = 0

    def inc(self, instr, k=1):
        instr.then_inc(self.h, k)
        self.n += k
        return self.n

    def dma(self, instr):
        return self.inc(instr, 16)

    def wait(self, eng, at=None):
        eng.wait_ge(self.h, self.n if at is None else at)


class Env:
    def __init__(self, nc):
        self.nc = nc
        self.ctx = ExitStack()
        self.t = {}
        self.sems = {}
        self._scopes = [self.ctx]

    _ROLES = ("ldA", "ldB", "rlA", "rlB", "pe", "ve", "se", "st", "ld", "wl",
              "ge", "ms", "rl", "ul", "el", "tl", "cp", "cc")

    def sem(self, name):
        key = name
        for r in self._ROLES:
            if name.endswith(r):
                key = "@" + r
                break
        if key not in self.sems:
            self.sems[key] = Sem(self.nc, self.ctx, key.replace("@", "g_"))
        return self.sems[key]

    def phase(self):
        from contextlib import contextmanager

        @contextmanager
        def _p():
            st = ExitStack()
            self._scopes.append(st)
            try:
                yield st
            finally:
                self._scopes.pop()
                st.close()
        return _p()

    def sb(self, name, shape, dt):
        return self._scopes[-1].enter_context(self.nc.sbuf_tensor(name, shape, dt))

    def ps(self, name, shape, dt=F32):
        return self._scopes[-1].enter_context(self.nc.psum_tensor(name, shape, dt))

    def dram(self, name, shape, dt, kind=None):
        if kind is None:
            h = self.nc.dram_tensor(name, shape, dt)
        else:
            h = self.nc.dram_tensor(name, shape, dt, kind=kind)
        self.t[name] = h.ap()
        return self.t[name]



def host_prep(inputs):
    import ml_dtypes
    f32 = np.float32
    asc = np.ascontiguousarray
    bf16 = ml_dtypes.bfloat16
    ins = {k: np.asarray(v) for k, v in inputs.items()}

    ident = np.eye(128).astype(bf16)
    ident32 = np.eye(128, dtype=f32)
    af = (1.0 / 1024.0) ** np.linspace(0.0, 1.0, 32, dtype=f32)
    af = np.concatenate([af, np.zeros(32, f32)])
    theta = np.arange(T, dtype=f32)[:, None] * af[None, :]
    cosT = asc(np.cos(theta).T.astype(f32))
    sinT = asc(np.sin(theta).T.astype(f32))
    p = np.arange(128)[:, None]
    f = np.arange(512)[None, :]
    mask4 = np.concatenate([(f >= p + off).astype(f32) for off in (0, 128, 256, 384)],
                           axis=1).astype(bf16)

    def vec_pk(v):
        n = v.shape[0] // 128
        return asc(v.reshape(n, 128).T.astype(f32))

    cores = []
    for c in range(8):
        b, j = c // 2, c % 2
        hs = slice(1024 * j, 1024 * (j + 1))
        hs1 = slice(512 * j, 512 * (j + 1))
        hv1 = slice(1024 * j, 1024 * (j + 1))
        d = {
            "xT": asc(ins["L0_attn_x"][b].T.astype(f32)),
            "ident": ident, "ident32": ident32, "cosT": cosT, "sinT": sinT, "mask4": mask4,
            "l0_wq": asc(ins["L0_attn_Wq"].T[:, hs]),
            "l0_wk": asc(ins["L0_attn_Wk"].T[:, hs]),
            "l0_wv": asc(ins["L0_attn_Wv"].T[:, hs]),
            "l0_wo": asc(ins["L0_attn_Wo"].T),
            "l0f_ln1": vec_pk(ins["L0_ffn_ln1_w"]),
            "l0f_ln2": vec_pk(ins["L0_ffn_ln2_w"]),
            "l0f_wq": asc(ins["L0_ffn_wq"][:, hs]),
            "l0f_wk": asc(ins["L0_ffn_wk"][:, hs]),
            "l0f_wv": asc(ins["L0_ffn_wv"][:, hs]),
            "l0f_wo": ins["L0_ffn_wo"],
            "l0f_gate": asc(ins["L0_ffn_gate_w"].T),
            "l0f_egT": asc(ins["L0_ffn_eg_w"].reshape(16, 11, 128, 16, 128)
                           .transpose(0, 1, 4, 3, 2).reshape(16, 11, 128, 2048)),
            "l0f_euT": asc(ins["L0_ffn_eu_w"].reshape(16, 11, 128, 16, 128)
                           .transpose(0, 1, 4, 3, 2).reshape(16, 11, 128, 2048)),
            "l0f_edT": asc(ins["L0_ffn_ed_w"].reshape(16, 16, 128, 11, 128)
                           .transpose(0, 1, 4, 3, 2).reshape(16, 16, 128, 1408)),
            "l0f_sgT": asc(ins["L0_ffn_sg_w"].reshape(22, 128, 16, 128)
                           .transpose(0, 3, 2, 1).reshape(22, 128, 2048)),
            "l0f_suT": asc(ins["L0_ffn_su_w"].reshape(22, 128, 16, 128)
                           .transpose(0, 3, 2, 1).reshape(22, 128, 2048)),
            "l0f_sdT": asc(ins["L0_ffn_sd_w"].T),
            "l1_wq": asc(ins["L1_attn_Wq"][:, hs1]),
            "l1_wk": asc(ins["L1_attn_Wk"][:, hs1]),
            "l1_wkg1": ins["L1_attn_Wkg1"],
            "l1_wkg2": asc(ins["L1_attn_Wkg2"][:, hs1]),
            "l1_bkg2": vec_pk(ins["L1_attn_bkg2"][hs1]),
            "l1_wv": asc(ins["L1_attn_Wv"][:, hv1]),
            "l1_wg": asc(ins["L1_attn_Wg"][:, hv1]),
            "l1_bg": asc(ins["L1_attn_bg"][hv1].astype(f32).reshape(1, 1024)),
            "l1_wo": ins["L1_attn_Wo"],
            "l1f_wT": asc(ins["L1_ffn_weight"].T),
            "l1f_rT": asc(ins["L1_ffn_rram_variation"].T),
        }
        cores.append(d)
    return cores



def emit_ln(env, block, pre, src_name, dst_bf=None, dst_f32=None, n_tb=NTB,
            eps=1e-5, weight=None, rms=False, wait_sems=(), done_sem=None,
            src_cols=None):
    nc = env.nc
    x = env.t[src_name]
    xnb = env.t[dst_bf] if dst_bf else None
    xnf = env.t[dst_f32] if dst_f32 else None
    xs = env.sb(pre + "_x", [128, NKC * 512], F32)
    sq = env.sb(pre + "_sq", [128, NKC * 512], BF16)
    onesb = env.sb(pre + "_ones", [128, 1], BF16)
    onesf = env.sb(pre + "_onesf", [128, 1], F32)
    ones1 = env.sb(pre + "_ones1", [1, 128], F32)
    wsb = env.sb(pre + "_w", [128, NKC], F32) if weight is not None else None
    stats = env.sb(pre + "_stats", [1, 512 * 6], F32)
    rsb = env.sb(pre + "_r", [128, 1024], F32)
    tmp = env.sb(pre + "_tmp", [128, NKC * 512], F32)
    outf = env.sb(pre + "_outf", [128, NKC * 512], F32)
    outw = env.sb(pre + "_outw", [128, NKC * 512], F32) if weight is not None else None
    outb = env.sb(pre + "_outb", [128, NKC * 512], BF16)
    ps_s = env.ps(pre + "_ps_s", [1, 512])
    ps_s2 = env.ps(pre + "_ps_s2", [1, 512])
    ps_r = env.ps(pre + "_ps_r", [128, 1024])

    ld = env.sem(pre + "ld")
    pe = env.sem(pre + "pe")
    ve = env.sem(pre + "ve")
    se = env.sem(pre + "se")
    st = env.sem(pre + "st")

    @block.gpsimd
    def _(g):
        for _ws in wait_sems:
            _ws.wait(g)
        g.memset(onesb[:], 1.0)
        g.memset(onesf[:], 1.0)
        env.sem(pre + "ge").inc(g.memset(ones1[:], 1.0))
    ge = env.sem(pre + "ge")
    ge_const = ge.n

    if weight is not None:
        w = env.t[weight]

        @block.sync
        def _(sy):
            for _ws in wait_sems:
                _ws.wait(sy)
            ld.dma(sy.dma_start(out=wsb[:], in_=w[:, :]))

    first = [True]
    st_after = [0]
    for tb in range(n_tb):
        ve_app_prev = ve.n
        se_out_prev = se.n
        st_prev_tb = st_after[0]

        @block.sync
        def _(sy):
            if first[0]:
                for wsem in wait_sems:
                    wsem.wait(sy)
                first[0] = False
            else:
                sy.wait_ge(ve.h, ve_app_prev)
                sy.wait_ge(se.h, se_out_prev)
            for kc in range(NKC):
                ld.dma(sy.dma_start(out=xs[:, kc * 512:(kc + 1) * 512],
                                    in_=x[kc * 128:(kc + 1) * 128, tb * 512:(tb + 1) * 512]))
        ld_tb = ld.n

        @block.scalar
        def _(s):
            if tb == 0:
                for _ws in wait_sems:
                    _ws.wait(s)
            s.wait_ge(ld.h, ld_tb)
            for kc in range(NKC):
                ii = s.activation(out=sq[:, kc * 512:(kc + 1) * 512],
                                  in_=xs[:, kc * 512:(kc + 1) * 512], func=AF.Square)
            se.inc(ii)
        se_sq = se.n

        @block.tensor
        def _(t):
            if tb == 0:
                for _ws in wait_sems:
                    _ws.wait(t)
            t.wait_ge(ge.h, ge_const)
            t.wait_ge(ld.h, ld_tb)
            if tb > 0:
                t.wait_ge(ve.h, ve_app_prev)
            if not rms:
                for kc in range(NKC):
                    ii = t.matmul(ps_s[:], onesf[:], xs[:, kc * 512:(kc + 1) * 512],
                                  start=(kc == 0), stop=(kc == NKC - 1))
            t.wait_ge(se.h, se_sq)
            for kc in range(NKC):
                ii = t.matmul(ps_s2[:], onesb[:], sq[:, kc * 512:(kc + 1) * 512],
                              start=(kc == 0), stop=(kc == NKC - 1))
            pe.inc(ii)
        pe_stats = pe.n

        mu = stats[:, 0:512]
        musq = stats[:, 512:1024]
        var = stats[:, 1024:1536]
        lnv = stats[:, 1536:2048]
        rstd = stats[:, 2048:2560]
        nmr = stats[:, 2560:3072]

        @block.vector
        def _(v):
            if tb == 0:
                for _ws in wait_sems:
                    _ws.wait(v)
            v.wait_ge(pe.h, pe_stats)
            if not rms:
                ve.inc(v.tensor_scalar(out=mu[:], in0=ps_s[:], scalar1=1.0 / D,
                                       scalar2=None, op0=ALU.mult))
                v.wait_ge(ve.h, ve.n)
                ve.inc(v.tensor_mul(out=musq[:], in0=mu[:], in1=mu[:]))
                ve.inc(v.tensor_scalar(out=var[:], in0=ps_s2[:], scalar1=1.0 / D,
                                       scalar2=eps, op0=ALU.mult, op1=ALU.add))
                v.wait_ge(ve.h, ve.n)
                ve.inc(v.tensor_sub(out=var[:], in0=var[:], in1=musq[:]))
            else:
                ve.inc(v.tensor_scalar(out=var[:], in0=ps_s2[:], scalar1=1.0 / D,
                                       scalar2=eps, op0=ALU.mult, op1=ALU.add))
        ve_var = ve.n

        @block.scalar
        def _(s):
            s.wait_ge(ve.h, ve_var)
            se.inc(s.activation(out=lnv[:], in_=var[:], func=AF.Ln))
        se_ln = se.n

        @block.vector
        def _(v):
            v.wait_ge(se.h, se_ln)
            ve.inc(v.tensor_scalar(out=lnv[:], in0=lnv[:], scalar1=-0.5, scalar2=None,
                                   op0=ALU.mult))
        ve_lnv = ve.n

        @block.scalar
        def _(s):
            s.wait_ge(ve.h, ve_lnv)
            se.inc(s.activation(out=rstd[:], in_=lnv[:], func=AF.Exp))
        se_rstd = se.n

        @block.vector
        def _(v):
            v.wait_ge(se.h, se_rstd)
            if not rms:
                ii = v.scalar_tensor_tensor(out=nmr[:], in0=mu[:],
                                            scalar=-1.0, in1=rstd[:],
                                            op0=ALU.mult, op1=ALU.mult)
                ve.inc(ii)
        ve_nmr = ve.n

        @block.tensor
        def _(t):
            t.wait_ge(se.h, se_rstd)
            t.wait_ge(ve.h, ve_nmr)
            ii = t.matmul(ps_r[:, 0:512], ones1[:], rstd[:], start=True, stop=True)
            if not rms:
                ii = t.matmul(ps_r[:, 512:1024], ones1[:], nmr[:], start=True, stop=True)
            pe.inc(ii)
        pe_bc = pe.n

        @block.vector
        def _(v):
            v.wait_ge(pe.h, pe_bc)
            ve.inc(v.tensor_copy(out=rsb[:, 0:512], in_=ps_r[:, 0:512]))
            if not rms:
                ve.inc(v.tensor_copy(out=rsb[:, 512:1024], in_=ps_r[:, 512:1024]))
            v.wait_ge(ve.h, ve.n)
            if tb > 0:
                v.wait_ge(se.h, se_out_prev)
            if rms:
                for kc in range(NKC):
                    sl = slice(kc * 512, (kc + 1) * 512)
                    ii = v.tensor_tensor(out=outf[:, sl], in0=xs[:, sl], in1=rsb[:, 0:512],
                                         op=ALU.mult)
            else:
                for kc in range(NKC):
                    sl = slice(kc * 512, (kc + 1) * 512)
                    ii = v.tensor_tensor(out=tmp[:, sl], in0=xs[:, sl], in1=rsb[:, 0:512],
                                         op=ALU.mult)
                ve.inc(ii)
                v.wait_ge(ve.h, ve.n)
                for kc in range(NKC):
                    sl = slice(kc * 512, (kc + 1) * 512)
                    ii = v.tensor_tensor(out=outf[:, sl], in0=tmp[:, sl], in1=rsb[:, 512:1024],
                                         op=ALU.add)
            ve.inc(ii)
        ve_app = ve.n

        @block.scalar
        def _(s):
            s.wait_ge(ve.h, ve_app)
            if tb > 0:
                s.wait_ge(st.h, st_prev_tb)
            for kc in range(NKC):
                sl = slice(kc * 512, (kc + 1) * 512)
                if weight is not None:
                    ii = s.activation(out=outb[:, sl], in_=outf[:, sl], func=AF.Copy,
                                      scale=wsb[:, kc:kc + 1])
                    if xnf is not None:
                        ii = s.activation(out=outw[:, sl], in_=outf[:, sl], func=AF.Copy,
                                          scale=wsb[:, kc:kc + 1])
                else:
                    ii = s.activation(out=outb[:, sl], in_=outf[:, sl], func=AF.Copy)
            se.inc(ii)
        se_out = se.n

        @block.sync
        def _(sy):
            sy.wait_ge(se.h, se_out)
            for kc in range(NKC):
                if xnb is not None:
                    st.dma(sy.dma_start(out=xnb[kc * 128:(kc + 1) * 128, tb * 512:(tb + 1) * 512],
                                        in_=outb[:, kc * 512:(kc + 1) * 512]))
                if xnf is not None:
                    st.dma(sy.dma_start(out=xnf[kc * 128:(kc + 1) * 128, tb * 512:(tb + 1) * 512],
                                        in_=(outw if weight is not None else outf)[:, kc * 512:(kc + 1) * 512]))
            if tb == n_tb - 1:
                sy.wait_ge(st.h, st.n)
                if done_sem is not None:
                    done_sem.inc(sy.nop())
        st_after[0] = st.n
    return st



def emit_qkv_rope(env, block, pre, src, wq, wk, wv, q_out, k_out, v_out,
                  rope=True, wait_sems=(), done_sem=None, dout=1024):
    nc = env.nc
    x = env.t[src]
    qf, kf, vt = env.t[q_out], env.t[k_out], env.t[v_out]
    nmo = dout // 128
    ndvb = dout // 512
    wq_sb = env.sb(pre + "_wq", [128, NKC * dout], BF16)
    wk_sb = env.sb(pre + "_wk", [128, NKC * dout], BF16)
    wv_sb = env.sb(pre + "_wv", [128, NKC * dout], BF16)
    xsb = env.sb(pre + "_x", [128, 2 * NKC * 512], BF16)
    cos_sb = env.sb(pre + "_cos", [64, 2 * 512], F32) if rope else None
    sin_sb = env.sb(pre + "_sin", [64, 2 * 512], F32) if rope else None
    qsb = env.sb(pre + "_q", [128, nmo * 512], BF16)
    ksb = env.sb(pre + "_k", [128, nmo * 512], BF16)
    vsb = env.sb(pre + "_v", [128, 4 * ndvb * 512], BF16)
    rtmp = env.sb(pre + "_rt", [128, 2 * 512], F32)
    psq = [env.ps(pre + f"_psq{i}", [128, 512]) for i in range(4)]
    psv = [env.ps(pre + f"_psv{i}", [128, 512]) for i in range(2)]

    wl = env.sem(pre + "wl")
    ld = env.sem(pre + "ld")
    pe = env.sem(pre + "pe")
    ve = env.sem(pre + "ve")
    se = env.sem(pre + "se")
    st = env.sem(pre + "st")

    @block.gpsimd
    def _(g):
        for _ws in wait_sems:
            _ws.wait(g)
        for w_sb, wname in ((wq_sb, wq), (wk_sb, wk), (wv_sb, wv)):
            w = env.t[wname]
            for kc in range(NKC):
                wl.dma(g.dma_start(out=w_sb[:, kc * dout:(kc + 1) * dout],
                                   in_=w[kc * 128:(kc + 1) * 128, :]))

    nqk = 2 * nmo
    nv = 4 * ndvb
    first = [True]
    st_prev = [0]
    for tb in range(NTB):
        buf = tb % 2
        xoff = buf * NKC * 512
        ve_prev_tb = ve.n
        se_prev_tb = se.n
        st_prev_tb = st_prev[0]

        @block.sync
        def _(sy):
            if first[0]:
                for wsem in wait_sems:
                    wsem.wait(sy)
                first[0] = False
            if tb >= 2:
                sy.wait_ge(pe.h, pe.n - (nqk + nv))
            if tb >= 1:
                sy.wait_ge(ld.h, ld.n)
            for kc in range(NKC):
                ld.dma(sy.dma_start(out=xsb[:, xoff + kc * 512:xoff + (kc + 1) * 512],
                                    in_=x[kc * 128:(kc + 1) * 128, tb * 512:(tb + 1) * 512]))
            if rope:
                ld.dma(sy.dma_start(out=cos_sb[:, buf * 512:(buf + 1) * 512],
                                    in_=env.t["cosT"][:, tb * 512:(tb + 1) * 512]))
                ld.dma(sy.dma_start(out=sin_sb[:, buf * 512:(buf + 1) * 512],
                                    in_=env.t["sinT"][:, tb * 512:(tb + 1) * 512]))
        ld_tb = ld.n

        pe_base = pe.n

        @block.tensor
        def _(t):
            if tb == 0:
                for _ws in wait_sems:
                    _ws.wait(t)
            wl.wait(t)
            t.wait_ge(ld.h, ld_tb)
            g = 0
            for w_sb in (wq_sb, wk_sb):
                for mo in range(nmo):
                    bank = psq[g % 4]
                    if g >= 4:
                        t.wait_ge(ve.h, ve_prev_tb + 2 * (g - 4) + 2)
                    else:
                        t.wait_ge(ve.h, ve_prev_tb)
                    for kc in range(NKC):
                        ii = t.matmul(bank[:],
                                      w_sb[:, kc * dout + mo * 128:kc * dout + (mo + 1) * 128],
                                      xsb[:, xoff + kc * 512:xoff + (kc + 1) * 512],
                                      start=(kc == 0), stop=(kc == NKC - 1))
                    pe.inc(ii)
                    g += 1
            gv = 0
            for tt in range(4):
                for dvb in range(ndvb):
                    bank = psv[gv % 2]
                    if gv >= 2:
                        t.wait_ge(se.h, se_prev_tb + (gv - 2) + 1)
                    else:
                        t.wait_ge(se.h, se_prev_tb)
                    for kc in range(NKC):
                        ii = t.matmul(bank[:],
                                      xsb[:, xoff + kc * 512 + tt * 128:xoff + kc * 512 + (tt + 1) * 128],
                                      wv_sb[:, kc * dout + dvb * 512:kc * dout + (dvb + 1) * 512],
                                      start=(kc == 0), stop=(kc == NKC - 1))
                    pe.inc(ii)
                    gv += 1

        @block.vector
        def _(v):
            if tb == 0:
                for _ws in wait_sems:
                    _ws.wait(v)
            v.wait_ge(st.h, st_prev_tb)
            g = 0
            for sbuf in (qsb, ksb):
                for mo in range(nmo):
                    v.wait_ge(pe.h, pe_base + g + 1)
                    bank = psq[g % 4]
                    osl = slice(mo * 512, (mo + 1) * 512)
                    if rope:
                        c = cos_sb[:, buf * 512:(buf + 1) * 512]
                        s_ = sin_sb[:, buf * 512:(buf + 1) * 512]
                        t1 = rtmp[:, 0:512]
                        t2 = rtmp[:, 512:1024]
                        v.wait_ge(ve.h, ve.n)
                        v.tensor_tensor(out=t1[0:64, :], in0=bank[0:64, :], in1=c, op=ALU.mult)
                        v.tensor_tensor(out=t2[0:64, :], in0=bank[64:128, :], in1=s_, op=ALU.mult)
                        v.tensor_tensor(out=t1[64:128, :], in0=bank[64:128, :], in1=c, op=ALU.mult)
                        ve.inc(v.tensor_tensor(out=t2[64:128, :], in0=bank[0:64, :], in1=s_, op=ALU.mult))
                        v.wait_ge(ve.h, ve.n)
                        v.tensor_add(out=sbuf[0:64, osl], in0=t1[0:64, :], in1=t2[0:64, :])
                        ii = v.tensor_sub(out=sbuf[64:128, osl], in0=t1[64:128, :], in1=t2[64:128, :])
                    else:
                        ve.inc(v.tensor_copy(out=sbuf[:, osl], in_=bank[:]))
                        ii = v.engine_nop()
                    ve.inc(ii)
                    g += 1

        @block.scalar
        def _(s):
            if tb == 0:
                for _ws in wait_sems:
                    _ws.wait(s)
            s.wait_ge(st.h, st_prev_tb)
            gv = 0
            for tt in range(4):
                for dvb in range(ndvb):
                    s.wait_ge(pe.h, pe_base + nqk + gv + 1)
                    bank = psv[gv % 2]
                    ii = s.activation(out=vsb[:, (tt * ndvb + dvb) * 512:(tt * ndvb + dvb + 1) * 512],
                                      in_=bank[:], func=AF.Copy)
                    se.inc(ii)
                    gv += 1

        ve_tb = ve.n
        se_tb = se.n

        @block.sync
        def _(sy):
            sy.wait_ge(ve.h, ve_tb)
            for mo in range(nmo):
                st.dma(sy.dma_start(out=qf[mo * 128:(mo + 1) * 128, tb * 512:(tb + 1) * 512],
                                    in_=qsb[:, mo * 512:(mo + 1) * 512]))
                st.dma(sy.dma_start(out=kf[mo * 128:(mo + 1) * 128, tb * 512:(tb + 1) * 512],
                                    in_=ksb[:, mo * 512:(mo + 1) * 512]))
            sy.wait_ge(se.h, se_tb)
            for tt in range(4):
                for dvb in range(ndvb):
                    st.dma(sy.dma_start(
                        out=vt[tb * 512 + tt * 128:tb * 512 + (tt + 1) * 128,
                               dvb * 512:(dvb + 1) * 512],
                        in_=vsb[:, (tt * ndvb + dvb) * 512:(tt * ndvb + dvb + 1) * 512]))
            if tb == NTB - 1:
                sy.wait_ge(st.h, st.n)
                if done_sem is not None:
                    done_sem.inc(sy.nop())
        st_prev[0] = st.n
    return st



def emit_stab(env, block, pre, q_name, k_name, negb_name, scale, nheads=8,
              wait_sems=(), done_sem=None):
    nc = env.nc
    qf, kf = env.t[q_name], env.t[k_name]
    x_sb = env.sb(pre + "_x", [128, 4 * 512], BF16)
    sq = env.sb(pre + "_sq", [128, 2 * 512], BF16)
    onesb = env.sb(pre + "_ones", [128, 1], BF16)
    ones1 = env.sb(pre + "_ones1", [1, 128], F32)
    srow = env.sb(pre + "_srow", [1, 2 * nheads * NTB], F32)
    brow = env.sb(pre + "_brow", [1, 2 * nheads], F32)
    nrow = env.sb(pre + "_nrow", [1, nheads], F32)
    nb_sb = env.sb(pre + "_nb", [128, nheads], F32)
    ps_n = [env.ps(pre + f"_psn{i}", [1, 512]) for i in range(4)]
    ps_b = env.ps(pre + "_psb", [128, nheads])

    ld = env.sem(pre + "ld")
    pe = env.sem(pre + "pe")
    ve = env.sem(pre + "ve")
    se = env.sem(pre + "se")
    ge = env.sem(pre + "ge")
    st = env.sem(pre + "st")

    @block.gpsimd
    def _(g):
        for _ws in wait_sems:
            _ws.wait(g)
        g.memset(onesb[:], 1.0)
        ge.inc(g.memset(ones1[:], 1.0))
    ge_c = ge.n

    units = [(ti, hh, tb) for ti in range(2) for hh in range(nheads) for tb in range(NTB)]
    se_h = {}
    pe_h = {}
    ve_h = {}
    ld_h = {}
    for u, (ti, hh, tb) in enumerate(units):
        srcf = qf if ti == 0 else kf

        @block.sync
        def _(sy):
            if u == 0:
                for _ws in wait_sems:
                    _ws.wait(sy)
            if u >= 4:
                sy.wait_ge(se.h, se_h[u - 4])
            if u >= 1:
                sy.wait_ge(ld.h, ld.n)
            ld.dma(sy.dma_start(out=x_sb[:, (u % 4) * 512:(u % 4 + 1) * 512],
                                in_=srcf[hh * 128:(hh + 1) * 128, tb * 512:(tb + 1) * 512]))
        ld_h[u] = ld.n

        @block.scalar
        def _(s):
            s.wait_ge(ld.h, ld_h[u])
            if u >= 2:
                s.wait_ge(pe.h, pe_h[u - 2])
            se.inc(s.activation(out=sq[:, (u % 2) * 512:(u % 2 + 1) * 512],
                                in_=x_sb[:, (u % 4) * 512:(u % 4 + 1) * 512],
                                func=AF.Square))
        se_h[u] = se.n

        @block.tensor
        def _(t):
            if u == 0:
                t.wait_ge(ge.h, ge_c)
            t.wait_ge(se.h, se_h[u])
            if u >= 4:
                t.wait_ge(ve.h, ve_h[u - 4])
            pe.inc(t.matmul(ps_n[u % 4][:], onesb[:], sq[:, (u % 2) * 512:(u % 2 + 1) * 512],
                            start=True, stop=True))
        pe_h[u] = pe.n

        @block.vector
        def _(v):
            v.wait_ge(pe.h, pe_h[u])
            ve.inc(v.reduce_max(out=srow[:, u:u + 1], in_=ps_n[u % 4][:], axis=AX))
        ve_h[u] = ve.n

    @block.vector
    def _(v):
        def syn(ii):
            ve.inc(ii)
            v.wait_ge(ve.h, ve.n)
        v.wait_ge(ve.h, ve_h[len(units) - 1])
        for ti in range(2):
            for hh in range(nheads):
                base = (ti * nheads + hh) * NTB
                v.reduce_max(out=brow[:, ti * nheads + hh:ti * nheads + hh + 1],
                             in_=srow[:, base:base + NTB], axis=AX)
        syn(v.engine_nop())
        for hh in range(nheads):
            v.tensor_mul(out=nrow[:, hh:hh + 1], in0=brow[:, hh:hh + 1],
                         in1=brow[:, nheads + hh:nheads + hh + 1])
        syn(v.engine_nop())
    ve_prod = ve.n

    @block.scalar
    def _(s):
        s.wait_ge(ve.h, ve_prod)
        se.inc(s.activation(out=nrow[:], in_=nrow[:], func=AF.Ln))
    se_ln = se.n

    @block.vector
    def _(v):
        v.wait_ge(se.h, se_ln)
        ve.inc(v.tensor_scalar(out=nrow[:], in0=nrow[:], scalar1=-0.5, scalar2=None,
                               op0=ALU.mult))
    ve_hl = ve.n

    @block.scalar
    def _(s):
        s.wait_ge(ve.h, ve_hl)
        se.inc(s.activation(out=nrow[:], in_=nrow[:], func=AF.Exp))
    se_exp = se.n

    @block.vector
    def _(v):
        v.wait_ge(se.h, se_exp)
        ve.inc(v.tensor_scalar(out=nrow[:], in0=nrow[:], scalar1=-float(scale),
                               scalar2=None, op0=ALU.mult))
    ve_nb = ve.n

    @block.tensor
    def _(t):
        t.wait_ge(ve.h, ve_nb)
        pe.inc(t.matmul(ps_b[:], ones1[:], nrow[:], start=True, stop=True))
    pe_b = pe.n

    @block.vector
    def _(v):
        v.wait_ge(pe.h, pe_b)
        ve.inc(v.tensor_copy(out=nb_sb[:], in_=ps_b[:]))
    ve_f = ve.n

    @block.sync
    def _(sy):
        sy.wait_ge(ve.h, ve_f)
        st.dma(sy.dma_start(out=env.t[negb_name][:, :], in_=nb_sb[:]))
        sy.wait_ge(st.h, st.n)
        if done_sem is not None:
            done_sem.inc(sy.nop())
    return st


def emit_attn(env, block, pre, q_name, k_name, v_name, y_name, scale,
              negb_name, extra_recip_scale=1.0, nheads=8, wait_sems=(),
              done_sem=None):
    nc = env.nc
    qf, kf, vt, yf = env.t[q_name], env.t[k_name], env.t[v_name], env.t[y_name]
    NKT = T // 128
    q_sb = env.sb(pre + "_q", [128, T], BF16)
    k_sb = env.sb(pre + "_k", [128, T], BF16)
    v_sb = env.sb(pre + "_v", [128, T], BF16)
    a_sb = env.sb(pre + "_a", [128, NKT * 512], BF16)
    ou_sb = env.sb(pre + "_ou", [128, NTB * 512], BF16)
    y_sb = env.sb(pre + "_y", [128, 2 * 512], BF16)
    mask_sb = env.sb(pre + "_mask", [128, 2048], BF16)
    onesb = env.sb(pre + "_ones", [128, 1], BF16)
    ones1 = env.sb(pre + "_ones1", [1, 128], F32)
    nb_sb = env.sb(pre + "_nb", [128, nheads], F32)
    scv = env.sb(pre + "_scv", [128, 1], F32)
    sums = env.sb(pre + "_sums", [1, NTB * 512], F32)
    rsb = env.sb(pre + "_rsb", [128, 2 * 512], F32)
    ps_s = env.ps(pre + "_ps_s", [128, 2048])
    ps_o = env.ps(pre + "_ps_o", [128, 512])
    ps_n = env.ps(pre + "_ps_n", [1, 512])
    ps_r = env.ps(pre + "_ps_r", [128, 1024])

    ld = env.sem(pre + "ld")
    pe = env.sem(pre + "pe")
    ve = env.sem(pre + "ve")
    se = env.sem(pre + "se")
    ge = env.sem(pre + "ge")
    st = env.sem(pre + "st")

    @block.gpsimd
    def _(g):
        for _ws in wait_sems:
            _ws.wait(g)
        g.memset(onesb[:], 1.0)
        g.memset(ones1[:], 1.0)
        ge.inc(g.memset(scv[:], float(scale)))

    @block.sync
    def _(sy):
        for _ws in wait_sems:
            _ws.wait(sy)
        ld.dma(sy.dma_start(out=mask_sb[:], in_=env.t["mask4"][:, :]))
        ld.dma(sy.dma_start(out=nb_sb[:], in_=env.t[negb_name][:, :]))
    ge_const = ge.n

    for h in range(nheads):
        st_prev_head = st.n
        ve_prev_head = ve.n
        se_prev_head = se.n
        pe_prev_head = pe.n

        @block.sync
        def _(sy):
            if h > 0:
                sy.wait_ge(pe.h, pe_prev_head)
                sy.wait_ge(ve.h, ve_prev_head)
                sy.wait_ge(ld.h, ld.n)
            for tb in range(NTB):
                ld.dma(sy.dma_start(out=q_sb[:, tb * 512:(tb + 1) * 512],
                                    in_=qf[h * 128:(h + 1) * 128, tb * 512:(tb + 1) * 512]))
                ld.dma(sy.dma_start(out=k_sb[:, tb * 512:(tb + 1) * 512],
                                    in_=kf[h * 128:(h + 1) * 128, tb * 512:(tb + 1) * 512]))
            for kt in range(NKT):
                ld.dma(sy.dma_start(out=v_sb[:, kt * 128:(kt + 1) * 128],
                                    in_=vt[kt * 128:(kt + 1) * 128, h * 128:(h + 1) * 128]))
        ld_head = ld.n
        negB = nb_sb[:, h:h + 1]

        se_exp_h = {}
        ve_mask_h = {}
        pe_s_h = {}
        se_ou_h = {}
        ve_sum_h = {}
        for qb in range(NTB):
            nkt = 4 * (qb + 1)
            nbat = nkt // 2
            for bat in range(nbat):
                se_exp_h[(qb, bat)] = se.n + bat + 1
            se_ou_h[qb] = se.n + nbat + 1
            ve_mask_h[(qb, nbat - 2)] = ve.n + 1
            ve_mask_h[(qb, nbat - 1)] = ve.n + 2
            ve_sum_h[qb] = ve.n + 3

            @block.tensor
            def _(t):
                if h == 0 and qb == 0:
                    t.wait_ge(ge.h, ge_const)
                t.wait_ge(ld.h, ld_head)
                if qb > 0:
                    t.wait_ge(se.h, se_ou_h[qb - 1])
                    t.wait_ge(ve.h, ve_sum_h[qb - 1])
                for bat in range(nbat):
                    reg = bat % 2
                    if bat >= 2:
                        t.wait_ge(se.h, se_exp_h[(qb, bat - 2)])
                    elif qb > 0:
                        t.wait_ge(se.h, se_exp_h[(qb - 1, (4 * qb) // 2 - 1)])
                    for i in range(2):
                        kt = bat * 2 + i
                        ii = t.matmul(ps_s[:, reg * 1024 + i * 512: reg * 1024 + (i + 1) * 512],
                                      k_sb[:, kt * 128:(kt + 1) * 128],
                                      q_sb[:, qb * 512:(qb + 1) * 512],
                                      start=True, stop=True)
                    pe.inc(ii)
                    pe_s_h[(qb, bat)] = pe.n
                    if bat >= 1:
                        bb = bat - 1
                        t.wait_ge(se.h, se_exp_h[(qb, bb)])
                        if bb >= nbat - 2:
                            t.wait_ge(ve.h, ve_mask_h[(qb, bb)])
                        for i in range(2):
                            kt = bb * 2 + i
                            t.matmul(ps_n[:], onesb[:], a_sb[:, kt * 512:(kt + 1) * 512],
                                     start=(kt == 0), stop=False, skip_group_check=True)
                            ii = t.matmul(ps_o[:], v_sb[:, kt * 128:(kt + 1) * 128],
                                          a_sb[:, kt * 512:(kt + 1) * 512],
                                          start=(kt == 0), stop=False, skip_group_check=True)
                        pe.inc(ii)
                bb = nbat - 1
                t.wait_ge(se.h, se_exp_h[(qb, bb)])
                t.wait_ge(ve.h, ve_mask_h[(qb, bb)])
                for i in range(2):
                    kt = bb * 2 + i
                    t.matmul(ps_n[:], onesb[:], a_sb[:, kt * 512:(kt + 1) * 512],
                             start=False, stop=(i == 1), skip_group_check=True)
                    ii = t.matmul(ps_o[:], v_sb[:, kt * 128:(kt + 1) * 128],
                                  a_sb[:, kt * 512:(kt + 1) * 512],
                                  start=False, stop=(i == 1), skip_group_check=True)
                pe.inc(ii)
            pe_oav = pe.n

            @block.scalar
            def _(s):
                if h == 0 and qb == 0:
                    s.wait_ge(ge.h, ge_const)
                    s.wait_ge(ld.h, ld_head)
                for bat in range(nbat):
                    reg = bat % 2
                    s.wait_ge(pe.h, pe_s_h[(qb, bat)])
                    assert se.n + 1 == se_exp_h[(qb, bat)]
                    se.inc(s.activation(out=a_sb[:, bat * 1024:(bat + 1) * 1024],
                                        in_=ps_s[:, reg * 1024:(reg + 1) * 1024],
                                        func=AF.Exp, bias=negB, scale=scv[:, 0:1]))
                s.wait_ge(pe.h, pe_oav)
                se.inc(s.activation(out=ou_sb[:, qb * 512:(qb + 1) * 512],
                                    in_=ps_o[:], func=AF.Copy))
            assert se.n == se_ou_h[qb]

            @block.vector
            def _(v):
                for bb in (nbat - 2, nbat - 1):
                    v.wait_ge(se.h, se_exp_h[(qb, bb)])
                    moff = 0 if bb == nbat - 2 else 1024
                    ve.inc(v.tensor_tensor(out=a_sb[:, bb * 1024:(bb + 1) * 1024],
                                           in0=a_sb[:, bb * 1024:(bb + 1) * 1024],
                                           in1=mask_sb[:, moff:moff + 1024], op=ALU.mult))
                v.wait_ge(pe.h, pe_oav)
                ve.inc(v.tensor_copy(out=sums[:, qb * 512:(qb + 1) * 512], in_=ps_n[:]))
            assert ve.n == ve_sum_h[qb]

        @block.vector
        def _(v):
            v.wait_ge(ve.h, ve_sum_h[NTB - 1])
            if extra_recip_scale != 1.0:
                ve.inc(v.tensor_scalar(out=sums[:], in0=sums[:],
                                       scalar1=1.0 / extra_recip_scale, scalar2=None,
                                       op0=ALU.mult))
                v.wait_ge(ve.h, ve.n)
            ve.inc(v.reciprocal(out=sums[:], in_=sums[:]))
        ve_rec = ve.n

        pe_bc_h = {}
        ve_y_h = {}
        for qb in range(NTB):
            @block.tensor
            def _(t):
                if qb == 0:
                    t.wait_ge(ve.h, ve_rec)
                if qb >= 2:
                    t.wait_ge(ve.h, ve_y_h[qb - 2])
                pe.inc(t.matmul(ps_r[:, (qb % 2) * 512:(qb % 2 + 1) * 512], ones1[:],
                                sums[:, qb * 512:(qb + 1) * 512], start=True, stop=True))
            pe_bc_h[qb] = pe.n

            @block.vector
            def _(v):
                v.wait_ge(pe.h, pe_bc_h[qb])
                v.wait_ge(se.h, se_ou_h[NTB - 1])
                if qb >= 2:
                    v.wait_ge(st.h, st.n)
                ve.inc(v.tensor_copy(out=rsb[:, (qb % 2) * 512:(qb % 2 + 1) * 512],
                                     in_=ps_r[:, (qb % 2) * 512:(qb % 2 + 1) * 512]))
                v.wait_ge(ve.h, ve.n)
                ve.inc(v.tensor_mul(out=y_sb[:, (qb % 2) * 512:(qb % 2 + 1) * 512],
                                    in0=ou_sb[:, qb * 512:(qb + 1) * 512],
                                    in1=rsb[:, (qb % 2) * 512:(qb % 2 + 1) * 512]))
            ve_y_h[qb] = ve.n

            @block.sync
            def _(sy):
                sy.wait_ge(ve.h, ve_y_h[qb])
                st.dma(sy.dma_start(out=yf[h * 128:(h + 1) * 128, qb * 512:(qb + 1) * 512],
                                    in_=y_sb[:, (qb % 2) * 512:(qb % 2 + 1) * 512]))
                if h == nheads - 1 and qb == NTB - 1:
                    sy.wait_ge(st.h, st.n)
                    if done_sem is not None:
                        done_sem.inc(sy.nop())
    return st



def emit_mm_fm(env, block, pre, x_name, w_name, n, out_bf=None, out_f32=None,
               residual=None, din=D, dout=D, wait_sems=(), done_sem=None,
               res_out_bf=None, evac_scale=None):
    nc = env.nc
    x = env.t[x_name]
    w = env.t[w_name]
    nkc = din // 128
    nmo = dout // 128
    ntb = n // 512
    assert ntb % 2 == 0
    w_sb = env.sb(pre + "_w", [128, nkc * dout], BF16)
    xsb = env.sb(pre + "_x", [128, 2 * nkc * 512], BF16)
    rsb = env.sb(pre + "_res", [128, 4 * 512], F32) if residual is not None else None
    want_bf = out_bf is not None or res_out_bf is not None
    osb_f = env.sb(pre + "_of", [128, 2 * nmo * 512], F32) \
        if (out_f32 is not None or residual is not None) else None
    osb_b = env.sb(pre + "_ob", [128, 2 * nmo * 512], BF16) if want_bf else None
    scv = env.sb(pre + "_scv", [128, 1], F32) if evac_scale is not None else None
    ps = [env.ps(pre + f"_ps{i}", [128, 512]) for i in range(4)]

    wl = env.sem(pre + "wl")
    ld = env.sem(pre + "ld")
    rl = env.sem(pre + "rl") if residual is not None else None
    pe = env.sem(pre + "pe")
    ve = env.sem(pre + "ve")
    se = env.sem(pre + "se")
    st = env.sem(pre + "st")

    @block.gpsimd
    def _(g):
        for _ws in wait_sems:
            _ws.wait(g)
        if scv is not None:
            env.sem(pre + "ms").inc(g.memset(scv[:], float(evac_scale)))
        for kc in range(nkc):
            wl.dma(g.dma_start(out=w_sb[:, kc * dout:(kc + 1) * dout],
                               in_=w[kc * 128:(kc + 1) * 128, :]))

    first = [True]
    for tp_ in range(ntb // 2):
        tb0 = 2 * tp_
        evq_prev_v = ve.n
        evq_prev_s = se.n
        st_prev_end = st.n
        pe_now = pe.n

        @block.sync
        def _(sy):
            if first[0]:
                for _ws in wait_sems:
                    _ws.wait(sy)
                first[0] = False
            else:
                sy.wait_ge(pe.h, pe_now)
                sy.wait_ge(ld.h, ld.n)
            for half in range(2):
                for kc in range(nkc):
                    ld.dma(sy.dma_start(
                        out=xsb[:, half * nkc * 512 + kc * 512:half * nkc * 512 + (kc + 1) * 512],
                        in_=x[kc * 128:(kc + 1) * 128,
                              (tb0 + half) * 512:(tb0 + half + 1) * 512]))
        ld_tb = ld.n
        pe_base = pe.n

        @block.tensor
        def _(t):
            if tp_ == 0:
                for _ws in wait_sems:
                    _ws.wait(t)
                wl.wait(t)
                if scv is not None:
                    env.sem(pre + "ms").wait(t)
            t.wait_ge(ld.h, ld_tb)
            for mo in range(nmo):
                b0 = ps[2 * (mo % 2)]
                b1 = ps[2 * (mo % 2) + 1]
                ev = ve if residual is not None else se
                evprev = evq_prev_v if residual is not None else evq_prev_s
                per_mo = 2 if residual is not None else 1
                if mo >= 2:
                    t.wait_ge(ev.h, evprev + per_mo * (mo - 1))
                else:
                    t.wait_ge(ev.h, evprev)
                for kc in range(nkc):
                    wsl = w_sb[:, kc * dout + mo * 128:kc * dout + (mo + 1) * 128]
                    t.matmul(b0[:], wsl, xsb[:, kc * 512:(kc + 1) * 512],
                             start=(kc == 0), stop=(kc == nkc - 1))
                    ii = t.matmul(b1[:], wsl,
                                  xsb[:, nkc * 512 + kc * 512:nkc * 512 + (kc + 1) * 512],
                                  start=(kc == 0), stop=(kc == nkc - 1))
                pe.inc(ii)

        if residual is not None:
            res = env.t[residual]
            rl_base = rl.n

            @block.gpsimd
            def _(g):
                for mo in range(nmo):
                    if mo >= 2:
                        g.wait_ge(ve.h, evq_prev_v + 2 * (mo - 1))
                    elif tp_ > 0:
                        g.wait_ge(ve.h, evq_prev_v)
                    g.wait_ge(rl.h, rl.n)
                    for half in range(2):
                        rl.dma(g.dma_start(
                            out=rsb[:, ((2 * mo + half) % 4) * 512:((2 * mo + half) % 4 + 1) * 512],
                            in_=res[mo * 128:(mo + 1) * 128,
                                    (tb0 + half) * 512:(tb0 + half + 1) * 512]))

            @block.vector
            def _(v):
                v.wait_ge(st.h, st_prev_end)
                for mo in range(nmo):
                    v.wait_ge(pe.h, pe_base + mo + 1)
                    v.wait_ge(rl.h, rl_base + 32 * (mo + 1))
                    for half in range(2):
                        bank = ps[2 * (mo % 2) + half]
                        ii = v.tensor_add(
                            out=osb_f[:, (2 * mo + half) * 512:(2 * mo + half + 1) * 512],
                            in0=bank[:],
                            in1=rsb[:, ((2 * mo + half) % 4) * 512:((2 * mo + half) % 4 + 1) * 512])
                        ve.inc(ii)

            if want_bf:
                @block.scalar
                def _(s):
                    s.wait_ge(st.h, st_prev_end)
                    for mo in range(nmo):
                        s.wait_ge(ve.h, evq_prev_v + 2 * mo + 2)
                        for half in range(2):
                            ii = s.activation(
                                out=osb_b[:, (2 * mo + half) * 512:(2 * mo + half + 1) * 512],
                                in_=osb_f[:, (2 * mo + half) * 512:(2 * mo + half + 1) * 512],
                                func=AF.Copy)
                        se.inc(ii)
        else:
            @block.scalar
            def _(s):
                s.wait_ge(st.h, st_prev_end)
                for mo in range(nmo):
                    s.wait_ge(pe.h, pe_base + mo + 1)
                    _sc = scv[:, 0:1] if scv is not None else 1.0
                    ii = None
                    for half in range(2):
                        bank = ps[2 * (mo % 2) + half]
                        sl = slice((2 * mo + half) * 512, (2 * mo + half + 1) * 512)
                        if want_bf:
                            ii = s.activation(out=osb_b[:, sl], in_=bank[:],
                                              func=AF.Copy, scale=_sc)
                        if out_f32 is not None:
                            ii = s.activation(out=osb_f[:, sl], in_=bank[:],
                                              func=AF.Copy, scale=_sc)
                    se.inc(ii)

        @block.sync
        def _(sy):
            for mo in range(nmo):
                if residual is not None:
                    sy.wait_ge(ve.h, evq_prev_v + 2 * mo + 2)
                    if want_bf:
                        sy.wait_ge(se.h, evq_prev_s + mo + 1)
                else:
                    sy.wait_ge(se.h, evq_prev_s + mo + 1)
                for half in range(2):
                    sl = slice((2 * mo + half) * 512, (2 * mo + half + 1) * 512)
                    cols = slice((tb0 + half) * 512, (tb0 + half + 1) * 512)
                    if out_f32 is not None:
                        st.dma(sy.dma_start(out=env.t[out_f32][mo * 128:(mo + 1) * 128, cols],
                                            in_=osb_f[:, sl]))
                    if want_bf:
                        ob = out_bf or res_out_bf
                        st.dma(sy.dma_start(out=env.t[ob][mo * 128:(mo + 1) * 128, cols],
                                            in_=osb_b[:, sl]))
            if tp_ == ntb // 2 - 1:
                sy.wait_ge(st.h, st.n)
                if done_sem is not None:
                    done_sem.inc(sy.nop())
    return st



def emit_glu(env, block, pre, x_name, w1_name, w2_name, out_name, n,
             din=D, dout=2816, wait_sems=(), done_sem=None):
    nc = env.nc
    x = env.t[x_name]
    w1, w2 = env.t[w1_name], env.t[w2_name]
    nkc = din // 128
    nmo = dout // 128
    ntb = n // 512
    x_sb = env.sb(pre + "_x", [128, nkc * n], BF16)
    w1_sb = env.sb(pre + "_w1", [128, 2 * din], BF16)
    w2_sb = env.sb(pre + "_w2", [128, 2 * din], BF16)
    gsb = env.sb(pre + "_g", [128, 2 * 512], BF16)
    osb = env.sb(pre + "_o", [128, 2 * 512], BF16)
    ps_g = [env.ps(pre + f"_pg{i}", [128, 512]) for i in range(2)]
    ps_u = [env.ps(pre + f"_pu{i}", [128, 512]) for i in range(2)]

    wl = env.sem(pre + "wl")
    ld = env.sem(pre + "ld")
    pe = env.sem(pre + "pe")
    ve = env.sem(pre + "ve")
    se = env.sem(pre + "se")
    st = env.sem(pre + "st")

    @block.sync
    def _(sy):
        for _ws in wait_sems:
            _ws.wait(sy)
        for kc in range(nkc):
            ld.dma(sy.dma_start(out=x_sb[:, kc * n:(kc + 1) * n],
                                in_=x[kc * 128:(kc + 1) * 128, :]))
    ld_x = ld.n

    pe_h = {}
    se_h = {}
    ve_h = {}
    st_h = {}
    for mo in range(nmo):
        slot = mo % 2

        @block.gpsimd
        def _(g):
            if mo == 0:
                for _ws in wait_sems:
                    _ws.wait(g)
            if mo >= 2:
                g.wait_ge(pe.h, pe_h[mo - 2])
            g.wait_ge(wl.h, wl.n)
            wl.dma(g.dma_start(out=w1_sb[:, slot * din:(slot + 1) * din], in_=w1[mo]))
            wl.dma(g.dma_start(out=w2_sb[:, slot * din:(slot + 1) * din], in_=w2[mo]))
        wl_mo = wl.n

        for tb in range(ntb):
            gi = mo * ntb + tb
            pslot = gi % 2

            @block.tensor
            def _(t):
                if gi == 0:
                    t.wait_ge(ld.h, ld_x)
                t.wait_ge(wl.h, wl_mo)
                if gi >= 2:
                    t.wait_ge(se.h, se_h[gi - 2])
                    t.wait_ge(ve.h, ve_h[gi - 2])
                for kc in range(nkc):
                    ii = t.matmul(ps_g[pslot][:], w1_sb[:, slot * din + kc * 128:slot * din + (kc + 1) * 128],
                                  x_sb[:, kc * n + tb * 512:kc * n + (tb + 1) * 512],
                                  start=(kc == 0), stop=(kc == nkc - 1))
                pe.inc(ii)
                for kc in range(nkc):
                    ii = t.matmul(ps_u[pslot][:], w2_sb[:, slot * din + kc * 128:slot * din + (kc + 1) * 128],
                                  x_sb[:, kc * n + tb * 512:kc * n + (tb + 1) * 512],
                                  start=(kc == 0), stop=(kc == nkc - 1))
                pe.inc(ii)
            pe_h[gi] = pe.n
            if tb == ntb - 1:
                pe_h[mo] = pe.n

            @block.scalar
            def _(s):
                s.wait_ge(pe.h, pe_h[gi] - 1)
                if gi >= 2:
                    s.wait_ge(ve.h, ve_h[gi - 2])
                se.inc(s.activation(out=gsb[:, pslot * 512:(pslot + 1) * 512],
                                    in_=ps_g[pslot][:], func=AF.Silu))
            se_h[gi] = se.n

            @block.vector
            def _(v):
                v.wait_ge(pe.h, pe_h[gi])
                v.wait_ge(se.h, se_h[gi])
                if gi >= 2:
                    v.wait_ge(st.h, st_h[gi - 2])
                ve.inc(v.tensor_mul(out=osb[:, pslot * 512:(pslot + 1) * 512],
                                    in0=gsb[:, pslot * 512:(pslot + 1) * 512],
                                    in1=ps_u[pslot][:]))
            ve_h[gi] = ve.n

            @block.sync
            def _(sy):
                sy.wait_ge(ve.h, ve_h[gi])
                sy.wait_ge(st.h, st.n)
                st.dma(sy.dma_start(out=env.t[out_name][mo * 128:(mo + 1) * 128,
                                                        tb * 512:(tb + 1) * 512],
                                    in_=osb[:, pslot * 512:(pslot + 1) * 512]))
            st_h[gi] = st.n
    @block.sync
    def _(sy):
        sy.wait_ge(st.h, st.n)
        if done_sem is not None:
            done_sem.inc(sy.nop())
    return st



def emit_gate(env, block, pre, x_name, gate_name, cw_name, n,
              wait_sems=(), done_sem=None):
    nc = env.nc
    x = env.t[x_name]
    gw = env.t[gate_name]
    cwt = env.t[cw_name]
    ntt = n // 128
    gw_sb = env.sb(pre + "_gw", [128, NKC * 16], F32)
    xsb = env.sb(pre + "_x", [128, NKC * 512], F32)
    ident = env.sb(pre + "_id", [128, 128], F32)
    l_sb = env.sb(pre + "_l", [16, 512], F32)
    lt = env.sb(pre + "_lt", [128, 16], F32)
    e_sb = env.sb(pre + "_e", [128, 16], F32)
    work = env.sb(pre + "_wk", [128, 16], F32)
    eqm = env.sb(pre + "_eq", [128, 16], F32)
    mstat = env.sb(pre + "_ms", [128, 4], F32)
    cw = env.sb(pre + "_cw", [128, 16], F32)
    cwT_sb = env.sb(pre + "_cwt", [16, 128], F32)
    ps_l = env.ps(pre + "_psl", [16, 512])
    ps_t = env.ps(pre + "_pst", [128, 16])
    ps_c = env.ps(pre + "_psc", [16, 128])

    wl = env.sem(pre + "wl")
    ld = env.sem(pre + "ld")
    pe = env.sem(pre + "pe")
    ve = env.sem(pre + "ve")
    se = env.sem(pre + "se")
    st = env.sem(pre + "st")

    @block.gpsimd
    def _(g):
        for _ws in wait_sems:
            _ws.wait(g)
        for kc in range(NKC):
            wl.dma(g.dma_start(out=gw_sb[:, kc * 16:(kc + 1) * 16],
                               in_=gw[kc * 128:(kc + 1) * 128, :]))

    @block.sync
    def _(sy):
        for _ws in wait_sems:
            _ws.wait(sy)
        ld.dma(sy.dma_start(out=ident[:], in_=env.t["ident32"][:, :]))

    first = [True]
    for tb in range(n // 512):
        ve_prev = ve.n
        st_prev = st.n

        @block.sync
        def _(sy):
            if tb > 0:
                sy.wait_ge(pe.h, pe.n)
            for kc in range(NKC):
                ld.dma(sy.dma_start(out=xsb[:, kc * 512:(kc + 1) * 512],
                                    in_=x[kc * 128:(kc + 1) * 128, tb * 512:(tb + 1) * 512]))
        ld_tb = ld.n

        @block.tensor
        def _(t):
            if first[0]:
                wl.wait(t)
                first[0] = False
            t.wait_ge(ld.h, ld_tb)
            if tb > 0:
                t.wait_ge(ve.h, ve_prev)
            else:
                pass
            for kc in range(NKC):
                ii = t.matmul(ps_l[:], gw_sb[:, kc * 16:(kc + 1) * 16],
                              xsb[:, kc * 512:(kc + 1) * 512],
                              start=(kc == 0), stop=(kc == NKC - 1))
            pe.inc(ii)
        pe_l = pe.n

        @block.vector
        def _(v):
            v.wait_ge(pe.h, pe_l)
            ve.inc(v.tensor_copy(out=l_sb[:], in_=ps_l[:]))
        ve_l = ve.n

        for tt in range(4):
            pe_t_prev = pe.n

            @block.tensor
            def _(t):
                t.wait_ge(ve.h, ve_l if tt == 0 else ve.n)
                pe.inc(t.transpose(ps_t[:, :], l_sb[:, tt * 128:(tt + 1) * 128],
                                   ident[0:16, 0:16]))
            pe_tt = pe.n

            @block.vector
            def _(v):
                n0 = [0]

                def sync(ii):
                    ve.inc(ii)
                    v.wait_ge(ve.h, ve.n)
                v.wait_ge(pe.h, pe_tt)
                sync(v.tensor_copy(out=lt[:], in_=ps_t[:]))
                sync(v.reduce_max(out=mstat[:, 0:1], in_=lt[:], axis=AX, negate=True))
            ve_ltmax = ve.n

            @block.scalar
            def _(s):
                s.wait_ge(ve.h, ve_ltmax)
                se.inc(s.activation(out=e_sb[:], in_=lt[:], func=AF.Exp,
                                    bias=mstat[:, 0:1]))
            se_e = se.n

            @block.vector
            def _(v):
                n0 = [0]

                def sync(ii):
                    ve.inc(ii)
                    v.wait_ge(ve.h, ve.n)
                v.wait_ge(se.h, se_e)
                sync(v.tensor_copy(out=work[:], in_=e_sb[:]))
                for i in range(6):
                    sync(v.reduce_max(out=mstat[:, 1:2], in_=work[:], axis=AX))
                    sync(v.tensor_scalar(out=eqm[:], in0=work[:], scalar1=mstat[:, 1:2],
                                         scalar2=-1e30, op0=ALU.is_equal, op1=ALU.mult))
                    sync(v.tensor_add(out=work[:], in0=work[:], in1=eqm[:]))
                sync(v.tensor_scalar(out=cw[:], in0=work[:], scalar1=-1e29, scalar2=None,
                                     op0=ALU.is_lt))
                sync(v.tensor_mul(out=cw[:], in0=cw[:], in1=e_sb[:]))
                sync(v.reduce_sum(out=mstat[:, 2:3], in_=cw[:], axis=AX))
                sync(v.reciprocal(out=mstat[:, 3:4], in_=mstat[:, 2:3]))
                sync(v.tensor_scalar(out=cw[:], in0=cw[:], scalar1=mstat[:, 3:4],
                                     scalar2=None, op0=ALU.mult))
            ve_cw = ve.n

            @block.tensor
            def _(t):
                t.wait_ge(ve.h, ve_cw)
                if tt > 0 or tb > 0:
                    t.wait_ge(st.h, st.n)
                pe.inc(t.transpose(ps_c[:, :], cw[:], ident[:]))
            pe_cwt = pe.n

            @block.vector
            def _(v):
                v.wait_ge(pe.h, pe_cwt)
                ve.inc(v.tensor_copy(out=cwT_sb[:], in_=ps_c[:]))
            ve_cwt = ve.n

            @block.sync
            def _(sy):
                sy.wait_ge(ve.h, ve_cwt)
                st.dma(sy.dma_start(out=cwt[:, tb * 512 + tt * 128:tb * 512 + (tt + 1) * 128],
                                    in_=cwT_sb[:]))
                if tb == n // 512 - 1 and tt == 3:
                    sy.wait_ge(st.h, st.n)
                    if done_sem is not None:
                        done_sem.inc(sy.nop())
    return st



def emit_moe(env, block, pre, t_name, cw_name, x2s_name, s_name, xown_name,
             x1_name, n=TH, wait_sems=(), done_sem=None):
    nc = env.nc
    E, NIC, NDC = 16, 11, 16
    t_dram = env.t[t_name]
    cw_dram = env.t[cw_name]
    egT, euT, edT = env.t["l0f_egT"], env.t["l0f_euT"], env.t["l0f_edT"]
    t_sb = env.sb(pre + "_t", [128, NKC * 1024], BF16)
    racc = env.sb(pre + "_racc", [128, NDC * 1024], F32)
    hh = env.sb(pre + "_hh", [128, 2 * NIC * 1024], BF16)
    egw = env.sb(pre + "_egw", [128, 2 * 2048], BF16)
    euw = env.sb(pre + "_euw", [128, 2 * 2048], BF16)
    edw = env.sb(pre + "_edw", [128, 2 * 2 * 1408], BF16)
    cwr = env.sb(pre + "_cwr", [1, 2 * 1024], F32)
    cwb = env.sb(pre + "_cwb", [128, 2 * 1024], F32)
    silu_sb = env.sb(pre + "_sil", [128, 1024], BF16)
    tmp_sb = env.sb(pre + "_tmp", [128, 1024], BF16)
    ones1 = env.sb(pre + "_ones1", [1, 128], F32)
    ep_x2s = env.sb(pre + "_ex2s", [128, 512], F32)
    ep_s = env.sb(pre + "_es", [128, 512], F32)
    ep_x = env.sb(pre + "_ex", [128, 512], F32)
    ep_f = env.sb(pre + "_ef", [128, 512], F32)
    ep_f2 = env.sb(pre + "_ef2", [128, 512], F32)
    ep_o = env.sb(pre + "_eo", [128, 512], F32)
    ps_g = env.ps(pre + "_psg", [128, 1024])
    ps_u = env.ps(pre + "_psu", [128, 1024])
    ps_d = [env.ps(pre + f"_psd{i}", [128, 1024]) for i in range(2)]
    ps_c = ps_d[0]

    ul = env.sem(pre + "ul")
    el = env.sem(pre + "el")
    tl = env.sem(pre + "tl")
    pe = env.sem(pre + "pe")
    ve = env.sem(pre + "ve")
    se = env.sem(pre + "se")
    ge = env.sem(pre + "ge")
    st = env.sem(pre + "st")

    @block.gpsimd
    def _(g):
        for _ws in wait_sems:
            _ws.wait(g)
        ge.inc(g.memset(ones1[:], 1.0))
    ge_ones = ge.n

    first = [True]
    last_racc_ve = [0]
    for th in range(2):
        tcol = slice(th * 1024, (th + 1) * 1024)
        pe_all_prev = pe.n
        ve_all_prev = ve.n

        @block.sync
        def _(sy):
            if first[0]:
                for _ws in wait_sems:
                    _ws.wait(sy)
                first[0] = False
            else:
                sy.wait_ge(pe.h, pe_all_prev)
                sy.wait_ge(ve.h, ve_all_prev)
            sy.wait_ge(tl.h, tl.n)
            for kc in range(NKC):
                tl.dma(sy.dma_start(out=t_sb[:, kc * 1024:(kc + 1) * 1024],
                                    in_=t_dram[kc * 128:(kc + 1) * 128, tcol]))
        tl_t = tl.n

        pe_up_h = {}
        pe_up_h2 = {}
        se_sil_h = {}
        ve_t1_h = {}
        ve_hh_h = {}
        ve_cwb_h = {}
        pe_cw_h = {}
        pe_dn_h = {}
        ve_racc_h = {}

        for grp in range(E // 2):
            es = [2 * grp, 2 * grp + 1]
            for ei, e in enumerate(es):
                gidx = grp * 2 + ei
                slot = gidx % 2

                @block.sync
                def _(sy):
                    if gidx >= 2:
                        sy.wait_ge(pe.h, pe_cw_h[gidx - 2])
                    sy.wait_ge(tl.h, tl.n)
                    tl.dma(sy.dma_start(out=cwr[:, slot * 1024:(slot + 1) * 1024],
                                        in_=cw_dram[e:e + 1, tcol]))
                tl_cw = tl.n

                @block.tensor
                def _(t):
                    if gidx == 0 and th == 0:
                        t.wait_ge(ge.h, ge_ones)
                    t.wait_ge(tl.h, tl_cw)
                    if gidx >= 1:
                        t.wait_ge(ve.h, ve_cwb_h[gidx - 1])
                    t.wait_ge(ve.h, last_racc_ve[0])
                    for half in range(2):
                        ii = t.matmul(ps_c[:, half * 512:(half + 1) * 512], ones1[:],
                                      cwr[:, slot * 1024 + half * 512:slot * 1024 + (half + 1) * 512],
                                      start=True, stop=True)
                    pe.inc(ii)
                pe_cw_h[gidx] = pe.n

                @block.vector
                def _(v):
                    v.wait_ge(pe.h, pe_cw_h[gidx])
                    if gidx >= 2:
                        v.wait_ge(ve.h, ve_t1_h[(gidx - 2) * NIC + NIC - 1])
                    ve.inc(v.tensor_copy(out=cwb[:, slot * 1024:(slot + 1) * 1024],
                                         in_=ps_c[:]))
                ve_cwb_h[gidx] = ve.n

                for ic in range(NIC):
                    idx = gidx * NIC + ic
                    pslot = idx % 2

                    @block.gpsimd
                    def _(g):
                        if idx >= 2:
                            g.wait_ge(pe.h, pe_up_h[idx - 2])
                        g.wait_ge(ul.h, ul.n)
                        g.dma_start(out=egw[:, pslot * 2048:(pslot + 1) * 2048],
                                    in_=egT[e, ic]).then_inc(ul.h, 16)
                        g.dma_start(out=euw[:, pslot * 2048:(pslot + 1) * 2048],
                                    in_=euT[e, ic]).then_inc(ul.h, 16)
                    ul.n += 32
                    ul_ic = ul.n

                    @block.tensor
                    def _(t):
                        t.wait_ge(tl.h, tl_t)
                        t.wait_ge(ul.h, ul_ic)
                        if idx >= 1:
                            t.wait_ge(se.h, se_sil_h[idx - 1])
                        for kc in range(NKC):
                            for half in range(2):
                                ii = t.matmul(ps_g[:, half * 512:(half + 1) * 512],
                                              egw[:, pslot * 2048 + kc * 128:pslot * 2048 + (kc + 1) * 128],
                                              t_sb[:, kc * 1024 + half * 512:kc * 1024 + (half + 1) * 512],
                                              start=(kc == 0), stop=(kc == NKC - 1))
                        pe.inc(ii)
                        if idx >= 1:
                            t.wait_ge(ve.h, ve_hh_h[idx - 1])
                        for kc in range(NKC):
                            for half in range(2):
                                ii = t.matmul(ps_u[:, half * 512:(half + 1) * 512],
                                              euw[:, pslot * 2048 + kc * 128:pslot * 2048 + (kc + 1) * 128],
                                              t_sb[:, kc * 1024 + half * 512:kc * 1024 + (half + 1) * 512],
                                              start=(kc == 0), stop=(kc == NKC - 1))
                        pe.inc(ii)
                    pe_up_h[idx] = pe.n

                    @block.scalar
                    def _(s):
                        s.wait_ge(pe.h, pe_up_h[idx] - 1)
                        if idx >= 1:
                            s.wait_ge(ve.h, ve_t1_h[idx - 1])
                        se.inc(s.activation(out=silu_sb[:], in_=ps_g[:], func=AF.Silu))
                    se_sil_h[idx] = se.n

                    @block.vector
                    def _(v):
                        v.wait_ge(se.h, se_sil_h[idx])
                        v.wait_ge(ve.h, ve_cwb_h[gidx])
                        if idx >= 1:
                            v.wait_ge(ve.h, ve_hh_h[idx - 1])
                        ve.inc(v.tensor_mul(out=tmp_sb[:],
                                            in0=silu_sb[:],
                                            in1=cwb[:, slot * 1024:(slot + 1) * 1024]))
                        ve_t1_h[idx] = ve.n
                        v.wait_ge(ve.h, ve.n)
                        v.wait_ge(pe.h, pe_up_h[idx])
                        hsl = slice((ei * NIC + ic) * 1024, (ei * NIC + ic + 1) * 1024)
                        ve.inc(v.tensor_mul(out=hh[:, hsl], in0=tmp_sb[:], in1=ps_u[:]))
                    ve_hh_h[idx] = ve.n

            ve_hh_done = ve.n

            for dc in range(NDC):
                dslot = dc % 2

                @block.gpsimd
                def _(g):
                    if dc >= 2:
                        g.wait_ge(pe.h, pe_dn_h[dc - 2])
                    g.wait_ge(el.h, el.n)
                    for ei, e in enumerate(es):
                        g.dma_start(out=edw[:, (dslot * 2 + ei) * 1408:(dslot * 2 + ei + 1) * 1408],
                                    in_=edT[e, dc]).then_inc(el.h, 16)
                el.n += 32
                el_dc = el.n

                dps = ps_d[dc % 2]

                @block.tensor
                def _(t):
                    t.wait_ge(el.h, el_dc)
                    t.wait_ge(ve.h, ve_hh_done)
                    if dc >= 2:
                        t.wait_ge(ve.h, ve_racc_h[dc - 2])
                    elif grp > 0 and dc == 0:
                        t.wait_ge(ve.h, ve_cwb_h[2 * grp])
                    for ei in range(2):
                        for ic in range(NIC):
                            for half in range(2):
                                ii = t.matmul(
                                    dps[:, half * 512:(half + 1) * 512],
                                    edw[:, (dslot * 2 + ei) * 1408 + ic * 128:(dslot * 2 + ei) * 1408 + (ic + 1) * 128],
                                    hh[:, (ei * NIC + ic) * 1024 + half * 512:(ei * NIC + ic) * 1024 + (half + 1) * 512],
                                    start=(ei == 0 and ic == 0), stop=(ei == 1 and ic == NIC - 1))
                    pe.inc(ii)
                pe_dn_h[dc] = pe.n

                @block.vector
                def _(v):
                    v.wait_ge(pe.h, pe_dn_h[dc])
                    if grp == 0:
                        ii = v.tensor_copy(out=racc[:, dc * 1024:(dc + 1) * 1024], in_=dps[:])
                    else:
                        ii = v.tensor_add(out=racc[:, dc * 1024:(dc + 1) * 1024],
                                          in0=racc[:, dc * 1024:(dc + 1) * 1024], in1=dps[:])
                    ve.inc(ii)
                ve_racc_h[dc] = ve.n
                last_racc_ve[0] = ve.n

        x2s = env.t[x2s_name]
        s_f = env.t[s_name]
        xo = env.t[xown_name]
        x1 = env.t[x1_name]
        ve_moe_done = ve.n
        ve_ep_prev = [ve.n]
        for dc in range(NDC):
            for half in range(2):
                colg = th * 1024 + half * 512
                eslot = 0

                @block.sync
                def _(sy):
                    sy.wait_ge(ve.h, ve_ep_prev[0])
                    sy.wait_ge(tl.h, tl.n)
                    tl.dma(sy.dma_start(out=ep_x2s[:, eslot * 512:(eslot + 1) * 512],
                                        in_=x2s[dc * 128:(dc + 1) * 128, colg:colg + 512]))
                    tl.dma(sy.dma_start(out=ep_s[:, eslot * 512:(eslot + 1) * 512],
                                        in_=s_f[dc * 128:(dc + 1) * 128, colg:colg + 512]))
                    tl.dma(sy.dma_start(out=ep_x[:, eslot * 512:(eslot + 1) * 512],
                                        in_=xo[dc * 128:(dc + 1) * 128, colg:colg + 512]))
                tl_ep = tl.n

                @block.vector
                def _(v):
                    v.wait_ge(tl.h, tl_ep)
                    v.wait_ge(ve.h, ve_moe_done)
                    ve.inc(v.tensor_add(out=ep_f[:, eslot * 512:(eslot + 1) * 512],
                                        in0=racc[:, dc * 1024 + half * 512:dc * 1024 + (half + 1) * 512],
                                        in1=ep_x2s[:, eslot * 512:(eslot + 1) * 512]))
                ve_f = ve.n

                @block.scalar
                def _(s):
                    s.wait_ge(ve.h, ve_f)
                    se.inc(s.activation(out=ep_f2[:, eslot * 512:(eslot + 1) * 512],
                                        in_=ep_f[:, eslot * 512:(eslot + 1) * 512],
                                        func=AF.Silu))
                se_sil2 = se.n

                @block.vector
                def _(v):
                    v.wait_ge(se.h, se_sil2)
                    v.wait_ge(st.h, st.n)
                    ve.inc(v.tensor_mul(out=ep_o[:, eslot * 512:(eslot + 1) * 512],
                                        in0=ep_f2[:, eslot * 512:(eslot + 1) * 512],
                                        in1=ep_s[:, eslot * 512:(eslot + 1) * 512]))
                    v.wait_ge(ve.h, ve.n)
                    ve.inc(v.tensor_add(out=ep_o[:, eslot * 512:(eslot + 1) * 512],
                                        in0=ep_o[:, eslot * 512:(eslot + 1) * 512],
                                        in1=ep_x[:, eslot * 512:(eslot + 1) * 512]))
                ve_o = ve.n
                ve_ep_prev[0] = ve_o

                @block.sync
                def _(sy):
                    sy.wait_ge(ve.h, ve_o)
                    st.dma(sy.dma_start(out=x1[dc * 128:(dc + 1) * 128, colg:colg + 512],
                                        in_=ep_o[:, eslot * 512:(eslot + 1) * 512]))

    @block.sync
    def _(sy):
        sy.wait_ge(st.h, st.n)
        if done_sem is not None:
            done_sem.inc(sy.nop())
    return st



def emit_gla_gk(env, block, pre, x_name, out_name, wait_sems=(), done_sem=None):
    nc = env.nc
    x = env.t[x_name]
    w1 = env.t["l1_wkg1"]
    w2 = env.t["l1_wkg2"]
    bk = env.t["l1_bkg2"]
    gk_out = env.t[out_name]
    w1_sb = env.sb(pre + "_w1", [128, NKC * 16], BF16)
    w2_sb = env.sb(pre + "_w2", [16, 512], BF16)
    bk_sb = env.sb(pre + "_bk", [128, 4], F32)
    xsb = env.sb(pre + "_x", [128, NKC * 512], BF16)
    z1_sb = env.sb(pre + "_z1", [16, 512], BF16)
    z2_sb = env.sb(pre + "_z2", [128, 4 * 512], F32)
    ab_sb = env.sb(pre + "_ab", [128, 512], F32)
    ln_sb = env.sb(pre + "_ln", [128, 512], F32)
    mn_sb = env.sb(pre + "_mn", [128, 512], F32)
    gk_sb = env.sb(pre + "_gk", [128, 4 * 512], F32)
    ps_z1 = env.ps(pre + "_psz1", [16, 512])
    ps_z2 = env.ps(pre + "_psz2", [128, 512])

    wl = env.sem(pre + "wl")
    ld = env.sem(pre + "ld")
    pe = env.sem(pre + "pe")
    ve = env.sem(pre + "ve")
    se = env.sem(pre + "se")
    st = env.sem(pre + "st")

    @block.gpsimd
    def _(g):
        for _ws in wait_sems:
            _ws.wait(g)
        for kc in range(NKC):
            wl.dma(g.dma_start(out=w1_sb[:, kc * 16:(kc + 1) * 16],
                               in_=w1[kc * 128:(kc + 1) * 128, :]))
        wl.dma(g.dma_start(out=w2_sb[:], in_=w2[:, :]))

    @block.sync
    def _(sy):
        for _ws in wait_sems:
            _ws.wait(sy)
        ld.dma(sy.dma_start(out=bk_sb[:], in_=bk[:, :]))

    for tb in range(NTB):
        se_prev = se.n
        ve_prev = ve.n
        st_prev = st.n

        @block.sync
        def _(sy):
            if tb > 0:
                sy.wait_ge(pe.h, pe.n)
            for kc in range(NKC):
                ld.dma(sy.dma_start(out=xsb[:, kc * 512:(kc + 1) * 512],
                                    in_=x[kc * 128:(kc + 1) * 128, tb * 512:(tb + 1) * 512]))
        ld_tb = ld.n

        @block.tensor
        def _(t):
            if tb == 0:
                wl.wait(t)
                t.wait_ge(ld.h, ld_tb)
            else:
                t.wait_ge(ld.h, ld_tb)
                t.wait_ge(ve.h, ve_prev)
            for kc in range(NKC):
                ii = t.matmul(ps_z1[:], w1_sb[:, kc * 16:(kc + 1) * 16],
                              xsb[:, kc * 512:(kc + 1) * 512],
                              start=(kc == 0), stop=(kc == NKC - 1))
            pe.inc(ii)
        pe_z1 = pe.n

        @block.vector
        def _(v):
            v.wait_ge(pe.h, pe_z1)
            ve.inc(v.tensor_copy(out=z1_sb[:], in_=ps_z1[:]))
        ve_z1 = ve.n

        for mo in range(4):
            @block.tensor
            def _(t):
                t.wait_ge(ve.h, ve_z1)
                if tb > 0 or mo > 0:
                    t.wait_ge(ve.h, ve.n)
                pe.inc(t.matmul(ps_z2[:], w2_sb[:, mo * 128:(mo + 1) * 128], z1_sb[:],
                                start=True, stop=True))
            pe_z2 = pe.n

            msl = slice(mo * 512, (mo + 1) * 512)

            @block.vector
            def _(v):
                v.wait_ge(pe.h, pe_z2)
                v.wait_ge(ld.h, ld_tb)
                if tb > 0:
                    v.wait_ge(st.h, st_prev)
                ve.inc(v.tensor_scalar(out=z2_sb[:, msl], in0=ps_z2[:],
                                       scalar1=bk_sb[:, mo:mo + 1], scalar2=None,
                                       op0=ALU.add))
            ve_z2 = ve.n

            @block.scalar
            def _(s):
                s.wait_ge(ve.h, ve_z2)
                se.inc(s.activation(out=ab_sb[:], in_=z2_sb[:, msl], func=AF.Abs))
                s.wait_ge(se.h, se.n)
                se.inc(s.activation(out=ab_sb[:], in_=ab_sb[:], func=AF.Exp, scale=-1.0))
                s.wait_ge(se.h, se.n)
                se.inc(s.activation(out=ln_sb[:], in_=ab_sb[:], func=AF.Ln, bias=1.0))
            se_ln = se.n

            @block.vector
            def _(v):
                v.wait_ge(se.h, se_ln)
                msl = slice(mo * 512, (mo + 1) * 512)
                n0 = [ve.n]

                def syn(ii):
                    ve.inc(ii)
                    v.wait_ge(ve.h, ve.n)
                syn(v.tensor_scalar_min(out=mn_sb[:], in0=z2_sb[:, msl], scalar1=0.0))
                syn(v.tensor_sub(out=mn_sb[:], in0=mn_sb[:], in1=ln_sb[:]))
                syn(v.tensor_scalar(out=gk_sb[:, msl], in0=mn_sb[:], scalar1=1.0 / 16.0,
                                    scalar2=None, op0=ALU.mult))
            ve_gk = ve.n

            @block.sync
            def _(sy):
                sy.wait_ge(ve.h, ve_gk)
                st.dma(sy.dma_start(out=gk_out[mo * 128:(mo + 1) * 128,
                                               tb * 512:(tb + 1) * 512],
                                    in_=gk_sb[:, mo * 512:(mo + 1) * 512]))
        if tb == NTB - 1:
            @block.sync
            def _(sy):
                sy.wait_ge(st.h, st.n)
                if done_sem is not None:
                    done_sem.inc(sy.nop())
    return st


def emit_gla_vg(env, block, pre, x_name, v_out, g_out, wait_sems=(), done_sem=None):
    nc = env.nc
    x = env.t[x_name]
    wv, wg = env.t["l1_wv"], env.t["l1_wg"]
    bg = env.t["l1_bg"]
    vt, gt = env.t[v_out], env.t[g_out]
    wv_sb = env.sb(pre + "_wv", [128, NKC * 1024], BF16)
    wg_sb = env.sb(pre + "_wg", [128, NKC * 1024], BF16)
    bg_sb = env.sb(pre + "_bg", [1, 1024], BF16)
    onesr = env.sb(pre + "_or", [1, 128], BF16)
    xsb = env.sb(pre + "_x", [128, 2 * NKC * 512], BF16)
    vsb = env.sb(pre + "_v", [128, 8 * 512], BF16)
    gsb = env.sb(pre + "_g", [128, 8 * 512], F32)
    psv = [env.ps(pre + f"_psv{i}", [128, 512]) for i in range(2)]
    psg = [env.ps(pre + f"_psg{i}", [128, 512]) for i in range(2)]

    wl = env.sem(pre + "wl")
    ld = env.sem(pre + "ld")
    pe = env.sem(pre + "pe")
    ve = env.sem(pre + "ve")
    se = env.sem(pre + "se")
    st = env.sem(pre + "st")

    ms = env.sem(pre + "ms")

    @block.gpsimd
    def _(g):
        for _ws in wait_sems:
            _ws.wait(g)
        ms.inc(g.memset(onesr[:], 1.0))
        for kc in range(NKC):
            wl.dma(g.dma_start(out=wv_sb[:, kc * 1024:(kc + 1) * 1024],
                               in_=wv[kc * 128:(kc + 1) * 128, :]))
            wl.dma(g.dma_start(out=wg_sb[:, kc * 1024:(kc + 1) * 1024],
                               in_=wg[kc * 128:(kc + 1) * 128, :]))
        wl.dma(g.dma_start(out=bg_sb[:], in_=bg[:, :]))

    first = [True]
    for tb in range(NTB):
        buf = tb % 2
        xoff = buf * NKC * 512
        se_prev = se.n
        ve_prev = ve.n
        st_prev = st.n
        pe_now = pe.n

        @block.sync
        def _(sy):
            if first[0]:
                for _ws in wait_sems:
                    _ws.wait(sy)
                first[0] = False
            if tb >= 2:
                sy.wait_ge(pe.h, pe_now - 16)
            if tb >= 1:
                sy.wait_ge(ld.h, ld.n)
            for kc in range(NKC):
                ld.dma(sy.dma_start(out=xsb[:, xoff + kc * 512:xoff + (kc + 1) * 512],
                                    in_=x[kc * 128:(kc + 1) * 128, tb * 512:(tb + 1) * 512]))
        ld_tb = ld.n
        pe_base = pe.n

        @block.tensor
        def _(t):
            if tb == 0:
                wl.wait(t)
                ms.wait(t)
            t.wait_ge(ld.h, ld_tb)
            gidx = 0
            for tt in range(4):
                for dvb in range(2):
                    if gidx >= 2:
                        t.wait_ge(se.h, se_prev + (gidx - 2) + 1)
                    else:
                        t.wait_ge(se.h, se_prev)
                    for kc in range(NKC):
                        ii = t.matmul(psv[gidx % 2][:],
                                      xsb[:, xoff + kc * 512 + tt * 128:xoff + kc * 512 + (tt + 1) * 128],
                                      wv_sb[:, kc * 1024 + dvb * 512:kc * 1024 + (dvb + 1) * 512],
                                      start=(kc == 0), stop=(kc == NKC - 1))
                    pe.inc(ii)
                    if gidx >= 2:
                        t.wait_ge(ve.h, ve_prev + (gidx - 2) + 1)
                    else:
                        t.wait_ge(ve.h, ve_prev)
                    for kc in range(NKC):
                        t.matmul(psg[gidx % 2][:],
                                 xsb[:, xoff + kc * 512 + tt * 128:xoff + kc * 512 + (tt + 1) * 128],
                                 wg_sb[:, kc * 1024 + dvb * 512:kc * 1024 + (dvb + 1) * 512],
                                 start=(kc == 0), stop=False)
                    ii = t.matmul(psg[gidx % 2][:], onesr[:],
                                  bg_sb[:, dvb * 512:(dvb + 1) * 512],
                                  start=False, stop=True)
                    pe.inc(ii)
                    gidx += 1

        @block.scalar
        def _(s):
            gidx = 0
            for tt in range(4):
                for dvb in range(2):
                    s.wait_ge(pe.h, pe_base + 2 * gidx + 1)
                    if tb > 0 and gidx == 0:
                        s.wait_ge(st.h, st_prev)
                    ii = s.activation(out=vsb[:, gidx * 512:(gidx + 1) * 512],
                                      in_=psv[gidx % 2][:], func=AF.Copy)
                    se.inc(ii)
                    gidx += 1

        @block.vector
        def _(v):
            gidx = 0
            for tt in range(4):
                for dvb in range(2):
                    v.wait_ge(pe.h, pe_base + 2 * gidx + 2)
                    if tb > 0 and gidx == 0:
                        v.wait_ge(st.h, st_prev)
                    ve.inc(v.tensor_copy(out=gsb[:, gidx * 512:(gidx + 1) * 512],
                                         in_=psg[gidx % 2][:]))
                    gidx += 1

        se_tb = se.n
        ve_tb = ve.n

        @block.sync
        def _(sy):
            sy.wait_ge(se.h, se_tb)
            sy.wait_ge(ve.h, ve_tb)
            gidx = 0
            for tt in range(4):
                for dvb in range(2):
                    rows = slice(tb * 512 + tt * 128, tb * 512 + (tt + 1) * 128)
                    cols = slice(dvb * 512, (dvb + 1) * 512)
                    st.dma(sy.dma_start(out=vt[rows, cols],
                                        in_=vsb[:, gidx * 512:(gidx + 1) * 512]))
                    st.dma(sy.dma_start(out=gt[rows, cols],
                                        in_=gsb[:, gidx * 512:(gidx + 1) * 512]))
                    gidx += 1
            if tb == NTB - 1:
                sy.wait_ge(st.h, st.n)
                if done_sem is not None:
                    done_sem.inc(sy.nop())
    return st



def emit_gla_core(env, block, pre, q_name, k_name, gk_name, v_name, g_name,
                  o_name, nheads=4, wait_sems=(), done_sem=None):
    nc = env.nc
    NCH = T // 128
    H = nheads
    qf, kf, gkf = env.t[q_name], env.t[k_name], env.t[gk_name]
    vt, gt, of = env.t[v_name], env.t[g_name], env.t[o_name]
    HB = H * 128
    HV = H * 256
    qs = env.sb(pre + "_q", [128, HB], BF16)
    ks = env.sb(pre + "_k", [128, HB], BF16)
    gks = env.sb(pre + "_gk", [128, HB], F32)
    vs = env.sb(pre + "_v", [128, HV], BF16)
    gs = env.sb(pre + "_g", [128, HV], F32)
    qs2 = env.sb(pre + "_q2", [128, HB], BF16)
    ks2 = env.sb(pre + "_k2", [128, HB], BF16)
    gks2 = env.sb(pre + "_gk2", [128, HB], F32)
    vs2 = env.sb(pre + "_v2", [128, HV], BF16)
    gs2 = env.sb(pre + "_g2", [128, HV], F32)
    gkc = env.sb(pre + "_gkc", [128, HB], F32)
    e1 = env.sb(pre + "_e1", [128, HB], F32)
    e2 = env.sb(pre + "_e2", [128, HB], F32)
    e3 = env.sb(pre + "_e3", [128, HB], F32)
    qd = env.sb(pre + "_qd", [128, HB], BF16)
    kk = env.sb(pre + "_kk", [128, HB], BF16)
    kd = env.sb(pre + "_kd", [128, HB], BF16)
    kdT = env.sb(pre + "_kdT", [128, HB], BF16)
    amask = env.sb(pre + "_am", [128, HB], BF16)
    tril = env.sb(pre + "_tril", [128, 128], BF16)
    ident = env.sb(pre + "_id", [128, 128], BF16)
    S_f32 = env.sb(pre + "_S", [128, HV], F32)
    S_bf = env.sb(pre + "_Sb", [128, HV], BF16)
    sqs = env.sb(pre + "_sq", [128, HV], F32)
    stat = env.sb(pre + "_st", [128, H * 8], F32)
    on_sb = env.sb(pre + "_on", [128, HV], F32)
    gsil = env.sb(pre + "_gs", [128, HV], F32)
    ofin = env.sb(pre + "_ofin", [128, HV], BF16)
    oT = env.sb(pre + "_oT", [128, HV], BF16)
    ps_a = env.ps(pre + "_psa", [128, HB])
    ps_o = env.ps(pre + "_pso", [128, HV])
    ps_s2 = env.ps(pre + "_pss2", [128, HV])
    ps_t = env.ps(pre + "_pst", [128, HB], BF16)
    ps_t2 = env.ps(pre + "_pst2", [128, HV], BF16)

    ld = env.sem(pre + "ld")
    pe = env.sem(pre + "pe")
    ve = env.sem(pre + "ve")
    se = env.sem(pre + "se")
    st = env.sem(pre + "st")
    ge = env.sem(pre + "ge")

    @block.gpsimd
    def _(g):
        for _ws in wait_sems:
            _ws.wait(g)
        g.memset(S_f32[:], 0.0)
        ge.inc(g.memset(S_bf[:], 0.0))
    ge_init = ge.n

    @block.sync
    def _(sy):
        for _ws in wait_sems:
            _ws.wait(sy)
        ld.dma(sy.dma_start(out=tril[:], in_=env.t["mask4"][:, 0:128]))
        ld.dma(sy.dma_start(out=ident[:], in_=env.t["ident"][:, :]))

    P = {}
    for n in range(NCH):
        bufs = (qs, ks, gks, vs, gs) if n % 2 == 0 else (qs2, ks2, gks2, vs2, gs2)
        qb, kb, gkb, vb, gb = bufs

        @block.sync
        def _(sy):
            if n >= 2:
                sy.wait_ge(pe.h, P[("pe_all", n - 2)])
                sy.wait_ge(ve.h, P[("ve_all", n - 2)])
                sy.wait_ge(se.h, P[("se_all", n - 2)])
            if n >= 1:
                sy.wait_ge(ld.h, ld.n)
            for h in range(H):
                ld.dma(sy.dma_start(out=qb[:, h * 128:(h + 1) * 128],
                                    in_=qf[h * 128:(h + 1) * 128, n * 128:(n + 1) * 128]))
                ld.dma(sy.dma_start(out=kb[:, h * 128:(h + 1) * 128],
                                    in_=kf[h * 128:(h + 1) * 128, n * 128:(n + 1) * 128]))
                ld.dma(sy.dma_start(out=gkb[:, h * 128:(h + 1) * 128],
                                    in_=gkf[h * 128:(h + 1) * 128, n * 128:(n + 1) * 128]))
            ld.dma(sy.dma_start(out=vb[:], in_=vt[n * 128:(n + 1) * 128, :]))
            ld.dma(sy.dma_start(out=gb[:], in_=gt[n * 128:(n + 1) * 128, :]))
        ld_n = ld.n

        def hq(h):
            return slice(h * 128, (h + 1) * 128)

        def hv_(h):
            return slice(h * 256, (h + 1) * 256)

        @block.vector
        def _(v):
            v.wait_ge(ld.h, ld_n)
            if n > 0:
                v.wait_ge(se.h, P[("se_e", n - 1)])
            for h in range(H):
                ve.inc(v.tensor_tensor_scan(out=gkc[:, hq(h)], data0=gkb[:, hq(h)],
                                            data1=gkb[:, hq(h)], initial=0.0,
                                            op0=ALU.add, op1=ALU.bypass))
        P[("ve_scan", n)] = ve.n

        @block.scalar
        def _(s):
            if n == 0:
                s.wait_ge(ge.h, ge_init)
            s.wait_ge(ve.h, P[("ve_scan", n)])
            if n > 0:
                s.wait_ge(ve.h, P[("ve_S", n - 1)])
                s.wait_ge(ve.h, P[("ve_dx", n - 1)])
            for h in range(H):
                s.activation(out=e1[:, hq(h)], in_=gkc[:, hq(h)], func=AF.Exp)
                s.activation(out=e2[:, hq(h)], in_=gkc[:, hq(h)], func=AF.Exp, scale=-1.0)
                ii = s.activation(out=e3[:, hq(h)], in_=gkc[:, hq(h)], func=AF.Exp,
                                  scale=-1.0, bias=gkc[:, h * 128 + 127:h * 128 + 128])
            se.inc(ii)
        P[("se_e", n)] = se.n

        @block.vector
        def _(v):
            v.wait_ge(se.h, P[("se_e", n)])
            if n > 0:
                v.wait_ge(pe.h, P[("pe_o", n - 1)])
            for h in range(H):
                v.tensor_mul(out=qd[:, hq(h)], in0=qb[:, hq(h)], in1=e1[:, hq(h)])
                v.tensor_mul(out=kk[:, hq(h)], in0=kb[:, hq(h)], in1=e2[:, hq(h)])
                ii = v.tensor_mul(out=kd[:, hq(h)], in0=kb[:, hq(h)], in1=e3[:, hq(h)])
            ve.inc(ii)
        P[("ve_dx", n)] = ve.n

        @block.tensor
        def _(t):
            t.wait_ge(ve.h, P[("ve_dx", n)])
            if n > 0:
                t.wait_ge(ve.h, P[("ve_kdT", n - 1)])
            for h in range(H):
                ii = t.transpose(ps_t[:, hq(h)], kd[:, hq(h)], ident[:])
            pe.inc(ii)
        P[("pe_t", n)] = pe.n

        @block.vector
        def _(v):
            v.wait_ge(pe.h, P[("pe_t", n)])
            ii = v.tensor_copy(out=kdT[:], in_=ps_t[:])
            ve.inc(ii)
        P[("ve_kdT", n)] = ve.n

        @block.tensor
        def _(t):
            if n > 0:
                t.wait_ge(ve.h, P[("ve_am", n - 1)])
            for h in range(H):
                ii = t.matmul(ps_a[:, hq(h)], kk[:, hq(h)], qd[:, hq(h)],
                              start=True, stop=True)
            pe.inc(ii)
        P[("pe_a", n)] = pe.n

        @block.vector
        def _(v):
            v.wait_ge(pe.h, P[("pe_a", n)])
            for h in range(H):
                ii = v.tensor_mul(out=amask[:, hq(h)], in0=ps_a[:, hq(h)], in1=tril[:])
            ve.inc(ii)
        P[("ve_am", n)] = ve.n

        @block.tensor
        def _(t):
            t.wait_ge(ve.h, P[("ve_am", n)])
            if n > 0:
                t.wait_ge(ve.h, P[("ve_on", n - 1)])
                t.wait_ge(se.h, P[("se_Sb", n - 1)])
                t.wait_ge(ve.h, P[("ve_S", n - 1)])
            for h in range(H):
                t.matmul(ps_o[:, hv_(h)], qd[:, hq(h)], S_bf[:, hv_(h)],
                         start=True, stop=False)
                t.matmul(ps_o[:, hv_(h)], amask[:, hq(h)], vb[:, hv_(h)],
                         start=False, stop=True)
                ii = t.matmul(ps_s2[:, hv_(h)], kdT[:, hq(h)], vb[:, hv_(h)],
                              start=True, stop=True)
            pe.inc(ii)
        P[("pe_o", n)] = pe.n

        @block.vector
        def _(v):
            v.wait_ge(pe.h, P[("pe_o", n)])
            for h in range(H):
                ii = v.scalar_tensor_tensor(out=S_f32[:, hv_(h)], in0=S_f32[:, hv_(h)],
                                            scalar=e1[:, h * 128 + 127:h * 128 + 128],
                                            in1=ps_s2[:, hv_(h)],
                                            op0=ALU.mult, op1=ALU.add)
            ve.inc(ii)
        P[("ve_S", n)] = ve.n

        @block.scalar
        def _(s):
            s.wait_ge(ve.h, P[("ve_S", n)])
            ii = s.activation(out=S_bf[:], in_=S_f32[:], func=AF.Copy)
            se.inc(ii)
        P[("se_Sb", n)] = se.n

        @block.vector
        def _(v):
            v.wait_ge(pe.h, P[("pe_o", n)])
            for h in range(H):
                ii = v.reduce_sum(out=stat[:, h * 8:h * 8 + 1], in_=ps_o[:, hv_(h)], axis=AX)
            ve.inc(ii)
        P[("ve_s1", n)] = ve.n

        @block.scalar
        def _(s):
            s.wait_ge(pe.h, P[("pe_o", n)])
            if n > 0:
                s.wait_ge(ve.h, P[("ve_var", n - 1)])
            ii = s.activation(out=sqs[:], in_=ps_o[:], func=AF.Square)
            se.inc(ii)
        P[("se_sq", n)] = se.n

        @block.vector
        def _(v):
            def syn(ii):
                ve.inc(ii)
                v.wait_ge(ve.h, ve.n)
            v.wait_ge(se.h, P[("se_sq", n)])
            v.wait_ge(ve.h, P[("ve_s1", n)])
            for h in range(H):
                v.reduce_sum(out=stat[:, h * 8 + 1:h * 8 + 2], in_=sqs[:, hv_(h)], axis=AX)
            syn(v.engine_nop())
            for h in range(H):
                v.tensor_scalar(out=stat[:, h * 8:h * 8 + 1], in0=stat[:, h * 8:h * 8 + 1],
                                scalar1=1.0 / 256, scalar2=None, op0=ALU.mult)
                v.tensor_scalar(out=stat[:, h * 8 + 1:h * 8 + 2],
                                in0=stat[:, h * 8 + 1:h * 8 + 2],
                                scalar1=1.0 / 256, scalar2=1e-5, op0=ALU.mult, op1=ALU.add)
            syn(v.engine_nop())
            for h in range(H):
                v.tensor_mul(out=stat[:, h * 8 + 2:h * 8 + 3], in0=stat[:, h * 8:h * 8 + 1],
                             in1=stat[:, h * 8:h * 8 + 1])
            syn(v.engine_nop())
            for h in range(H):
                v.tensor_sub(out=stat[:, h * 8 + 1:h * 8 + 2],
                             in0=stat[:, h * 8 + 1:h * 8 + 2],
                             in1=stat[:, h * 8 + 2:h * 8 + 3])
            syn(v.engine_nop())
        P[("ve_var", n)] = ve.n

        @block.scalar
        def _(s):
            s.wait_ge(ve.h, P[("ve_var", n)])
            for h in range(H):
                ii = s.activation(out=stat[:, h * 8 + 3:h * 8 + 4],
                                  in_=stat[:, h * 8 + 1:h * 8 + 2], func=AF.Ln)
            se.inc(ii)
        P[("se_lnv", n)] = se.n

        @block.vector
        def _(v):
            v.wait_ge(se.h, P[("se_lnv", n)])
            for h in range(H):
                ii = v.tensor_scalar(out=stat[:, h * 8 + 3:h * 8 + 4],
                                     in0=stat[:, h * 8 + 3:h * 8 + 4],
                                     scalar1=-0.5, scalar2=None, op0=ALU.mult)
            ve.inc(ii)
        P[("ve_l2", n)] = ve.n

        @block.scalar
        def _(s):
            s.wait_ge(ve.h, P[("ve_l2", n)])
            for h in range(H):
                s.activation(out=stat[:, h * 8 + 4:h * 8 + 5],
                             in_=stat[:, h * 8 + 3:h * 8 + 4], func=AF.Exp)
            if n > 0:
                s.wait_ge(ve.h, P[("ve_on", n - 1)])
            ii = s.activation(out=gsil[:], in_=gb[:], func=AF.Silu)
            se.inc(ii)
        P[("se_rstd", n)] = se.n

        @block.vector
        def _(v):
            v.wait_ge(se.h, P[("se_rstd", n)])
            if n > 0:
                v.wait_ge(pe.h, P[("pe_oT", n - 1)])
            for h in range(H):
                v.tensor_scalar(out=on_sb[:, hv_(h)], in0=ps_o[:, hv_(h)],
                                scalar1=stat[:, h * 8:h * 8 + 1],
                                scalar2=stat[:, h * 8 + 4:h * 8 + 5],
                                op0=ALU.subtract, op1=ALU.mult)
            ve.inc(v.engine_nop())
            v.wait_ge(ve.h, ve.n)
            ii = v.tensor_mul(out=ofin[:], in0=on_sb[:], in1=gsil[:])
            ve.inc(ii)
        P[("ve_on", n)] = ve.n

        @block.tensor
        def _(t):
            t.wait_ge(ve.h, P[("ve_on", n)])
            if n > 0:
                t.wait_ge(ve.h, P[("ve_oT", n - 1)])
            for h in range(H):
                t.transpose(ps_t2[:, h * 256:h * 256 + 128], ofin[:, h * 256:h * 256 + 128],
                            ident[:])
                ii = t.transpose(ps_t2[:, h * 256 + 128:h * 256 + 256],
                                 ofin[:, h * 256 + 128:h * 256 + 256], ident[:])
            pe.inc(ii)
        P[("pe_oT", n)] = pe.n

        @block.vector
        def _(v):
            v.wait_ge(pe.h, P[("pe_oT", n)])
            v.wait_ge(st.h, st.n)
            ii = v.tensor_copy(out=oT[:], in_=ps_t2[:])
            ve.inc(ii)
        P[("ve_oT", n)] = ve.n

        @block.sync
        def _(sy):
            sy.wait_ge(ve.h, P[("ve_oT", n)])
            for h in range(H):
                st.dma(sy.dma_start(out=of[h * 256:h * 256 + 128, n * 128:(n + 1) * 128],
                                    in_=oT[:, h * 256:h * 256 + 128]))
                st.dma(sy.dma_start(out=of[h * 256 + 128:h * 256 + 256,
                                           n * 128:(n + 1) * 128],
                                    in_=oT[:, h * 256 + 128:h * 256 + 256]))
        P[("pe_all", n)] = pe.n
        P[("ve_all", n)] = ve.n
        P[("se_all", n)] = se.n

    @block.sync
    def _(sy):
        sy.wait_ge(st.h, st.n)
        if done_sem is not None:
            done_sem.inc(sy.nop())
    return st



def emit_l1ffn(env, block, pre, xn1f_name, s1_name, x1_name, out_name,
               n=TH, wait_sems=(), done_sem=None):
    nc = env.nc
    xn1f = env.t[xn1f_name]
    wT, rT = env.t["l1f_wT"], env.t["l1f_rT"]
    s1d, x1d = env.t[s1_name], env.t[x1_name]
    outd = env.t[out_name]
    wr_sb = env.sb(pre + "_wr", [128, NKC * D], BF16)
    _s1scope = env.phase()
    _s1scope.__enter__()
    wld = env.sb(pre + "_wld", [128, 2 * 2048], F32)
    rld = env.sb(pre + "_rld", [128, 2 * 2048], F32)
    tq1 = env.sb(pre + "_tq1", [128, 2048], F32)
    tq2 = env.sb(pre + "_tq2", [128, 2048], F32)
    xldf = None
    xq = env.sb(pre + "_xq", [128, NKC * 512], BF16)
    m1 = env.sb(pre + "_m1", [128, 512], F32)
    m2 = env.sb(pre + "_m2", [128, 512], F32)
    g1 = env.sb(pre + "_g1", [128, 2 * 512], F32)
    es = env.sb(pre + "_es", [128, 2 * 512], F32)
    ex = env.sb(pre + "_ex", [128, 2 * 512], F32)
    eo = env.sb(pre + "_eo", [128, NKC * 512], F32)
    ps = [env.ps(pre + f"_ps{i}", [128, 512]) for i in range(4)]

    wl = env.sem(pre + "wl")
    ld = env.sem(pre + "ld")
    rl = env.sem(pre + "rl")
    pe = env.sem(pre + "pe")
    ve = env.sem(pre + "ve")
    se = env.sem(pre + "se")
    st = env.sem(pre + "st")

    ve_wr_h = {}
    for kc in range(NKC):
        slot = kc % 2

        @block.gpsimd
        def _(g):
            for _ws in wait_sems:
                if kc == 0:
                    _ws.wait(g)
            if kc >= 2:
                g.wait_ge(ve.h, ve_wr_h[kc - 2])
            g.wait_ge(wl.h, wl.n)
            wl.dma(g.dma_start(out=wld[:, slot * 2048:(slot + 1) * 2048],
                               in_=wT[kc * 128:(kc + 1) * 128, :]))
            wl.dma(g.dma_start(out=rld[:, slot * 2048:(slot + 1) * 2048],
                               in_=rT[kc * 128:(kc + 1) * 128, :]))
        wl_kc = wl.n

        @block.vector
        def _(v):
            v.wait_ge(wl.h, wl_kc)
            wsl = slice(slot * 2048, (slot + 1) * 2048)

            def syn(ii):
                ve.inc(ii)
                v.wait_ge(ve.h, ve.n)
            v.tensor_scalar(out=tq1[:], in0=wld[:, wsl], scalar1=0.2, scalar2=0.6,
                            op0=ALU.is_gt, op1=ALU.mult)
            syn(v.tensor_scalar(out=tq2[:], in0=wld[:, wsl], scalar1=-0.2, scalar2=0.6,
                                op0=ALU.is_lt, op1=ALU.mult))
            syn(v.tensor_sub(out=tq1[:], in0=tq1[:], in1=tq2[:]))
            syn(v.tensor_add(out=wr_sb[:, kc * D:(kc + 1) * D], in0=tq1[:],
                             in1=rld[:, wsl]))
        ve_wr_h[kc] = ve.n
    ve_wr_done = ve.n
    _s1scope.__exit__(None, None, None)
    xldf = env.sb(pre + "_xld", [128, NKC * 512], F32)

    for tb in range(n // 512):
        se_prev = se.n
        ve_prev = ve.n
        st_base = st.n
        rl_base = rl.n

        @block.sync
        def _(sy):
            if tb == 0:
                for _ws in wait_sems:
                    _ws.wait(sy)
                sy.wait_ge(ve.h, ve_wr_done)
            else:
                sy.wait_ge(pe.h, pe.n)
            for kc in range(NKC):
                ld.dma(sy.dma_start(out=xldf[:, kc * 512:(kc + 1) * 512],
                                    in_=xn1f[kc * 128:(kc + 1) * 128,
                                             tb * 512:(tb + 1) * 512]))
        ld_tb = ld.n

        @block.vector
        def _(v):
            v.wait_ge(ld.h, ld_tb)
            v.wait_ge(ve.h, ve_wr_done)
            for kc in range(NKC):
                xsl = slice(kc * 512, (kc + 1) * 512)
                v.wait_ge(ve.h, ve.n)
                v.tensor_scalar(out=m1[:], in0=xldf[:, xsl], scalar1=0.33, scalar2=0.5,
                                op0=ALU.is_gt, op1=ALU.mult)
                ve.inc(v.tensor_scalar(out=m2[:], in0=xldf[:, xsl], scalar1=-0.33,
                                       scalar2=0.5, op0=ALU.is_lt, op1=ALU.mult))
                v.wait_ge(ve.h, ve.n)
                ve.inc(v.tensor_sub(out=xq[:, xsl], in0=m1[:], in1=m2[:]))
        ve_xq = ve.n
        ve_ep_base = ve.n
        pe_base = pe.n

        @block.tensor
        def _(t):
            t.wait_ge(ve.h, ve_xq)
            for mo in range(NKC):
                bank = ps[mo % 4]
                if mo >= 4:
                    t.wait_ge(se.h, se_prev + (mo - 4) + 1)
                else:
                    t.wait_ge(se.h, se_prev)
                for kc in range(NKC):
                    ii = t.matmul(bank[:], wr_sb[:, kc * D + mo * 128:kc * D + (mo + 1) * 128],
                                  xq[:, kc * 512:(kc + 1) * 512],
                                  start=(kc == 0), stop=(kc == NKC - 1))
                pe.inc(ii)

        @block.scalar
        def _(s):
            for mo in range(NKC):
                s.wait_ge(pe.h, pe_base + mo + 1)
                if mo >= 2:
                    s.wait_ge(ve.h, ve_ep_base + 2 * (mo - 1))
                se.inc(s.activation(out=g1[:, (mo % 2) * 512:(mo % 2 + 1) * 512],
                                    in_=ps[mo % 4][:], func=AF.Silu))

        @block.gpsimd
        def _(g):
            for mo in range(0, NKC, 2):
                if mo >= 2:
                    g.wait_ge(ve.h, ve_ep_base + 2 * mo)
                elif tb > 0:
                    g.wait_ge(ve.h, ve_prev)
                g.wait_ge(rl.h, rl.n)
                for m2 in (mo, mo + 1):
                    rl.dma(g.dma_start(out=es[:, (m2 % 2) * 512:(m2 % 2 + 1) * 512],
                                       in_=s1d[m2 * 128:(m2 + 1) * 128, tb * 512:(tb + 1) * 512]))
                    rl.dma(g.dma_start(out=ex[:, (m2 % 2) * 512:(m2 % 2 + 1) * 512],
                                       in_=x1d[m2 * 128:(m2 + 1) * 128, tb * 512:(tb + 1) * 512]))
        rl_tb = rl.n

        @block.vector
        def _(v):
            v.wait_ge(st.h, st_base)
            for mo in range(NKC):
                v.wait_ge(se.h, se_prev + mo + 1)
                v.wait_ge(rl.h, rl_base + 64 * (mo // 2 + 1))
                sl2 = slice((mo % 2) * 512, (mo % 2 + 1) * 512)
                ve.inc(v.tensor_mul(out=eo[:, mo * 512:(mo + 1) * 512],
                                    in0=g1[:, sl2], in1=es[:, sl2]))
                v.wait_ge(ve.h, ve.n)
                ve.inc(v.tensor_add(out=eo[:, mo * 512:(mo + 1) * 512],
                                    in0=eo[:, mo * 512:(mo + 1) * 512], in1=ex[:, sl2]))

        @block.sync
        def _(sy):
            for mo in range(NKC):
                sy.wait_ge(ve.h, ve_ep_base + 2 * (mo + 1))
                st.dma(sy.dma_start(out=outd[mo * 128:(mo + 1) * 128,
                                             tb * 512:(tb + 1) * 512],
                                    in_=eo[:, mo * 512:(mo + 1) * 512]))
            if tb == n // 512 - 1:
                sy.wait_ge(st.h, st.n)
                if done_sem is not None:
                    done_sem.inc(sy.nop())
    return st



def _inp(env, name, shape, dt):
    env.dram(name, shape, dt, kind="ExternalInput")


def _out(env, name, shape, dt):
    env.dram(name, shape, dt, kind="ExternalOutput")


def build_L1():
    nc = bass.Bass("TRN2", target_bir_lowering=False)
    env = Env(nc)
    with env.ctx:
        _inp(env, "xT", [D, T], F32)
        _inp(env, "ident", [128, 128], BF16)
        _inp(env, "cosT", [64, T], F32)
        _inp(env, "sinT", [64, T], F32)
        _inp(env, "mask4", [128, 2048], BF16)
        for w in ("l0_wq", "l0_wk", "l0_wv", "l0f_wq", "l0f_wk", "l0f_wv"):
            _inp(env, w, [D, 1024], F32)
        _inp(env, "l0f_ln1", [128, NKC], F32)
        _out(env, "xn_f32", [D, T], F32)
        _out(env, "y", [1024, T], BF16)
        _out(env, "y2", [1024, T], BF16)
        env.dram("xn_bf", [D, T], BF16)
        env.dram("h_bf", [D, T], BF16)
        for nm in ("q", "k", "q2", "k2"):
            env.dram(nm, [1024, T], BF16)
        for nm in ("v", "v2"):
            env.dram(nm, [T, 1024], BF16)
        with nc.Block() as block:
            d0 = env.sem("d0"); d1 = env.sem("d1"); d2 = env.sem("d2")
            d3 = env.sem("d3"); d4 = env.sem("d4"); d5 = env.sem("d5")
            with env.phase():
                emit_ln(env, block, "ln0", "xT", dst_bf="xn_bf", dst_f32="xn_f32",
                        done_sem=d0)
            with env.phase():
                emit_qkv_rope(env, block, "qkv0", "xn_bf", "l0_wq", "l0_wk", "l0_wv",
                              "q", "k", "v", rope=True, wait_sems=(d0,), done_sem=d1)
            with env.phase():
                emit_attn(env, block, "at0", "q", "k", "v", "y", 128.0 ** -0.5,
                          extra_recip_scale=1.0 / 3.0, wait_sems=(d1,), done_sem=d2)
            with env.phase():
                emit_ln(env, block, "rms1", "xn_f32", dst_bf="h_bf", eps=1e-6,
                        weight="l0f_ln1", rms=True, wait_sems=(d2,), done_sem=d3)
            with env.phase():
                emit_qkv_rope(env, block, "qkv1", "h_bf", "l0f_wq", "l0f_wk", "l0f_wv",
                              "q2", "k2", "v2", rope=False, wait_sems=(d3,), done_sem=d4)
            with env.phase():
                emit_attn(env, block, "at1", "q2", "k2", "v2", "y2", 128.0 ** -0.5,
                          extra_recip_scale=1.0, wait_sems=(d4,), done_sem=d5)
    return nc


def build_L2():
    nc = bass.Bass("TRN2", target_bir_lowering=False)
    env = Env(nc)
    with env.ctx:
        _inp(env, "xn_own", [D, TH], F32)
        _inp(env, "y_own", [D, TH], BF16)
        _inp(env, "y2_own", [D, TH], BF16)
        _inp(env, "xT_own", [D, TH], F32)
        _inp(env, "l0_wo", [D, D], F32)
        _inp(env, "l0f_wo", [D, D], F32)
        _inp(env, "l0f_ln2", [128, NKC], F32)
        _inp(env, "l0f_gate", [D, 16], F32)
        _inp(env, "l0f_egT", [16, 11, 128, 2048], F32)
        _inp(env, "l0f_euT", [16, 11, 128, 2048], F32)
        _inp(env, "l0f_edT", [16, 16, 128, 1408], F32)
        _inp(env, "l0f_sgT", [22, 128, D], F32)
        _inp(env, "l0f_suT", [22, 128, D], F32)
        _inp(env, "l0f_sdT", [2816, D], F32)
        _inp(env, "ident", [128, 128], BF16)
        _inp(env, "ident32", [128, 128], F32)
        _out(env, "x1_f32", [D, TH], F32)
        _out(env, "xn1_bf", [D, TH], BF16)
        _out(env, "xn1_f32", [D, TH], F32)
        env.dram("s_f32", [D, TH], F32)
        env.dram("x2_f32", [D, TH], F32)
        env.dram("t_bf", [D, TH], BF16)
        env.dram("t_f32", [D, TH], F32)
        env.dram("cwT", [16, TH], F32)
        env.dram("shs_bf", [2816, TH], BF16)
        env.dram("x2s_f32", [D, TH], F32)
        with nc.Block() as block:
            ds = [env.sem(f"d{i}") for i in range(8)]
            with env.phase():
                emit_mm_fm(env, block, "wo0", "y_own", "l0_wo", TH, out_f32="s_f32",
                           done_sem=ds[0])
            with env.phase():
                emit_mm_fm(env, block, "wo1", "y2_own", "l0f_wo", TH,
                           residual="xn_own", out_f32="x2_f32",
                           wait_sems=(ds[0],), done_sem=ds[1])
            with env.phase():
                emit_ln(env, block, "rms2", "x2_f32", dst_bf="t_bf", dst_f32="t_f32",
                        eps=1e-6, weight="l0f_ln2", rms=True, n_tb=NTBH,
                        wait_sems=(ds[1],), done_sem=ds[2])
            with env.phase():
                emit_gate(env, block, "gate", "t_f32", "l0f_gate", "cwT", TH,
                          wait_sems=(ds[2],), done_sem=ds[3])
            with env.phase():
                emit_glu(env, block, "glu", "t_bf", "l0f_sgT", "l0f_suT", "shs_bf", TH,
                         wait_sems=(ds[3],), done_sem=ds[4])
            with env.phase():
                emit_mm_fm(env, block, "sdn", "shs_bf", "l0f_sdT", TH, din=2816,
                           residual="x2_f32", out_f32="x2s_f32",
                           wait_sems=(ds[4],), done_sem=ds[5])
            with env.phase():
                emit_moe(env, block, "moe", "t_bf", "cwT", "x2s_f32", "s_f32",
                         "xT_own", "x1_f32", wait_sems=(ds[5],), done_sem=ds[6])
            with env.phase():
                emit_ln(env, block, "ln1", "x1_f32", dst_bf="xn1_bf", dst_f32="xn1_f32",
                        n_tb=NTBH, wait_sems=(ds[6],), done_sem=ds[7])
    return nc


def build_L3():
    nc = bass.Bass("TRN2", target_bir_lowering=False)
    env = Env(nc)
    with env.ctx:
        _inp(env, "xn1_full", [D, T], BF16)
        _inp(env, "l1_wq", [D, 512], F32)
        _inp(env, "l1_wk", [D, 512], F32)
        _inp(env, "l1_wkg1", [D, 16], F32)
        _inp(env, "l1_wkg2", [16, 512], F32)
        _inp(env, "l1_bkg2", [128, 4], F32)
        _inp(env, "l1_wv", [D, 1024], F32)
        _inp(env, "l1_wg", [D, 1024], F32)
        _inp(env, "l1_bg", [1, 1024], F32)
        _inp(env, "ident", [128, 128], BF16)
        _inp(env, "mask4", [128, 2048], BF16)
        _out(env, "o", [1024, T], BF16)
        env.dram("gq", [512, T], BF16)
        env.dram("gkk", [512, T], BF16)
        env.dram("gk", [512, T], F32)
        env.dram("gv", [T, 1024], BF16)
        env.dram("gg", [T, 1024], F32)
        with nc.Block() as block:
            ds = [env.sem(f"d{i}") for i in range(5)]
            with env.phase():
                emit_mm_fm(env, block, "glq", "xn1_full", "l1_wq", T, dout=512,
                           out_bf="gq", done_sem=ds[0])
            with env.phase():
                emit_mm_fm(env, block, "glk", "xn1_full", "l1_wk", T, dout=512,
                           out_bf="gkk", evac_scale=1.0 / 16.0,
                           wait_sems=(ds[0],), done_sem=ds[1])
            with env.phase():
                emit_gla_gk(env, block, "ggk", "xn1_full", "gk",
                            wait_sems=(ds[1],), done_sem=ds[2])
            with env.phase():
                emit_gla_vg(env, block, "gvg", "xn1_full", "gv", "gg",
                            wait_sems=(ds[2],), done_sem=ds[3])
            with env.phase():
                emit_gla_core(env, block, "gc", "gq", "gkk", "gk", "gv", "gg", "o",
                              wait_sems=(ds[3],), done_sem=ds[4])
    return nc


def build_L4():
    nc = bass.Bass("TRN2", target_bir_lowering=False)
    env = Env(nc)
    with env.ctx:
        _inp(env, "o_own", [D, TH], BF16)
        _inp(env, "l1_wo", [D, D], F32)
        _inp(env, "xn1_f32", [D, TH], F32)
        _inp(env, "x1_f32", [D, TH], F32)
        _inp(env, "l1f_wT", [D, D], F32)
        _inp(env, "l1f_rT", [D, D], F32)
        _out(env, "out", [D, TH], F32)
        env.dram("s1_f32", [D, TH], F32)
        with nc.Block() as block:
            ds = [env.sem(f"d{i}") for i in range(2)]
            with env.phase():
                emit_mm_fm(env, block, "s1", "o_own", "l1_wo", TH, out_f32="s1_f32",
                           done_sem=ds[0])
            with env.phase():
                emit_l1ffn(env, block, "ff", "xn1_f32", "s1_f32", "x1_f32", "out",
                           wait_sems=(ds[0],), done_sem=ds[1])
    return nc


def kernel(**inputs):
    if FUSED:
        return kernel_fused(inputs)
    cores = host_prep(inputs)
    f32 = np.float32

    nc1 = build_L1()
    keys1 = ["xT", "ident", "cosT", "sinT", "mask4", "l0_wq", "l0_wk", "l0_wv",
             "l0f_wq", "l0f_wk", "l0f_wv", "l0f_ln1"]
    maps1 = [{k: cores[c][k] for k in keys1} for c in range(8)]
    r1 = run_bass_kernel_spmd(nc1, maps1, core_ids=list(range(8))).results

    maps2 = []
    for c in range(8):
        b, j = c // 2, c % 2
        own = slice(j * TH, (j + 1) * TH)
        y_own = np.concatenate([r1[2 * b]["y"][:, own], r1[2 * b + 1]["y"][:, own]], axis=0)
        y2_own = np.concatenate([r1[2 * b]["y2"][:, own], r1[2 * b + 1]["y2"][:, own]], axis=0)
        m = {"xn_own": np.ascontiguousarray(r1[c]["xn_f32"][:, own]),
             "y_own": np.ascontiguousarray(y_own),
             "y2_own": np.ascontiguousarray(y2_own),
             "xT_own": np.ascontiguousarray(cores[c]["xT"][:, own])}
        for k in ("l0_wo", "l0f_wo", "l0f_ln2", "l0f_gate", "l0f_egT", "l0f_euT",
                  "l0f_edT", "l0f_sgT", "l0f_suT", "l0f_sdT", "ident", "ident32"):
            m[k] = cores[c][k]
        maps2.append(m)
    nc2 = build_L2()
    r2 = run_bass_kernel_spmd(nc2, maps2, core_ids=list(range(8))).results

    maps3 = []
    for c in range(8):
        b = c // 2
        xn1_full = np.concatenate([r2[2 * b]["xn1_bf"], r2[2 * b + 1]["xn1_bf"]], axis=1)
        m = {"xn1_full": np.ascontiguousarray(xn1_full)}
        for k in ("l1_wq", "l1_wk", "l1_wkg1", "l1_wkg2", "l1_bkg2", "l1_wv",
                  "l1_wg", "l1_bg", "ident", "mask4"):
            m[k] = cores[c][k]
        maps3.append(m)
    nc3 = build_L3()
    r3 = run_bass_kernel_spmd(nc3, maps3, core_ids=list(range(8))).results

    maps4 = []
    for c in range(8):
        b, j = c // 2, c % 2
        own = slice(j * TH, (j + 1) * TH)
        o_own = np.concatenate([r3[2 * b]["o"][:, own], r3[2 * b + 1]["o"][:, own]], axis=0)
        m = {"o_own": np.ascontiguousarray(o_own),
             "xn1_f32": r2[c]["xn1_f32"], "x1_f32": r2[c]["x1_f32"]}
        for k in ("l1_wo", "l1f_wT", "l1f_rT"):
            m[k] = cores[c][k]
        maps4.append(m)
    nc4 = build_L4()
    r4 = run_bass_kernel_spmd(nc4, maps4, core_ids=list(range(8))).results

    out = np.empty((4, T, D), f32)
    for c in range(8):
        b, j = c // 2, c % 2
        out[b, j * TH:(j + 1) * TH, :] = r4[c]["out"].T
    return out



def _gvals(env, g):
    if not hasattr(env, "_gv"):
        pid = g.partition_id()
        j = g.compute_val(pid % 2)
        pj = g.compute_val(pid + 1 - 2 * (pid % 2))
        env._gv = {
            "0": 0,
            "j2048": g.compute_val(j * 2048),
            "oj2048": g.compute_val(2048 - j * 2048),
            "nj2048": g.compute_val(2048 - j * 2048),
            "j1024TH": g.compute_val(j * (1024 * TH)),
            "nj1024TH": g.compute_val((1 - j) * (1024 * TH)),
            "p1024TH": g.compute_val(pj * (1024 * TH)),
            "pDTH": g.compute_val(pj * (D * TH)),
            "nj2048c": g.compute_val(2048 - j * 2048),
        }
    return env._gv


def emit_copy_dyn(env, block, pre, copies, wait_sems=(), done_sem=None):
    nc = env.nc
    cp = env.sem(pre + "cp")

    @block.gpsimd
    def _(g):
        for _ws in wait_sems:
            _ws.wait(g)
        gv = _gvals(env, g)
        for (dst, dk_, src, sk_, rows, cols, srs, drs) in copies:
            cp.dma(g.dma_start(
                out=bass.AP(env.t[dst].tensor, gv[dk_], [[drs, rows], [1, cols]]),
                in_=bass.AP(env.t[src].tensor, gv[sk_], [[srs, rows], [1, cols]])))
        cp.wait(g)
        if done_sem is not None:
            done_sem.inc(g.nop())
    return cp


def emit_allgather(env, block, pre, pairs, wait_sems=(), done_sem=None,
                   block_until_done=True):
    nc = env.nc
    cc = env.sem(pre + "cc")

    @block.gpsimd
    def _(g):
        for _ws in wait_sems:
            _ws.wait(g)
        for contrib, gath in pairs:
            cc.inc(g.collective_compute(
                "AllGather", mybir.AluOpType.bypass,
                replica_groups=[[0, 1, 2, 3, 4, 5, 6, 7]],
                ins=[env.t[contrib][:, :].opt()], outs=[env.t[gath][:, :].opt()]))
        if block_until_done:
            cc.wait(g)
        if done_sem is not None:
            done_sem.inc(g.nop())
    return cc


def build_fused():
    nc = bass.Bass("TRN2", target_bir_lowering=False)
    env = Env(nc)
    with env.ctx:
        _inp(env, "xT", [D, T], F32)
        _inp(env, "xT_own", [D, TH], F32)
        _inp(env, "ident", [128, 128], BF16)
        _inp(env, "ident32", [128, 128], F32)
        _inp(env, "cosT", [64, T], F32)
        _inp(env, "sinT", [64, T], F32)
        _inp(env, "mask4", [128, 2048], BF16)
        for w in ("l0_wq", "l0_wk", "l0_wv", "l0f_wq", "l0f_wk", "l0f_wv"):
            _inp(env, w, [D, 1024], F32)
        _inp(env, "l0f_ln1", [128, NKC], F32)
        _inp(env, "l0f_ln2", [128, NKC], F32)
        _inp(env, "l0_wo", [D, D], F32)
        _inp(env, "l0f_wo", [D, D], F32)
        _inp(env, "l0f_gate", [D, 16], F32)
        _inp(env, "l0f_egT", [16, 11, 128, 2048], F32)
        _inp(env, "l0f_euT", [16, 11, 128, 2048], F32)
        _inp(env, "l0f_edT", [16, 16, 128, 1408], F32)
        _inp(env, "l0f_sgT", [22, 128, D], F32)
        _inp(env, "l0f_suT", [22, 128, D], F32)
        _inp(env, "l0f_sdT", [2816, D], F32)
        _inp(env, "l1_wq", [D, 512], F32)
        _inp(env, "l1_wk", [D, 512], F32)
        _inp(env, "l1_wkg1", [D, 16], F32)
        _inp(env, "l1_wkg2", [16, 512], F32)
        _inp(env, "l1_bkg2", [128, 4], F32)
        _inp(env, "l1_wv", [D, 1024], F32)
        _inp(env, "l1_wg", [D, 1024], F32)
        _inp(env, "l1_bg", [1, 1024], F32)
        _inp(env, "l1_wo", [D, D], F32)
        _inp(env, "l1f_wT", [D, D], F32)
        _inp(env, "l1f_rT", [D, D], F32)
        _out(env, "out", [D, TH], F32)
        env.dram("xn_bf", [D, T], BF16)
        env.dram("xn_f32", [D, T], F32)
        env.dram("h_bf", [D, T], BF16)
        for nm in ("q", "k", "q2", "k2"):
            env.dram(nm, [1024, T], BF16)
        for nm in ("v", "v2"):
            env.dram(nm, [T, 1024], BF16)
        env.dram("y", [1024, T], BF16)
        env.dram("y2", [1024, T], BF16)
        env.dram("nb0", [128, 8], F32)
        env.dram("nb1", [128, 8], F32)
        env.dram("cy", [1024, TH], BF16)
        env.dram("cy2", [1024, TH], BF16)
        env.dram("gy", [8 * 1024, TH], BF16)
        env.dram("gy2", [8 * 1024, TH], BF16)
        env.dram("y_own", [D, TH], BF16)
        env.dram("y2_own", [D, TH], BF16)
        env.dram("xn_own", [D, TH], F32)
        env.dram("s_f32", [D, TH], F32)
        env.dram("x2_f32", [D, TH], F32)
        env.dram("t_bf", [D, TH], BF16)
        env.dram("t_f32", [D, TH], F32)
        env.dram("cwT", [16, TH], F32)
        env.dram("shs_bf", [2816, TH], BF16)
        env.dram("x2s_f32", [D, TH], F32)
        env.dram("x1_f32", [D, TH], F32)
        env.dram("xn1_bf", [D, TH], BF16)
        env.dram("xn1_f32", [D, TH], F32)
        env.dram("gxn1", [8 * D, TH], BF16)
        env.dram("xn1_full", [D, T], BF16)
        env.dram("gq", [512, T], BF16)
        env.dram("gkk", [512, T], BF16)
        env.dram("gk", [512, T], F32)
        env.dram("gv", [T, 1024], BF16)
        env.dram("gg", [T, 1024], F32)
        env.dram("o", [1024, T], BF16)
        env.dram("co", [1024, TH], BF16)
        env.dram("go", [8 * 1024, TH], BF16)
        env.dram("o_own", [D, TH], BF16)
        env.dram("s1_f32", [D, TH], F32)

        with nc.Block() as block:
            ds = [env.sem(f"fd{i}") for i in range(26)]
            with env.phase():
                emit_ln(env, block, "ln0", "xT", dst_bf="xn_bf", dst_f32="xn_f32",
                        done_sem=ds[0])
            with env.phase():
                emit_qkv_rope(env, block, "qkv0", "xn_bf", "l0_wq", "l0_wk", "l0_wv",
                              "q", "k", "v", rope=True, wait_sems=(ds[0],), done_sem=ds[1])
            dstab0 = env.sem("dstab0")
            with env.phase():
                emit_stab(env, block, "sb0", "q", "k", "nb0", 128.0 ** -0.5,
                          wait_sems=(ds[1],), done_sem=dstab0)
            with env.phase():
                emit_attn(env, block, "at0", "q", "k", "v", "y", 128.0 ** -0.5,
                          "nb0", extra_recip_scale=1.0 / 3.0,
                          wait_sems=(dstab0,), done_sem=ds[2])
            with env.phase():
                emit_ln(env, block, "rms1", "xn_f32", dst_bf="h_bf", eps=1e-6,
                        weight="l0f_ln1", rms=True, wait_sems=(ds[2],), done_sem=ds[3])
            with env.phase():
                emit_qkv_rope(env, block, "qkv1", "h_bf", "l0f_wq", "l0f_wk", "l0f_wv",
                              "q2", "k2", "v2", rope=False, wait_sems=(ds[3],), done_sem=ds[4])
            dstab1 = env.sem("dstab1")
            with env.phase():
                emit_stab(env, block, "sb1", "q2", "k2", "nb1", 128.0 ** -0.5,
                          wait_sems=(ds[4],), done_sem=dstab1)
            with env.phase():
                emit_attn(env, block, "at1", "q2", "k2", "v2", "y2", 128.0 ** -0.5,
                          "nb1", extra_recip_scale=1.0,
                          wait_sems=(dstab1,), done_sem=ds[5])
            with env.phase():
                emit_copy_dyn(env, block, "exA", [
                    ("cy", "0", "y", "oj2048", 1024, TH, T, TH),
                    ("xn_own", "0", "xn_f32", "j2048", D, TH, T, TH),
                ], wait_sems=(ds[2],), done_sem=ds[6])
                emit_allgather(env, block, "agA", [("cy", "gy")],
                               wait_sems=(ds[6],), done_sem=ds[7],
                               block_until_done=False)
            with env.phase():
                emit_copy_dyn(env, block, "asA", [
                    ("y_own", "j1024TH", "y", "j2048", 1024, TH, T, TH),
                    ("y_own", "nj1024TH", "gy", "p1024TH", 1024, TH, TH, TH),
                ], wait_sems=(env.sem("agAcc"),), done_sem=ds[8])
                emit_copy_dyn(env, block, "exA2", [
                    ("cy2", "0", "y2", "oj2048", 1024, TH, T, TH),
                ], wait_sems=(ds[5],), done_sem=env.sem("fe2"))
                emit_allgather(env, block, "agA2", [("cy2", "gy2")],
                               wait_sems=(env.sem("fe2"),), done_sem=None,
                               block_until_done=False)
            with env.phase():
                emit_mm_fm(env, block, "wo0", "y_own", "l0_wo", TH, out_f32="s_f32",
                           wait_sems=(ds[8], ds[5]), done_sem=ds[9])
            with env.phase():
                emit_copy_dyn(env, block, "asA2", [
                    ("y2_own", "j1024TH", "y2", "j2048", 1024, TH, T, TH),
                    ("y2_own", "nj1024TH", "gy2", "p1024TH", 1024, TH, TH, TH),
                ], wait_sems=(env.sem("agAcc"),), done_sem=env.sem("fa2"))
            with env.phase():
                emit_mm_fm(env, block, "wo1", "y2_own", "l0f_wo", TH,
                           residual="xn_own", out_f32="x2_f32",
                           wait_sems=(ds[9], env.sem("fa2")), done_sem=ds[10])
            with env.phase():
                emit_ln(env, block, "rms2", "x2_f32", dst_bf="t_bf", dst_f32="t_f32",
                        eps=1e-6, weight="l0f_ln2", rms=True, n_tb=NTBH,
                        wait_sems=(ds[10],), done_sem=ds[11])
            with env.phase():
                emit_gate(env, block, "gate", "t_f32", "l0f_gate", "cwT", TH,
                          wait_sems=(ds[11],), done_sem=ds[12])
            with env.phase():
                emit_glu(env, block, "glu", "t_bf", "l0f_sgT", "l0f_suT", "shs_bf", TH,
                         wait_sems=(ds[12],), done_sem=ds[13])
            with env.phase():
                emit_mm_fm(env, block, "sdn", "shs_bf", "l0f_sdT", TH, din=2816,
                           residual="x2_f32", out_f32="x2s_f32",
                           wait_sems=(ds[13],), done_sem=ds[14])
            with env.phase():
                emit_moe(env, block, "moe", "t_bf", "cwT", "x2s_f32", "s_f32",
                         "xT_own", "x1_f32", wait_sems=(ds[14],), done_sem=ds[15])
            with env.phase():
                emit_ln(env, block, "ln1", "x1_f32", dst_bf="xn1_bf", dst_f32="xn1_f32",
                        n_tb=NTBH, wait_sems=(ds[15],), done_sem=ds[16])
            with env.phase():
                emit_allgather(env, block, "agB", [("xn1_bf", "gxn1")],
                               wait_sems=(ds[16],), done_sem=ds[17])
            with env.phase():
                emit_copy_dyn(env, block, "asB", [
                    ("xn1_full", "j2048", "xn1_bf", "0", D, TH, TH, T),
                    ("xn1_full", "nj2048c", "gxn1", "pDTH", D, TH, TH, T),
                ], wait_sems=(ds[17],), done_sem=ds[18])
            with env.phase():
                emit_mm_fm(env, block, "glq", "xn1_full", "l1_wq", T, dout=512,
                           out_bf="gq", wait_sems=(ds[18],), done_sem=ds[19])
            with env.phase():
                emit_mm_fm(env, block, "glk", "xn1_full", "l1_wk", T, dout=512,
                           out_bf="gkk", evac_scale=1.0 / 16.0,
                           wait_sems=(ds[19],), done_sem=ds[20])
            with env.phase():
                emit_gla_gk(env, block, "ggk", "xn1_full", "gk",
                            wait_sems=(ds[20],), done_sem=ds[21])
            with env.phase():
                emit_gla_vg(env, block, "gvg", "xn1_full", "gv", "gg",
                            wait_sems=(ds[21],), done_sem=ds[22])
            with env.phase():
                emit_gla_core(env, block, "gc", "gq", "gkk", "gk", "gv", "gg", "o",
                              wait_sems=(ds[22],), done_sem=ds[23])
            with env.phase():
                emit_copy_dyn(env, block, "exC", [
                    ("co", "0", "o", "oj2048", 1024, TH, T, TH),
                ], wait_sems=(ds[23],), done_sem=ds[24])
            with env.phase():
                emit_allgather(env, block, "agC", [("co", "go")],
                               wait_sems=(ds[24],), done_sem=ds[25])
            dsx = env.sem("fdx")
            dsy = env.sem("fdy")
            with env.phase():
                emit_copy_dyn(env, block, "asC", [
                    ("o_own", "j1024TH", "o", "j2048", 1024, TH, T, TH),
                    ("o_own", "nj1024TH", "go", "p1024TH", 1024, TH, TH, TH),
                ], wait_sems=(ds[25],), done_sem=dsx)
            with env.phase():
                emit_mm_fm(env, block, "s1", "o_own", "l1_wo", TH, out_f32="s1_f32",
                           wait_sems=(dsx,), done_sem=dsy)
            with env.phase():
                emit_l1ffn(env, block, "ff", "xn1_f32", "s1_f32", "x1_f32", "out",
                           wait_sems=(dsy,))
    return nc


FUSED = True


def kernel_fused(inputs):
    cores = host_prep(inputs)
    for c in range(8):
        j = c % 2
        cores[c]["xT_own"] = np.ascontiguousarray(cores[c]["xT"][:, j * TH:(j + 1) * TH])
    nc = build_fused()
    keys = ["xT", "xT_own", "ident", "ident32", "cosT", "sinT", "mask4",
            "l0_wq", "l0_wk", "l0_wv", "l0f_wq", "l0f_wk", "l0f_wv",
            "l0f_ln1", "l0f_ln2", "l0_wo", "l0f_wo", "l0f_gate",
            "l0f_egT", "l0f_euT", "l0f_edT", "l0f_sgT", "l0f_suT", "l0f_sdT",
            "l1_wq", "l1_wk", "l1_wkg1", "l1_wkg2", "l1_bkg2", "l1_wv", "l1_wg",
            "l1_bg", "l1_wo", "l1f_wT", "l1f_rT"]
    maps = [{k: cores[c][k] for k in keys} for c in range(8)]
    r = run_bass_kernel_spmd(nc, maps, core_ids=list(range(8))).results
    out = np.empty((4, T, D), np.float32)
    for c in range(8):
        b, j = c // 2, c % 2
        out[b, j * TH:(j + 1) * TH, :] = r[c]["out"].T
    return out


# version 09
# speedup vs baseline: 1.0717x; 1.0717x over previous
import numpy as np
from contextlib import ExitStack

import concourse.bass as bass
import concourse.mybir as mybir
from concourse.bass_utils import run_bass_kernel_spmd

F32 = mybir.dt.float32
BF16 = mybir.dt.bfloat16
AF = mybir.ActivationFunctionType
ALU = mybir.AluOpType
AX = mybir.AxisListType.X

D = 2048
T = 4096
TH = 2048
NKC = D // 128
NTB = T // 512
NTBH = TH // 512


class Sem:
    def __init__(self, nc, ctx, name):
        self.h = ctx.enter_context(nc.semaphore(name))
        self.n = 0

    def inc(self, instr, k=1):
        instr.then_inc(self.h, k)
        self.n += k
        return self.n

    def dma(self, instr):
        return self.inc(instr, 16)

    def wait(self, eng, at=None):
        eng.wait_ge(self.h, self.n if at is None else at)


class Env:
    def __init__(self, nc):
        self.nc = nc
        self.ctx = ExitStack()
        self.t = {}
        self.sems = {}
        self._scopes = [self.ctx]

    _ROLES = ("ldA", "ldB", "rlA", "rlB", "pe", "ve", "se", "st", "ld", "wl",
              "ge", "ms", "rl", "ul", "el", "tl", "cp", "cc")

    def sem(self, name):
        key = name
        for r in self._ROLES:
            if name.endswith(r):
                key = "@" + r
                break
        if key not in self.sems:
            self.sems[key] = Sem(self.nc, self.ctx, key.replace("@", "g_"))
        return self.sems[key]

    def phase(self):
        from contextlib import contextmanager

        @contextmanager
        def _p():
            st = ExitStack()
            self._scopes.append(st)
            try:
                yield st
            finally:
                self._scopes.pop()
                st.close()
        return _p()

    def sb(self, name, shape, dt):
        return self._scopes[-1].enter_context(self.nc.sbuf_tensor(name, shape, dt))

    def ps(self, name, shape, dt=F32):
        return self._scopes[-1].enter_context(self.nc.psum_tensor(name, shape, dt))

    def dram(self, name, shape, dt, kind=None):
        if kind is None:
            h = self.nc.dram_tensor(name, shape, dt)
        else:
            h = self.nc.dram_tensor(name, shape, dt, kind=kind)
        self.t[name] = h.ap()
        return self.t[name]



def host_prep(inputs):
    import ml_dtypes
    f32 = np.float32
    asc = np.ascontiguousarray
    bf16 = ml_dtypes.bfloat16
    ins = {k: np.asarray(v) for k, v in inputs.items()}

    ident = np.eye(128).astype(bf16)
    ident32 = np.eye(128, dtype=f32)
    af = (1.0 / 1024.0) ** np.linspace(0.0, 1.0, 32, dtype=f32)
    af = np.concatenate([af, np.zeros(32, f32)])
    theta = np.arange(T, dtype=f32)[:, None] * af[None, :]
    cosT = asc(np.cos(theta).T.astype(f32))
    sinT = asc(np.sin(theta).T.astype(f32))
    p = np.arange(128)[:, None]
    f = np.arange(512)[None, :]
    mask4 = np.concatenate([(f >= p + off).astype(f32) for off in (0, 128, 256, 384)],
                           axis=1).astype(bf16)

    def vec_pk(v):
        n = v.shape[0] // 128
        return asc(v.reshape(n, 128).T.astype(f32))

    cores = []
    for c in range(8):
        b, j = c // 2, c % 2
        hs = slice(1024 * j, 1024 * (j + 1))
        hs1 = slice(512 * j, 512 * (j + 1))
        hv1 = slice(1024 * j, 1024 * (j + 1))
        d = {
            "xT": asc(ins["L0_attn_x"][b].T.astype(f32)),
            "ident": ident, "ident32": ident32, "cosT": cosT, "sinT": sinT, "mask4": mask4,
            "l0_wq": asc(ins["L0_attn_Wq"].T[:, hs]),
            "l0_wk": asc(ins["L0_attn_Wk"].T[:, hs]),
            "l0_wv": asc(ins["L0_attn_Wv"].T[:, hs]),
            "l0_wo": asc(ins["L0_attn_Wo"].T),
            "l0f_ln1": vec_pk(ins["L0_ffn_ln1_w"]),
            "l0f_ln2": vec_pk(ins["L0_ffn_ln2_w"]),
            "l0f_wq": asc(ins["L0_ffn_wq"][:, hs]),
            "l0f_wk": asc(ins["L0_ffn_wk"][:, hs]),
            "l0f_wv": asc(ins["L0_ffn_wv"][:, hs]),
            "l0f_wo": ins["L0_ffn_wo"],
            "l0f_gate": asc(ins["L0_ffn_gate_w"].T),
            "l0f_egT": asc(ins["L0_ffn_eg_w"].reshape(16, 11, 128, 16, 128)
                           .transpose(0, 1, 4, 3, 2).reshape(16, 11, 128, 2048)),
            "l0f_euT": asc(ins["L0_ffn_eu_w"].reshape(16, 11, 128, 16, 128)
                           .transpose(0, 1, 4, 3, 2).reshape(16, 11, 128, 2048)),
            "l0f_edT": asc(ins["L0_ffn_ed_w"].reshape(16, 16, 128, 11, 128)
                           .transpose(0, 1, 4, 3, 2).reshape(16, 16, 128, 1408)),
            "l0f_sgT": asc(ins["L0_ffn_sg_w"].reshape(22, 128, 16, 128)
                           .transpose(0, 3, 2, 1).reshape(22, 128, 2048)),
            "l0f_suT": asc(ins["L0_ffn_su_w"].reshape(22, 128, 16, 128)
                           .transpose(0, 3, 2, 1).reshape(22, 128, 2048)),
            "l0f_sdT": asc(ins["L0_ffn_sd_w"].T),
            "l1_wq": asc(ins["L1_attn_Wq"][:, hs1]),
            "l1_wk": asc(ins["L1_attn_Wk"][:, hs1]),
            "l1_wkg1": ins["L1_attn_Wkg1"],
            "l1_wkg2": asc(ins["L1_attn_Wkg2"][:, hs1]),
            "l1_bkg2": vec_pk(ins["L1_attn_bkg2"][hs1]),
            "l1_wv": asc(ins["L1_attn_Wv"][:, hv1]),
            "l1_wg": asc(ins["L1_attn_Wg"][:, hv1]),
            "l1_bg": asc(ins["L1_attn_bg"][hv1].astype(f32).reshape(1, 1024)),
            "l1_wo": ins["L1_attn_Wo"],
            "l1f_wT": asc(ins["L1_ffn_weight"].T),
            "l1f_rT": asc(ins["L1_ffn_rram_variation"].T),
        }
        cores.append(d)
    return cores



def emit_ln(env, block, pre, src_name, dst_bf=None, dst_f32=None, n_tb=NTB,
            eps=1e-5, weight=None, rms=False, wait_sems=(), done_sem=None,
            src_cols=None, rms2_bf=None, rms2_weight=None, eps2=1e-6):
    nc = env.nc
    x = env.t[src_name]
    xnb = env.t[dst_bf] if dst_bf else None
    xnf = env.t[dst_f32] if dst_f32 else None
    xs = env.sb(pre + "_x", [128, NKC * 512], F32)
    sq = env.sb(pre + "_sq", [128, NKC * 512], BF16)
    onesb = env.sb(pre + "_ones", [128, 1], BF16)
    onesf = env.sb(pre + "_onesf", [128, 1], F32)
    ones1 = env.sb(pre + "_ones1", [1, 128], F32)
    wsb = env.sb(pre + "_w", [128, NKC], F32) if weight is not None else None
    w2sb = env.sb(pre + "_w2", [128, NKC], F32) if rms2_weight is not None else None
    h2sb = env.sb(pre + "_h2", [128, NKC * 512], BF16) if rms2_bf is not None else None
    tmp2 = env.sb(pre + "_t2", [128, NKC * 512], F32) if rms2_bf is not None else None
    stats = env.sb(pre + "_stats", [1, 512 * 8], F32)
    rsb = env.sb(pre + "_r", [128, 1536], F32)
    tmp = env.sb(pre + "_tmp", [128, NKC * 512], F32)
    outf = env.sb(pre + "_outf", [128, NKC * 512], F32)
    outw = env.sb(pre + "_outw", [128, NKC * 512], F32) if weight is not None else None
    outb = env.sb(pre + "_outb", [128, NKC * 512], BF16)
    ps_s = env.ps(pre + "_ps_s", [1, 512])
    ps_s2 = env.ps(pre + "_ps_s2", [1, 512])
    ps_r = env.ps(pre + "_ps_r", [128, 1536])

    ld = env.sem(pre + "ld")
    pe = env.sem(pre + "pe")
    ve = env.sem(pre + "ve")
    se = env.sem(pre + "se")
    st = env.sem(pre + "st")

    @block.gpsimd
    def _(g):
        for _ws in wait_sems:
            _ws.wait(g)
        g.memset(onesb[:], 1.0)
        g.memset(onesf[:], 1.0)
        env.sem(pre + "ge").inc(g.memset(ones1[:], 1.0))
    ge = env.sem(pre + "ge")
    ge_const = ge.n

    if weight is not None or rms2_weight is not None:
        @block.sync
        def _(sy):
            for _ws in wait_sems:
                _ws.wait(sy)
            if weight is not None:
                ld.dma(sy.dma_start(out=wsb[:], in_=env.t[weight][:, :]))
            if rms2_weight is not None:
                ld.dma(sy.dma_start(out=w2sb[:], in_=env.t[rms2_weight][:, :]))

    first = [True]
    st_after = [0]
    for tb in range(n_tb):
        ve_app_prev = ve.n
        se_out_prev = se.n
        st_prev_tb = st_after[0]

        @block.sync
        def _(sy):
            if first[0]:
                for wsem in wait_sems:
                    wsem.wait(sy)
                first[0] = False
            else:
                sy.wait_ge(ve.h, ve_app_prev)
                sy.wait_ge(se.h, se_out_prev)
            for kc in range(NKC):
                ld.dma(sy.dma_start(out=xs[:, kc * 512:(kc + 1) * 512],
                                    in_=x[kc * 128:(kc + 1) * 128, tb * 512:(tb + 1) * 512]))
        ld_tb = ld.n

        @block.scalar
        def _(s):
            if tb == 0:
                for _ws in wait_sems:
                    _ws.wait(s)
            s.wait_ge(ld.h, ld_tb)
            for kc in range(NKC):
                ii = s.activation(out=sq[:, kc * 512:(kc + 1) * 512],
                                  in_=xs[:, kc * 512:(kc + 1) * 512], func=AF.Square)
            se.inc(ii)
        se_sq = se.n

        @block.tensor
        def _(t):
            if tb == 0:
                for _ws in wait_sems:
                    _ws.wait(t)
            t.wait_ge(ge.h, ge_const)
            t.wait_ge(ld.h, ld_tb)
            if tb > 0:
                t.wait_ge(ve.h, ve_app_prev)
            if not rms:
                for kc in range(NKC):
                    ii = t.matmul(ps_s[:], onesf[:], xs[:, kc * 512:(kc + 1) * 512],
                                  start=(kc == 0), stop=(kc == NKC - 1))
            t.wait_ge(se.h, se_sq)
            for kc in range(NKC):
                ii = t.matmul(ps_s2[:], onesb[:], sq[:, kc * 512:(kc + 1) * 512],
                              start=(kc == 0), stop=(kc == NKC - 1))
            pe.inc(ii)
        pe_stats = pe.n

        mu = stats[:, 0:512]
        musq = stats[:, 512:1024]
        var = stats[:, 1024:1536]
        lnv = stats[:, 1536:2048]
        rstd = stats[:, 2048:2560]
        nmr = stats[:, 2560:3072]

        @block.vector
        def _(v):
            if tb == 0:
                for _ws in wait_sems:
                    _ws.wait(v)
            v.wait_ge(pe.h, pe_stats)
            if not rms:
                ve.inc(v.tensor_scalar(out=mu[:], in0=ps_s[:], scalar1=1.0 / D,
                                       scalar2=None, op0=ALU.mult))
                v.wait_ge(ve.h, ve.n)
                ve.inc(v.tensor_mul(out=musq[:], in0=mu[:], in1=mu[:]))
                ve.inc(v.tensor_scalar(out=var[:], in0=ps_s2[:], scalar1=1.0 / D,
                                       scalar2=eps, op0=ALU.mult, op1=ALU.add))
                v.wait_ge(ve.h, ve.n)
                ve.inc(v.tensor_sub(out=var[:], in0=var[:], in1=musq[:]))
            else:
                ve.inc(v.tensor_scalar(out=var[:], in0=ps_s2[:], scalar1=1.0 / D,
                                       scalar2=eps, op0=ALU.mult, op1=ALU.add))
        ve_var = ve.n

        if rms2_bf is not None:
            @block.vector
            def _(v):
                def syn(ii):
                    ve.inc(ii)
                    v.wait_ge(ve.h, ve.n)
                v.wait_ge(ve.h, ve_var)
                r2p = stats[:, 3072:3584]
                syn(v.reciprocal(out=r2p[:], in_=var[:]))
                syn(v.tensor_scalar(out=r2p[:], in0=r2p[:], scalar1=-eps,
                                    scalar2=1.0 + eps2, op0=ALU.mult, op1=ALU.add))
            ve_var = ve.n

        @block.scalar
        def _(s):
            s.wait_ge(ve.h, ve_var)
            if rms2_bf is not None:
                s.activation(out=stats[:, 3584:4096], in_=stats[:, 3072:3584], func=AF.Ln)
            se.inc(s.activation(out=lnv[:], in_=var[:], func=AF.Ln))
        se_ln = se.n

        @block.vector
        def _(v):
            v.wait_ge(se.h, se_ln)
            if rms2_bf is not None:
                v.tensor_scalar(out=stats[:, 3584:4096], in0=stats[:, 3584:4096],
                                scalar1=-0.5, scalar2=None, op0=ALU.mult)
            ve.inc(v.tensor_scalar(out=lnv[:], in0=lnv[:], scalar1=-0.5, scalar2=None,
                                   op0=ALU.mult))
        ve_lnv = ve.n

        @block.scalar
        def _(s):
            s.wait_ge(ve.h, ve_lnv)
            if rms2_bf is not None:
                s.activation(out=stats[:, 3584:4096], in_=stats[:, 3584:4096], func=AF.Exp)
            se.inc(s.activation(out=rstd[:], in_=lnv[:], func=AF.Exp))
        se_rstd = se.n

        @block.vector
        def _(v):
            v.wait_ge(se.h, se_rstd)
            if not rms:
                ii = v.scalar_tensor_tensor(out=nmr[:], in0=mu[:],
                                            scalar=-1.0, in1=rstd[:],
                                            op0=ALU.mult, op1=ALU.mult)
                ve.inc(ii)
        ve_nmr = ve.n

        @block.tensor
        def _(t):
            t.wait_ge(se.h, se_rstd)
            t.wait_ge(ve.h, ve_nmr)
            ii = t.matmul(ps_r[:, 0:512], ones1[:], rstd[:], start=True, stop=True)
            if not rms:
                ii = t.matmul(ps_r[:, 512:1024], ones1[:], nmr[:], start=True, stop=True)
            if rms2_bf is not None:
                ii = t.matmul(ps_r[:, 1024:1536], ones1[:], stats[:, 3584:4096],
                              start=True, stop=True)
            pe.inc(ii)
        pe_bc = pe.n

        @block.vector
        def _(v):
            v.wait_ge(pe.h, pe_bc)
            ve.inc(v.tensor_copy(out=rsb[:, 0:512], in_=ps_r[:, 0:512]))
            if not rms:
                ve.inc(v.tensor_copy(out=rsb[:, 512:1024], in_=ps_r[:, 512:1024]))
            if rms2_bf is not None:
                ve.inc(v.tensor_copy(out=rsb[:, 1024:1536], in_=ps_r[:, 1024:1536]))
            v.wait_ge(ve.h, ve.n)
            if tb > 0:
                v.wait_ge(se.h, se_out_prev)
            if rms:
                for kc in range(NKC):
                    sl = slice(kc * 512, (kc + 1) * 512)
                    ii = v.tensor_tensor(out=outf[:, sl], in0=xs[:, sl], in1=rsb[:, 0:512],
                                         op=ALU.mult)
            else:
                for kc in range(NKC):
                    sl = slice(kc * 512, (kc + 1) * 512)
                    ii = v.tensor_tensor(out=tmp[:, sl], in0=xs[:, sl], in1=rsb[:, 0:512],
                                         op=ALU.mult)
                ve.inc(ii)
                v.wait_ge(ve.h, ve.n)
                for kc in range(NKC):
                    sl = slice(kc * 512, (kc + 1) * 512)
                    ii = v.tensor_tensor(out=outf[:, sl], in0=tmp[:, sl], in1=rsb[:, 512:1024],
                                         op=ALU.add)
            ve.inc(ii)
            if rms2_bf is not None:
                v.wait_ge(ve.h, ve.n)
                for kc in range(NKC):
                    sl = slice(kc * 512, (kc + 1) * 512)
                    ii = v.tensor_tensor(out=tmp2[:, sl], in0=outf[:, sl],
                                         in1=rsb[:, 1024:1536], op=ALU.mult)
                ve.inc(ii)
        ve_app = ve.n

        @block.scalar
        def _(s):
            s.wait_ge(ve.h, ve_app)
            if tb > 0:
                s.wait_ge(st.h, st_prev_tb)
            for kc in range(NKC):
                sl = slice(kc * 512, (kc + 1) * 512)
                if weight is not None:
                    ii = s.activation(out=outb[:, sl], in_=outf[:, sl], func=AF.Copy,
                                      scale=wsb[:, kc:kc + 1])
                    if xnf is not None:
                        ii = s.activation(out=outw[:, sl], in_=outf[:, sl], func=AF.Copy,
                                          scale=wsb[:, kc:kc + 1])
                else:
                    ii = s.activation(out=outb[:, sl], in_=outf[:, sl], func=AF.Copy)
                if rms2_bf is not None:
                    ii = s.activation(out=h2sb[:, sl], in_=tmp2[:, sl], func=AF.Copy,
                                      scale=w2sb[:, kc:kc + 1])
            se.inc(ii)
        se_out = se.n

        @block.sync
        def _(sy):
            sy.wait_ge(se.h, se_out)
            for kc in range(NKC):
                if xnb is not None:
                    st.dma(sy.dma_start(out=xnb[kc * 128:(kc + 1) * 128, tb * 512:(tb + 1) * 512],
                                        in_=outb[:, kc * 512:(kc + 1) * 512]))
                if xnf is not None:
                    st.dma(sy.dma_start(out=xnf[kc * 128:(kc + 1) * 128, tb * 512:(tb + 1) * 512],
                                        in_=(outw if weight is not None else outf)[:, kc * 512:(kc + 1) * 512]))
                if rms2_bf is not None:
                    st.dma(sy.dma_start(out=env.t[rms2_bf][kc * 128:(kc + 1) * 128,
                                                           tb * 512:(tb + 1) * 512],
                                        in_=h2sb[:, kc * 512:(kc + 1) * 512]))
            if tb == n_tb - 1:
                sy.wait_ge(st.h, st.n)
                if done_sem is not None:
                    done_sem.inc(sy.nop())
        st_after[0] = st.n
    return st



def emit_qkv_rope(env, block, pre, src, wq, wk, wv, q_out, k_out, v_out,
                  rope=True, wait_sems=(), done_sem=None, dout=1024):
    nc = env.nc
    x = env.t[src]
    qf, kf, vt = env.t[q_out], env.t[k_out], env.t[v_out]
    nmo = dout // 128
    ndvb = dout // 512
    wq_sb = env.sb(pre + "_wq", [128, NKC * dout], BF16)
    wk_sb = env.sb(pre + "_wk", [128, NKC * dout], BF16)
    wv_sb = env.sb(pre + "_wv", [128, NKC * dout], BF16)
    xsb = env.sb(pre + "_x", [128, 2 * NKC * 512], BF16)
    cos_sb = env.sb(pre + "_cos", [64, 2 * 512], F32) if rope else None
    sin_sb = env.sb(pre + "_sin", [64, 2 * 512], F32) if rope else None
    qsb = env.sb(pre + "_q", [128, nmo * 512], BF16)
    ksb = env.sb(pre + "_k", [128, nmo * 512], BF16)
    vsb = env.sb(pre + "_v", [128, 4 * ndvb * 512], BF16)
    rtmp = env.sb(pre + "_rt", [128, 2 * 512], F32)
    psq = [env.ps(pre + f"_psq{i}", [128, 512]) for i in range(4)]
    psv = [env.ps(pre + f"_psv{i}", [128, 512]) for i in range(2)]

    wl = env.sem(pre + "wl")
    ld = env.sem(pre + "ld")
    pe = env.sem(pre + "pe")
    ve = env.sem(pre + "ve")
    se = env.sem(pre + "se")
    st = env.sem(pre + "st")

    @block.gpsimd
    def _(g):
        for _ws in wait_sems:
            _ws.wait(g)
        for w_sb, wname in ((wq_sb, wq), (wk_sb, wk), (wv_sb, wv)):
            w = env.t[wname]
            for kc in range(NKC):
                wl.dma(g.dma_start(out=w_sb[:, kc * dout:(kc + 1) * dout],
                                   in_=w[kc * 128:(kc + 1) * 128, :]))

    nqk = 2 * nmo
    nv = 4 * ndvb
    first = [True]
    st_prev = [0]
    for tb in range(NTB):
        buf = tb % 2
        xoff = buf * NKC * 512
        ve_prev_tb = ve.n
        se_prev_tb = se.n
        st_prev_tb = st_prev[0]

        @block.sync
        def _(sy):
            if first[0]:
                for wsem in wait_sems:
                    wsem.wait(sy)
                first[0] = False
            if tb >= 2:
                sy.wait_ge(pe.h, pe.n - (nqk + nv))
            if tb >= 1:
                sy.wait_ge(ld.h, ld.n)
            for kc in range(NKC):
                ld.dma(sy.dma_start(out=xsb[:, xoff + kc * 512:xoff + (kc + 1) * 512],
                                    in_=x[kc * 128:(kc + 1) * 128, tb * 512:(tb + 1) * 512]))
            if rope:
                ld.dma(sy.dma_start(out=cos_sb[:, buf * 512:(buf + 1) * 512],
                                    in_=env.t["cosT"][:, tb * 512:(tb + 1) * 512]))
                ld.dma(sy.dma_start(out=sin_sb[:, buf * 512:(buf + 1) * 512],
                                    in_=env.t["sinT"][:, tb * 512:(tb + 1) * 512]))
        ld_tb = ld.n

        pe_base = pe.n

        @block.tensor
        def _(t):
            if tb == 0:
                for _ws in wait_sems:
                    _ws.wait(t)
            wl.wait(t)
            t.wait_ge(ld.h, ld_tb)
            g = 0
            for w_sb in (wq_sb, wk_sb):
                for mo in range(nmo):
                    bank = psq[g % 4]
                    if g >= 4:
                        t.wait_ge(ve.h, ve_prev_tb + 2 * (g - 4) + 2)
                    else:
                        t.wait_ge(ve.h, ve_prev_tb)
                    for kc in range(NKC):
                        ii = t.matmul(bank[:],
                                      w_sb[:, kc * dout + mo * 128:kc * dout + (mo + 1) * 128],
                                      xsb[:, xoff + kc * 512:xoff + (kc + 1) * 512],
                                      start=(kc == 0), stop=(kc == NKC - 1))
                    pe.inc(ii)
                    g += 1
            gv = 0
            for tt in range(4):
                for dvb in range(ndvb):
                    bank = psv[gv % 2]
                    if gv >= 2:
                        t.wait_ge(se.h, se_prev_tb + (gv - 2) + 1)
                    else:
                        t.wait_ge(se.h, se_prev_tb)
                    for kc in range(NKC):
                        ii = t.matmul(bank[:],
                                      xsb[:, xoff + kc * 512 + tt * 128:xoff + kc * 512 + (tt + 1) * 128],
                                      wv_sb[:, kc * dout + dvb * 512:kc * dout + (dvb + 1) * 512],
                                      start=(kc == 0), stop=(kc == NKC - 1))
                    pe.inc(ii)
                    gv += 1

        @block.vector
        def _(v):
            if tb == 0:
                for _ws in wait_sems:
                    _ws.wait(v)
            v.wait_ge(st.h, st_prev_tb)
            g = 0
            for sbuf in (qsb, ksb):
                for mo in range(nmo):
                    v.wait_ge(pe.h, pe_base + g + 1)
                    bank = psq[g % 4]
                    osl = slice(mo * 512, (mo + 1) * 512)
                    if rope:
                        c = cos_sb[:, buf * 512:(buf + 1) * 512]
                        s_ = sin_sb[:, buf * 512:(buf + 1) * 512]
                        t1 = rtmp[:, 0:512]
                        t2 = rtmp[:, 512:1024]
                        v.wait_ge(ve.h, ve.n)
                        v.tensor_tensor(out=t1[0:64, :], in0=bank[0:64, :], in1=c, op=ALU.mult)
                        v.tensor_tensor(out=t2[0:64, :], in0=bank[64:128, :], in1=s_, op=ALU.mult)
                        v.tensor_tensor(out=t1[64:128, :], in0=bank[64:128, :], in1=c, op=ALU.mult)
                        ve.inc(v.tensor_tensor(out=t2[64:128, :], in0=bank[0:64, :], in1=s_, op=ALU.mult))
                        v.wait_ge(ve.h, ve.n)
                        v.tensor_add(out=sbuf[0:64, osl], in0=t1[0:64, :], in1=t2[0:64, :])
                        ii = v.tensor_sub(out=sbuf[64:128, osl], in0=t1[64:128, :], in1=t2[64:128, :])
                    else:
                        ve.inc(v.tensor_copy(out=sbuf[:, osl], in_=bank[:]))
                        ii = v.engine_nop()
                    ve.inc(ii)
                    g += 1

        @block.scalar
        def _(s):
            if tb == 0:
                for _ws in wait_sems:
                    _ws.wait(s)
            s.wait_ge(st.h, st_prev_tb)
            gv = 0
            for tt in range(4):
                for dvb in range(ndvb):
                    s.wait_ge(pe.h, pe_base + nqk + gv + 1)
                    bank = psv[gv % 2]
                    ii = s.activation(out=vsb[:, (tt * ndvb + dvb) * 512:(tt * ndvb + dvb + 1) * 512],
                                      in_=bank[:], func=AF.Copy)
                    se.inc(ii)
                    gv += 1

        ve_tb = ve.n
        se_tb = se.n

        @block.sync
        def _(sy):
            sy.wait_ge(ve.h, ve_tb)
            for mo in range(nmo):
                st.dma(sy.dma_start(out=qf[mo * 128:(mo + 1) * 128, tb * 512:(tb + 1) * 512],
                                    in_=qsb[:, mo * 512:(mo + 1) * 512]))
                st.dma(sy.dma_start(out=kf[mo * 128:(mo + 1) * 128, tb * 512:(tb + 1) * 512],
                                    in_=ksb[:, mo * 512:(mo + 1) * 512]))
            sy.wait_ge(se.h, se_tb)
            for tt in range(4):
                for dvb in range(ndvb):
                    st.dma(sy.dma_start(
                        out=vt[tb * 512 + tt * 128:tb * 512 + (tt + 1) * 128,
                               dvb * 512:(dvb + 1) * 512],
                        in_=vsb[:, (tt * ndvb + dvb) * 512:(tt * ndvb + dvb + 1) * 512]))
            if tb == NTB - 1:
                sy.wait_ge(st.h, st.n)
                if done_sem is not None:
                    done_sem.inc(sy.nop())
        st_prev[0] = st.n
    return st



def emit_stab(env, block, pre, q_name, k_name, negb_name, scale, nheads=8,
              wait_sems=(), done_sem=None):
    nc = env.nc
    qf, kf = env.t[q_name], env.t[k_name]
    x_sb = env.sb(pre + "_x", [128, 4 * 512], BF16)
    sq = env.sb(pre + "_sq", [128, 2 * 512], BF16)
    onesb = env.sb(pre + "_ones", [128, 1], BF16)
    ones1 = env.sb(pre + "_ones1", [1, 128], F32)
    srow = env.sb(pre + "_srow", [1, 2 * nheads * NTB], F32)
    brow = env.sb(pre + "_brow", [1, 2 * nheads], F32)
    nrow = env.sb(pre + "_nrow", [1, nheads], F32)
    nb_sb = env.sb(pre + "_nb", [128, nheads], F32)
    ps_n = [env.ps(pre + f"_psn{i}", [1, 512]) for i in range(4)]
    ps_b = env.ps(pre + "_psb", [128, nheads])

    ld = env.sem(pre + "ld")
    pe = env.sem(pre + "pe")
    ve = env.sem(pre + "ve")
    se = env.sem(pre + "se")
    ge = env.sem(pre + "ge")
    st = env.sem(pre + "st")

    @block.gpsimd
    def _(g):
        for _ws in wait_sems:
            _ws.wait(g)
        g.memset(onesb[:], 1.0)
        ge.inc(g.memset(ones1[:], 1.0))
    ge_c = ge.n

    units = [(ti, hh, tb) for ti in range(2) for hh in range(nheads) for tb in range(NTB)]
    se_h = {}
    pe_h = {}
    ve_h = {}
    ld_h = {}
    for u, (ti, hh, tb) in enumerate(units):
        srcf = qf if ti == 0 else kf

        @block.sync
        def _(sy):
            if u == 0:
                for _ws in wait_sems:
                    _ws.wait(sy)
            if u >= 4:
                sy.wait_ge(se.h, se_h[u - 4])
            if u >= 1:
                sy.wait_ge(ld.h, ld.n)
            ld.dma(sy.dma_start(out=x_sb[:, (u % 4) * 512:(u % 4 + 1) * 512],
                                in_=srcf[hh * 128:(hh + 1) * 128, tb * 512:(tb + 1) * 512]))
        ld_h[u] = ld.n

        @block.scalar
        def _(s):
            s.wait_ge(ld.h, ld_h[u])
            if u >= 2:
                s.wait_ge(pe.h, pe_h[u - 2])
            se.inc(s.activation(out=sq[:, (u % 2) * 512:(u % 2 + 1) * 512],
                                in_=x_sb[:, (u % 4) * 512:(u % 4 + 1) * 512],
                                func=AF.Square))
        se_h[u] = se.n

        @block.tensor
        def _(t):
            if u == 0:
                t.wait_ge(ge.h, ge_c)
            t.wait_ge(se.h, se_h[u])
            if u >= 4:
                t.wait_ge(ve.h, ve_h[u - 4])
            pe.inc(t.matmul(ps_n[u % 4][:], onesb[:], sq[:, (u % 2) * 512:(u % 2 + 1) * 512],
                            start=True, stop=True))
        pe_h[u] = pe.n

        @block.vector
        def _(v):
            v.wait_ge(pe.h, pe_h[u])
            ve.inc(v.reduce_max(out=srow[:, u:u + 1], in_=ps_n[u % 4][:], axis=AX))
        ve_h[u] = ve.n

    @block.vector
    def _(v):
        def syn(ii):
            ve.inc(ii)
            v.wait_ge(ve.h, ve.n)
        v.wait_ge(ve.h, ve_h[len(units) - 1])
        for ti in range(2):
            for hh in range(nheads):
                base = (ti * nheads + hh) * NTB
                v.reduce_max(out=brow[:, ti * nheads + hh:ti * nheads + hh + 1],
                             in_=srow[:, base:base + NTB], axis=AX)
        syn(v.engine_nop())
        for hh in range(nheads):
            v.tensor_mul(out=nrow[:, hh:hh + 1], in0=brow[:, hh:hh + 1],
                         in1=brow[:, nheads + hh:nheads + hh + 1])
        syn(v.engine_nop())
    ve_prod = ve.n

    @block.scalar
    def _(s):
        s.wait_ge(ve.h, ve_prod)
        se.inc(s.activation(out=nrow[:], in_=nrow[:], func=AF.Ln))
    se_ln = se.n

    @block.vector
    def _(v):
        v.wait_ge(se.h, se_ln)
        ve.inc(v.tensor_scalar(out=nrow[:], in0=nrow[:], scalar1=-0.5, scalar2=None,
                               op0=ALU.mult))
    ve_hl = ve.n

    @block.scalar
    def _(s):
        s.wait_ge(ve.h, ve_hl)
        se.inc(s.activation(out=nrow[:], in_=nrow[:], func=AF.Exp))
    se_exp = se.n

    @block.vector
    def _(v):
        v.wait_ge(se.h, se_exp)
        ve.inc(v.tensor_scalar(out=nrow[:], in0=nrow[:], scalar1=-float(scale),
                               scalar2=None, op0=ALU.mult))
    ve_nb = ve.n

    @block.tensor
    def _(t):
        t.wait_ge(ve.h, ve_nb)
        pe.inc(t.matmul(ps_b[:], ones1[:], nrow[:], start=True, stop=True))
    pe_b = pe.n

    @block.vector
    def _(v):
        v.wait_ge(pe.h, pe_b)
        ve.inc(v.tensor_copy(out=nb_sb[:], in_=ps_b[:]))
    ve_f = ve.n

    @block.sync
    def _(sy):
        sy.wait_ge(ve.h, ve_f)
        st.dma(sy.dma_start(out=env.t[negb_name][:, :], in_=nb_sb[:]))
        sy.wait_ge(st.h, st.n)
        if done_sem is not None:
            done_sem.inc(sy.nop())
    return st


def emit_attn(env, block, pre, q_name, k_name, v_name, y_name, scale,
              negb_name, extra_recip_scale=1.0, nheads=8, wait_sems=(),
              done_sem=None):
    nc = env.nc
    qf, kf, vt, yf = env.t[q_name], env.t[k_name], env.t[v_name], env.t[y_name]
    NKT = T // 128
    q_sb = env.sb(pre + "_q", [128, T], BF16)
    k_sb = env.sb(pre + "_k", [128, T], BF16)
    v_sb = env.sb(pre + "_v", [128, T], BF16)
    a_sb = env.sb(pre + "_a", [128, NKT * 512], BF16)
    ou_sb = env.sb(pre + "_ou", [128, NTB * 512], BF16)
    y_sb = env.sb(pre + "_y", [128, 2 * 512], BF16)
    mask_sb = env.sb(pre + "_mask", [128, 2048], BF16)
    onesb = env.sb(pre + "_ones", [128, 1], BF16)
    ones1 = env.sb(pre + "_ones1", [1, 128], F32)
    nb_sb = env.sb(pre + "_nb", [128, nheads], F32)
    scv = env.sb(pre + "_scv", [128, 1], F32)
    sums = env.sb(pre + "_sums", [1, NTB * 512], F32)
    rsb = env.sb(pre + "_rsb", [128, 2 * 512], F32)
    ps_s = env.ps(pre + "_ps_s", [128, 2048])
    ps_o = env.ps(pre + "_ps_o", [128, 512])
    ps_n = env.ps(pre + "_ps_n", [1, 512])
    ps_r = env.ps(pre + "_ps_r", [128, 1024])

    ld = env.sem(pre + "ld")
    pe = env.sem(pre + "pe")
    ve = env.sem(pre + "ve")
    se = env.sem(pre + "se")
    ge = env.sem(pre + "ge")
    st = env.sem(pre + "st")

    @block.gpsimd
    def _(g):
        for _ws in wait_sems:
            _ws.wait(g)
        g.memset(onesb[:], 1.0)
        g.memset(ones1[:], 1.0)
        ge.inc(g.memset(scv[:], float(scale)))

    @block.sync
    def _(sy):
        for _ws in wait_sems:
            _ws.wait(sy)
        ld.dma(sy.dma_start(out=mask_sb[:], in_=env.t["mask4"][:, :]))
        ld.dma(sy.dma_start(out=nb_sb[:], in_=env.t[negb_name][:, :]))
    ge_const = ge.n

    for h in range(nheads):
        st_prev_head = st.n
        ve_prev_head = ve.n
        se_prev_head = se.n
        pe_prev_head = pe.n

        @block.sync
        def _(sy):
            if h > 0:
                sy.wait_ge(pe.h, pe_prev_head)
                sy.wait_ge(ve.h, ve_prev_head)
                sy.wait_ge(ld.h, ld.n)
            for tb in range(NTB):
                ld.dma(sy.dma_start(out=q_sb[:, tb * 512:(tb + 1) * 512],
                                    in_=qf[h * 128:(h + 1) * 128, tb * 512:(tb + 1) * 512]))
                ld.dma(sy.dma_start(out=k_sb[:, tb * 512:(tb + 1) * 512],
                                    in_=kf[h * 128:(h + 1) * 128, tb * 512:(tb + 1) * 512]))
            for kt in range(NKT):
                ld.dma(sy.dma_start(out=v_sb[:, kt * 128:(kt + 1) * 128],
                                    in_=vt[kt * 128:(kt + 1) * 128, h * 128:(h + 1) * 128]))
        ld_head = ld.n
        negB = nb_sb[:, h:h + 1]

        se_exp_h = {}
        ve_mask_h = {}
        pe_s_h = {}
        se_ou_h = {}
        ve_sum_h = {}
        for qb in range(NTB):
            nkt = 4 * (qb + 1)
            nbat = nkt // 2
            for bat in range(nbat):
                se_exp_h[(qb, bat)] = se.n + bat + 1
            se_ou_h[qb] = se.n + nbat + 1
            ve_mask_h[(qb, nbat - 2)] = ve.n + 1
            ve_mask_h[(qb, nbat - 1)] = ve.n + 2
            ve_sum_h[qb] = ve.n + 3

            @block.tensor
            def _(t):
                if h == 0 and qb == 0:
                    t.wait_ge(ge.h, ge_const)
                t.wait_ge(ld.h, ld_head)
                if qb > 0:
                    t.wait_ge(se.h, se_ou_h[qb - 1])
                    t.wait_ge(ve.h, ve_sum_h[qb - 1])
                for bat in range(nbat):
                    reg = bat % 2
                    if bat >= 2:
                        t.wait_ge(se.h, se_exp_h[(qb, bat - 2)])
                    elif qb > 0:
                        t.wait_ge(se.h, se_exp_h[(qb - 1, (4 * qb) // 2 - 1)])
                    for i in range(2):
                        kt = bat * 2 + i
                        ii = t.matmul(ps_s[:, reg * 1024 + i * 512: reg * 1024 + (i + 1) * 512],
                                      k_sb[:, kt * 128:(kt + 1) * 128],
                                      q_sb[:, qb * 512:(qb + 1) * 512],
                                      start=True, stop=True)
                    pe.inc(ii)
                    pe_s_h[(qb, bat)] = pe.n
                    if bat >= 1:
                        bb = bat - 1
                        t.wait_ge(se.h, se_exp_h[(qb, bb)])
                        if bb >= nbat - 2:
                            t.wait_ge(ve.h, ve_mask_h[(qb, bb)])
                        for i in range(2):
                            kt = bb * 2 + i
                            t.matmul(ps_n[:], onesb[:], a_sb[:, kt * 512:(kt + 1) * 512],
                                     start=(kt == 0), stop=False, skip_group_check=True)
                            ii = t.matmul(ps_o[:], v_sb[:, kt * 128:(kt + 1) * 128],
                                          a_sb[:, kt * 512:(kt + 1) * 512],
                                          start=(kt == 0), stop=False, skip_group_check=True)
                        pe.inc(ii)
                bb = nbat - 1
                t.wait_ge(se.h, se_exp_h[(qb, bb)])
                t.wait_ge(ve.h, ve_mask_h[(qb, bb)])
                for i in range(2):
                    kt = bb * 2 + i
                    t.matmul(ps_n[:], onesb[:], a_sb[:, kt * 512:(kt + 1) * 512],
                             start=False, stop=(i == 1), skip_group_check=True)
                    ii = t.matmul(ps_o[:], v_sb[:, kt * 128:(kt + 1) * 128],
                                  a_sb[:, kt * 512:(kt + 1) * 512],
                                  start=False, stop=(i == 1), skip_group_check=True)
                pe.inc(ii)
            pe_oav = pe.n

            @block.scalar
            def _(s):
                if h == 0 and qb == 0:
                    s.wait_ge(ge.h, ge_const)
                    s.wait_ge(ld.h, ld_head)
                for bat in range(nbat):
                    reg = bat % 2
                    s.wait_ge(pe.h, pe_s_h[(qb, bat)])
                    assert se.n + 1 == se_exp_h[(qb, bat)]
                    se.inc(s.activation(out=a_sb[:, bat * 1024:(bat + 1) * 1024],
                                        in_=ps_s[:, reg * 1024:(reg + 1) * 1024],
                                        func=AF.Exp, bias=negB, scale=scv[:, 0:1]))
                s.wait_ge(pe.h, pe_oav)
                se.inc(s.activation(out=ou_sb[:, qb * 512:(qb + 1) * 512],
                                    in_=ps_o[:], func=AF.Copy))
            assert se.n == se_ou_h[qb]

            @block.vector
            def _(v):
                for bb in (nbat - 2, nbat - 1):
                    v.wait_ge(se.h, se_exp_h[(qb, bb)])
                    moff = 0 if bb == nbat - 2 else 1024
                    ve.inc(v.tensor_tensor(out=a_sb[:, bb * 1024:(bb + 1) * 1024],
                                           in0=a_sb[:, bb * 1024:(bb + 1) * 1024],
                                           in1=mask_sb[:, moff:moff + 1024], op=ALU.mult))
                v.wait_ge(pe.h, pe_oav)
                ve.inc(v.tensor_copy(out=sums[:, qb * 512:(qb + 1) * 512], in_=ps_n[:]))
            assert ve.n == ve_sum_h[qb]

        @block.vector
        def _(v):
            v.wait_ge(ve.h, ve_sum_h[NTB - 1])
            if extra_recip_scale != 1.0:
                ve.inc(v.tensor_scalar(out=sums[:], in0=sums[:],
                                       scalar1=1.0 / extra_recip_scale, scalar2=None,
                                       op0=ALU.mult))
                v.wait_ge(ve.h, ve.n)
            ve.inc(v.reciprocal(out=sums[:], in_=sums[:]))
        ve_rec = ve.n

        pe_bc_h = {}
        ve_y_h = {}
        for qb in range(NTB):
            @block.tensor
            def _(t):
                if qb == 0:
                    t.wait_ge(ve.h, ve_rec)
                if qb >= 2:
                    t.wait_ge(ve.h, ve_y_h[qb - 2])
                pe.inc(t.matmul(ps_r[:, (qb % 2) * 512:(qb % 2 + 1) * 512], ones1[:],
                                sums[:, qb * 512:(qb + 1) * 512], start=True, stop=True))
            pe_bc_h[qb] = pe.n

            @block.vector
            def _(v):
                v.wait_ge(pe.h, pe_bc_h[qb])
                v.wait_ge(se.h, se_ou_h[NTB - 1])
                if qb >= 2:
                    v.wait_ge(st.h, st.n)
                ve.inc(v.tensor_copy(out=rsb[:, (qb % 2) * 512:(qb % 2 + 1) * 512],
                                     in_=ps_r[:, (qb % 2) * 512:(qb % 2 + 1) * 512]))
                v.wait_ge(ve.h, ve.n)
                ve.inc(v.tensor_mul(out=y_sb[:, (qb % 2) * 512:(qb % 2 + 1) * 512],
                                    in0=ou_sb[:, qb * 512:(qb + 1) * 512],
                                    in1=rsb[:, (qb % 2) * 512:(qb % 2 + 1) * 512]))
            ve_y_h[qb] = ve.n

            @block.sync
            def _(sy):
                sy.wait_ge(ve.h, ve_y_h[qb])
                st.dma(sy.dma_start(out=yf[h * 128:(h + 1) * 128, qb * 512:(qb + 1) * 512],
                                    in_=y_sb[:, (qb % 2) * 512:(qb % 2 + 1) * 512]))
                if h == nheads - 1 and qb == NTB - 1:
                    sy.wait_ge(st.h, st.n)
                    if done_sem is not None:
                        done_sem.inc(sy.nop())
    return st



def emit_mm_fm(env, block, pre, x_name, w_name, n, out_bf=None, out_f32=None,
               residual=None, din=D, dout=D, wait_sems=(), done_sem=None,
               res_out_bf=None, evac_scale=None):
    nc = env.nc
    x = env.t[x_name]
    w = env.t[w_name]
    nkc = din // 128
    nmo = dout // 128
    ntb = n // 512
    assert ntb % 2 == 0
    w_sb = env.sb(pre + "_w", [128, nkc * dout], BF16)
    xsb = env.sb(pre + "_x", [128, 2 * nkc * 512], BF16)
    rsb = env.sb(pre + "_res", [128, 4 * 512], F32) if residual is not None else None
    want_bf = out_bf is not None or res_out_bf is not None
    osb_f = env.sb(pre + "_of", [128, 2 * nmo * 512], F32) \
        if (out_f32 is not None or residual is not None) else None
    osb_b = env.sb(pre + "_ob", [128, 2 * nmo * 512], BF16) if want_bf else None
    scv = env.sb(pre + "_scv", [128, 1], F32) if evac_scale is not None else None
    ps = [env.ps(pre + f"_ps{i}", [128, 512]) for i in range(4)]

    wl = env.sem(pre + "wl")
    ld = env.sem(pre + "ld")
    rl = env.sem(pre + "rl") if residual is not None else None
    pe = env.sem(pre + "pe")
    ve = env.sem(pre + "ve")
    se = env.sem(pre + "se")
    st = env.sem(pre + "st")

    @block.gpsimd
    def _(g):
        for _ws in wait_sems:
            _ws.wait(g)
        if scv is not None:
            env.sem(pre + "ms").inc(g.memset(scv[:], float(evac_scale)))
        for kc in range(nkc):
            wl.dma(g.dma_start(out=w_sb[:, kc * dout:(kc + 1) * dout],
                               in_=w[kc * 128:(kc + 1) * 128, :]))

    first = [True]
    for tp_ in range(ntb // 2):
        tb0 = 2 * tp_
        evq_prev_v = ve.n
        evq_prev_s = se.n
        st_prev_end = st.n
        pe_now = pe.n

        @block.sync
        def _(sy):
            if first[0]:
                for _ws in wait_sems:
                    _ws.wait(sy)
                first[0] = False
            else:
                sy.wait_ge(pe.h, pe_now)
                sy.wait_ge(ld.h, ld.n)
            for half in range(2):
                for kc in range(nkc):
                    ld.dma(sy.dma_start(
                        out=xsb[:, half * nkc * 512 + kc * 512:half * nkc * 512 + (kc + 1) * 512],
                        in_=x[kc * 128:(kc + 1) * 128,
                              (tb0 + half) * 512:(tb0 + half + 1) * 512]))
        ld_tb = ld.n
        pe_base = pe.n

        @block.tensor
        def _(t):
            if tp_ == 0:
                for _ws in wait_sems:
                    _ws.wait(t)
                wl.wait(t)
                if scv is not None:
                    env.sem(pre + "ms").wait(t)
            t.wait_ge(ld.h, ld_tb)
            for mo in range(nmo):
                b0 = ps[2 * (mo % 2)]
                b1 = ps[2 * (mo % 2) + 1]
                ev = ve if residual is not None else se
                evprev = evq_prev_v if residual is not None else evq_prev_s
                per_mo = 2 if residual is not None else 1
                if mo >= 2:
                    t.wait_ge(ev.h, evprev + per_mo * (mo - 1))
                else:
                    t.wait_ge(ev.h, evprev)
                for kc in range(nkc):
                    wsl = w_sb[:, kc * dout + mo * 128:kc * dout + (mo + 1) * 128]
                    t.matmul(b0[:], wsl, xsb[:, kc * 512:(kc + 1) * 512],
                             start=(kc == 0), stop=(kc == nkc - 1))
                    ii = t.matmul(b1[:], wsl,
                                  xsb[:, nkc * 512 + kc * 512:nkc * 512 + (kc + 1) * 512],
                                  start=(kc == 0), stop=(kc == nkc - 1))
                pe.inc(ii)

        if residual is not None:
            res = env.t[residual]
            rl_base = rl.n

            @block.gpsimd
            def _(g):
                for mo in range(nmo):
                    if mo >= 2:
                        g.wait_ge(ve.h, evq_prev_v + 2 * (mo - 1))
                    elif tp_ > 0:
                        g.wait_ge(ve.h, evq_prev_v)
                    g.wait_ge(rl.h, rl.n)
                    for half in range(2):
                        rl.dma(g.dma_start(
                            out=rsb[:, ((2 * mo + half) % 4) * 512:((2 * mo + half) % 4 + 1) * 512],
                            in_=res[mo * 128:(mo + 1) * 128,
                                    (tb0 + half) * 512:(tb0 + half + 1) * 512]))

            @block.vector
            def _(v):
                v.wait_ge(st.h, st_prev_end)
                for mo in range(nmo):
                    v.wait_ge(pe.h, pe_base + mo + 1)
                    v.wait_ge(rl.h, rl_base + 32 * (mo + 1))
                    for half in range(2):
                        bank = ps[2 * (mo % 2) + half]
                        ii = v.tensor_add(
                            out=osb_f[:, (2 * mo + half) * 512:(2 * mo + half + 1) * 512],
                            in0=bank[:],
                            in1=rsb[:, ((2 * mo + half) % 4) * 512:((2 * mo + half) % 4 + 1) * 512])
                        ve.inc(ii)

            if want_bf:
                @block.scalar
                def _(s):
                    s.wait_ge(st.h, st_prev_end)
                    for mo in range(nmo):
                        s.wait_ge(ve.h, evq_prev_v + 2 * mo + 2)
                        for half in range(2):
                            ii = s.activation(
                                out=osb_b[:, (2 * mo + half) * 512:(2 * mo + half + 1) * 512],
                                in_=osb_f[:, (2 * mo + half) * 512:(2 * mo + half + 1) * 512],
                                func=AF.Copy)
                        se.inc(ii)
        else:
            @block.scalar
            def _(s):
                s.wait_ge(st.h, st_prev_end)
                for mo in range(nmo):
                    s.wait_ge(pe.h, pe_base + mo + 1)
                    _sc = scv[:, 0:1] if scv is not None else 1.0
                    ii = None
                    for half in range(2):
                        bank = ps[2 * (mo % 2) + half]
                        sl = slice((2 * mo + half) * 512, (2 * mo + half + 1) * 512)
                        if want_bf:
                            ii = s.activation(out=osb_b[:, sl], in_=bank[:],
                                              func=AF.Copy, scale=_sc)
                        if out_f32 is not None:
                            ii = s.activation(out=osb_f[:, sl], in_=bank[:],
                                              func=AF.Copy, scale=_sc)
                    se.inc(ii)

        @block.sync
        def _(sy):
            for mo in range(nmo):
                if residual is not None:
                    sy.wait_ge(ve.h, evq_prev_v + 2 * mo + 2)
                    if want_bf:
                        sy.wait_ge(se.h, evq_prev_s + mo + 1)
                else:
                    sy.wait_ge(se.h, evq_prev_s + mo + 1)
                for half in range(2):
                    sl = slice((2 * mo + half) * 512, (2 * mo + half + 1) * 512)
                    cols = slice((tb0 + half) * 512, (tb0 + half + 1) * 512)
                    if out_f32 is not None:
                        st.dma(sy.dma_start(out=env.t[out_f32][mo * 128:(mo + 1) * 128, cols],
                                            in_=osb_f[:, sl]))
                    if want_bf:
                        ob = out_bf or res_out_bf
                        st.dma(sy.dma_start(out=env.t[ob][mo * 128:(mo + 1) * 128, cols],
                                            in_=osb_b[:, sl]))
            if tp_ == ntb // 2 - 1:
                sy.wait_ge(st.h, st.n)
                if done_sem is not None:
                    done_sem.inc(sy.nop())
    return st



def emit_glu(env, block, pre, x_name, w1_name, w2_name, out_name, n,
             din=D, dout=2816, wait_sems=(), done_sem=None):
    nc = env.nc
    x = env.t[x_name]
    w1, w2 = env.t[w1_name], env.t[w2_name]
    nkc = din // 128
    nmo = dout // 128
    ntb = n // 512
    x_sb = env.sb(pre + "_x", [128, nkc * n], BF16)
    w1_sb = env.sb(pre + "_w1", [128, 2 * din], BF16)
    w2_sb = env.sb(pre + "_w2", [128, 2 * din], BF16)
    gsb = env.sb(pre + "_g", [128, 2 * 512], BF16)
    osb = env.sb(pre + "_o", [128, 2 * 512], BF16)
    ps_g = [env.ps(pre + f"_pg{i}", [128, 512]) for i in range(2)]
    ps_u = [env.ps(pre + f"_pu{i}", [128, 512]) for i in range(2)]

    wl = env.sem(pre + "wl")
    ld = env.sem(pre + "ld")
    pe = env.sem(pre + "pe")
    ve = env.sem(pre + "ve")
    se = env.sem(pre + "se")
    st = env.sem(pre + "st")

    @block.sync
    def _(sy):
        for _ws in wait_sems:
            _ws.wait(sy)
        for kc in range(nkc):
            ld.dma(sy.dma_start(out=x_sb[:, kc * n:(kc + 1) * n],
                                in_=x[kc * 128:(kc + 1) * 128, :]))
    ld_x = ld.n

    pe_h = {}
    se_h = {}
    ve_h = {}
    st_h = {}
    for mo in range(nmo):
        slot = mo % 2

        @block.gpsimd
        def _(g):
            if mo == 0:
                for _ws in wait_sems:
                    _ws.wait(g)
            if mo >= 2:
                g.wait_ge(pe.h, pe_h[mo - 2])
            g.wait_ge(wl.h, wl.n)
            wl.dma(g.dma_start(out=w1_sb[:, slot * din:(slot + 1) * din], in_=w1[mo]))
            wl.dma(g.dma_start(out=w2_sb[:, slot * din:(slot + 1) * din], in_=w2[mo]))
        wl_mo = wl.n

        for tb in range(ntb):
            gi = mo * ntb + tb
            pslot = gi % 2

            @block.tensor
            def _(t):
                if gi == 0:
                    t.wait_ge(ld.h, ld_x)
                t.wait_ge(wl.h, wl_mo)
                if gi >= 2:
                    t.wait_ge(se.h, se_h[gi - 2])
                    t.wait_ge(ve.h, ve_h[gi - 2])
                for kc in range(nkc):
                    ii = t.matmul(ps_g[pslot][:], w1_sb[:, slot * din + kc * 128:slot * din + (kc + 1) * 128],
                                  x_sb[:, kc * n + tb * 512:kc * n + (tb + 1) * 512],
                                  start=(kc == 0), stop=(kc == nkc - 1))
                pe.inc(ii)
                for kc in range(nkc):
                    ii = t.matmul(ps_u[pslot][:], w2_sb[:, slot * din + kc * 128:slot * din + (kc + 1) * 128],
                                  x_sb[:, kc * n + tb * 512:kc * n + (tb + 1) * 512],
                                  start=(kc == 0), stop=(kc == nkc - 1))
                pe.inc(ii)
            pe_h[gi] = pe.n
            if tb == ntb - 1:
                pe_h[mo] = pe.n

            @block.scalar
            def _(s):
                s.wait_ge(pe.h, pe_h[gi] - 1)
                if gi >= 2:
                    s.wait_ge(ve.h, ve_h[gi - 2])
                se.inc(s.activation(out=gsb[:, pslot * 512:(pslot + 1) * 512],
                                    in_=ps_g[pslot][:], func=AF.Silu))
            se_h[gi] = se.n

            @block.vector
            def _(v):
                v.wait_ge(pe.h, pe_h[gi])
                v.wait_ge(se.h, se_h[gi])
                if gi >= 2:
                    v.wait_ge(st.h, st_h[gi - 2])
                ve.inc(v.tensor_mul(out=osb[:, pslot * 512:(pslot + 1) * 512],
                                    in0=gsb[:, pslot * 512:(pslot + 1) * 512],
                                    in1=ps_u[pslot][:]))
            ve_h[gi] = ve.n

            @block.sync
            def _(sy):
                sy.wait_ge(ve.h, ve_h[gi])
                sy.wait_ge(st.h, st.n)
                st.dma(sy.dma_start(out=env.t[out_name][mo * 128:(mo + 1) * 128,
                                                        tb * 512:(tb + 1) * 512],
                                    in_=osb[:, pslot * 512:(pslot + 1) * 512]))
            st_h[gi] = st.n
    @block.sync
    def _(sy):
        sy.wait_ge(st.h, st.n)
        if done_sem is not None:
            done_sem.inc(sy.nop())
    return st



def emit_gate(env, block, pre, x_name, gate_name, cw_name, n,
              wait_sems=(), done_sem=None):
    nc = env.nc
    x = env.t[x_name]
    gw = env.t[gate_name]
    cwt = env.t[cw_name]
    ntt = n // 128
    gw_sb = env.sb(pre + "_gw", [128, NKC * 16], F32)
    xsb = env.sb(pre + "_x", [128, NKC * 512], F32)
    ident = env.sb(pre + "_id", [128, 128], F32)
    l_sb = env.sb(pre + "_l", [16, 512], F32)
    lt = env.sb(pre + "_lt", [128, 16], F32)
    e_sb = env.sb(pre + "_e", [128, 16], F32)
    work = env.sb(pre + "_wk", [128, 16], F32)
    eqm = env.sb(pre + "_eq", [128, 16], F32)
    mstat = env.sb(pre + "_ms", [128, 4], F32)
    cw = env.sb(pre + "_cw", [128, 16], F32)
    cwT_sb = env.sb(pre + "_cwt", [16, 128], F32)
    ps_l = env.ps(pre + "_psl", [16, 512])
    ps_t = env.ps(pre + "_pst", [128, 16])
    ps_c = env.ps(pre + "_psc", [16, 128])

    wl = env.sem(pre + "wl")
    ld = env.sem(pre + "ld")
    pe = env.sem(pre + "pe")
    ve = env.sem(pre + "ve")
    se = env.sem(pre + "se")
    st = env.sem(pre + "st")

    @block.gpsimd
    def _(g):
        for _ws in wait_sems:
            _ws.wait(g)
        for kc in range(NKC):
            wl.dma(g.dma_start(out=gw_sb[:, kc * 16:(kc + 1) * 16],
                               in_=gw[kc * 128:(kc + 1) * 128, :]))

    @block.sync
    def _(sy):
        for _ws in wait_sems:
            _ws.wait(sy)
        ld.dma(sy.dma_start(out=ident[:], in_=env.t["ident32"][:, :]))

    first = [True]
    for tb in range(n // 512):
        ve_prev = ve.n
        st_prev = st.n

        @block.sync
        def _(sy):
            if tb > 0:
                sy.wait_ge(pe.h, pe.n)
            for kc in range(NKC):
                ld.dma(sy.dma_start(out=xsb[:, kc * 512:(kc + 1) * 512],
                                    in_=x[kc * 128:(kc + 1) * 128, tb * 512:(tb + 1) * 512]))
        ld_tb = ld.n

        @block.tensor
        def _(t):
            if first[0]:
                wl.wait(t)
                first[0] = False
            t.wait_ge(ld.h, ld_tb)
            if tb > 0:
                t.wait_ge(ve.h, ve_prev)
            else:
                pass
            for kc in range(NKC):
                ii = t.matmul(ps_l[:], gw_sb[:, kc * 16:(kc + 1) * 16],
                              xsb[:, kc * 512:(kc + 1) * 512],
                              start=(kc == 0), stop=(kc == NKC - 1))
            pe.inc(ii)
        pe_l = pe.n

        @block.vector
        def _(v):
            v.wait_ge(pe.h, pe_l)
            ve.inc(v.tensor_copy(out=l_sb[:], in_=ps_l[:]))
        ve_l = ve.n

        for tt in range(4):
            pe_t_prev = pe.n

            @block.tensor
            def _(t):
                t.wait_ge(ve.h, ve_l if tt == 0 else ve.n)
                pe.inc(t.transpose(ps_t[:, :], l_sb[:, tt * 128:(tt + 1) * 128],
                                   ident[0:16, 0:16]))
            pe_tt = pe.n

            @block.vector
            def _(v):
                n0 = [0]

                def sync(ii):
                    ve.inc(ii)
                    v.wait_ge(ve.h, ve.n)
                v.wait_ge(pe.h, pe_tt)
                sync(v.tensor_copy(out=lt[:], in_=ps_t[:]))
                sync(v.reduce_max(out=mstat[:, 0:1], in_=lt[:], axis=AX, negate=True))
            ve_ltmax = ve.n

            @block.scalar
            def _(s):
                s.wait_ge(ve.h, ve_ltmax)
                se.inc(s.activation(out=e_sb[:], in_=lt[:], func=AF.Exp,
                                    bias=mstat[:, 0:1]))
            se_e = se.n

            @block.vector
            def _(v):
                n0 = [0]

                def sync(ii):
                    ve.inc(ii)
                    v.wait_ge(ve.h, ve.n)
                v.wait_ge(se.h, se_e)
                sync(v.tensor_copy(out=work[:], in_=e_sb[:]))
                for i in range(6):
                    sync(v.reduce_max(out=mstat[:, 1:2], in_=work[:], axis=AX))
                    sync(v.tensor_scalar(out=eqm[:], in0=work[:], scalar1=mstat[:, 1:2],
                                         scalar2=-1e30, op0=ALU.is_equal, op1=ALU.mult))
                    sync(v.tensor_add(out=work[:], in0=work[:], in1=eqm[:]))
                sync(v.tensor_scalar(out=cw[:], in0=work[:], scalar1=-1e29, scalar2=None,
                                     op0=ALU.is_lt))
                sync(v.tensor_mul(out=cw[:], in0=cw[:], in1=e_sb[:]))
                sync(v.reduce_sum(out=mstat[:, 2:3], in_=cw[:], axis=AX))
                sync(v.reciprocal(out=mstat[:, 3:4], in_=mstat[:, 2:3]))
                sync(v.tensor_scalar(out=cw[:], in0=cw[:], scalar1=mstat[:, 3:4],
                                     scalar2=None, op0=ALU.mult))
            ve_cw = ve.n

            @block.tensor
            def _(t):
                t.wait_ge(ve.h, ve_cw)
                if tt > 0 or tb > 0:
                    t.wait_ge(st.h, st.n)
                pe.inc(t.transpose(ps_c[:, :], cw[:], ident[:]))
            pe_cwt = pe.n

            @block.vector
            def _(v):
                v.wait_ge(pe.h, pe_cwt)
                ve.inc(v.tensor_copy(out=cwT_sb[:], in_=ps_c[:]))
            ve_cwt = ve.n

            @block.sync
            def _(sy):
                sy.wait_ge(ve.h, ve_cwt)
                st.dma(sy.dma_start(out=cwt[:, tb * 512 + tt * 128:tb * 512 + (tt + 1) * 128],
                                    in_=cwT_sb[:]))
                if tb == n // 512 - 1 and tt == 3:
                    sy.wait_ge(st.h, st.n)
                    if done_sem is not None:
                        done_sem.inc(sy.nop())
    return st



def emit_moe(env, block, pre, t_name, cw_name, x2s_name, s_name, xown_name,
             x1_name, n=TH, wait_sems=(), done_sem=None):
    nc = env.nc
    E, NIC, NDC = 16, 11, 16
    t_dram = env.t[t_name]
    cw_dram = env.t[cw_name]
    egT, euT, edT = env.t["l0f_egT"], env.t["l0f_euT"], env.t["l0f_edT"]
    t_sb = env.sb(pre + "_t", [128, NKC * 1024], BF16)
    racc = env.sb(pre + "_racc", [128, NDC * 1024], F32)
    hh = env.sb(pre + "_hh", [128, 2 * NIC * 1024], BF16)
    egw = env.sb(pre + "_egw", [128, 2 * 2048], BF16)
    euw = env.sb(pre + "_euw", [128, 2 * 2048], BF16)
    edw = env.sb(pre + "_edw", [128, 2 * 2 * 1408], BF16)
    cwr = env.sb(pre + "_cwr", [1, 2 * 1024], F32)
    cwb = env.sb(pre + "_cwb", [128, 2 * 1024], F32)
    silu_sb = env.sb(pre + "_sil", [128, 1024], BF16)
    tmp_sb = env.sb(pre + "_tmp", [128, 1024], BF16)
    ones1 = env.sb(pre + "_ones1", [1, 128], F32)
    ep_x2s = env.sb(pre + "_ex2s", [128, 512], F32)
    ep_s = env.sb(pre + "_es", [128, 512], F32)
    ep_x = env.sb(pre + "_ex", [128, 512], F32)
    ep_f = env.sb(pre + "_ef", [128, 512], F32)
    ep_f2 = env.sb(pre + "_ef2", [128, 512], F32)
    ep_o = env.sb(pre + "_eo", [128, 512], F32)
    ps_g = env.ps(pre + "_psg", [128, 1024])
    ps_u = env.ps(pre + "_psu", [128, 1024])
    ps_d = [env.ps(pre + f"_psd{i}", [128, 1024]) for i in range(2)]
    ps_c = ps_d[0]

    ul = env.sem(pre + "ul")
    el = env.sem(pre + "el")
    tl = env.sem(pre + "tl")
    pe = env.sem(pre + "pe")
    ve = env.sem(pre + "ve")
    se = env.sem(pre + "se")
    ge = env.sem(pre + "ge")
    st = env.sem(pre + "st")

    @block.gpsimd
    def _(g):
        for _ws in wait_sems:
            _ws.wait(g)
        ge.inc(g.memset(ones1[:], 1.0))
    ge_ones = ge.n

    first = [True]
    last_racc_ve = [0]
    for th in range(2):
        tcol = slice(th * 1024, (th + 1) * 1024)
        pe_all_prev = pe.n
        ve_all_prev = ve.n

        @block.sync
        def _(sy):
            if first[0]:
                for _ws in wait_sems:
                    _ws.wait(sy)
                first[0] = False
            else:
                sy.wait_ge(pe.h, pe_all_prev)
                sy.wait_ge(ve.h, ve_all_prev)
            sy.wait_ge(tl.h, tl.n)
            for kc in range(NKC):
                tl.dma(sy.dma_start(out=t_sb[:, kc * 1024:(kc + 1) * 1024],
                                    in_=t_dram[kc * 128:(kc + 1) * 128, tcol]))
        tl_t = tl.n

        pe_up_h = {}
        pe_up_h2 = {}
        se_sil_h = {}
        ve_t1_h = {}
        ve_hh_h = {}
        ve_cwb_h = {}
        pe_cw_h = {}
        pe_dn_h = {}
        ve_racc_h = {}

        for grp in range(E // 2):
            es = [2 * grp, 2 * grp + 1]
            for ei, e in enumerate(es):
                gidx = grp * 2 + ei
                slot = gidx % 2

                @block.sync
                def _(sy):
                    if gidx >= 2:
                        sy.wait_ge(pe.h, pe_cw_h[gidx - 2])
                    sy.wait_ge(tl.h, tl.n)
                    tl.dma(sy.dma_start(out=cwr[:, slot * 1024:(slot + 1) * 1024],
                                        in_=cw_dram[e:e + 1, tcol]))
                tl_cw = tl.n

                @block.tensor
                def _(t):
                    if gidx == 0 and th == 0:
                        t.wait_ge(ge.h, ge_ones)
                    t.wait_ge(tl.h, tl_cw)
                    if gidx >= 1:
                        t.wait_ge(ve.h, ve_cwb_h[gidx - 1])
                    t.wait_ge(ve.h, last_racc_ve[0])
                    for half in range(2):
                        ii = t.matmul(ps_c[:, half * 512:(half + 1) * 512], ones1[:],
                                      cwr[:, slot * 1024 + half * 512:slot * 1024 + (half + 1) * 512],
                                      start=True, stop=True)
                    pe.inc(ii)
                pe_cw_h[gidx] = pe.n

                @block.vector
                def _(v):
                    v.wait_ge(pe.h, pe_cw_h[gidx])
                    if gidx >= 2:
                        v.wait_ge(ve.h, ve_t1_h[(gidx - 2) * NIC + NIC - 1])
                    ve.inc(v.tensor_copy(out=cwb[:, slot * 1024:(slot + 1) * 1024],
                                         in_=ps_c[:]))
                ve_cwb_h[gidx] = ve.n

                for ic in range(NIC):
                    idx = gidx * NIC + ic
                    pslot = idx % 2

                    @block.gpsimd
                    def _(g):
                        if idx >= 2:
                            g.wait_ge(pe.h, pe_up_h[idx - 2])
                        g.wait_ge(ul.h, ul.n)
                        g.dma_start(out=egw[:, pslot * 2048:(pslot + 1) * 2048],
                                    in_=egT[e, ic]).then_inc(ul.h, 16)
                        g.dma_start(out=euw[:, pslot * 2048:(pslot + 1) * 2048],
                                    in_=euT[e, ic]).then_inc(ul.h, 16)
                    ul.n += 32
                    ul_ic = ul.n

                    @block.tensor
                    def _(t):
                        t.wait_ge(tl.h, tl_t)
                        t.wait_ge(ul.h, ul_ic)
                        if idx >= 1:
                            t.wait_ge(se.h, se_sil_h[idx - 1])
                        for kc in range(NKC):
                            for half in range(2):
                                ii = t.matmul(ps_g[:, half * 512:(half + 1) * 512],
                                              egw[:, pslot * 2048 + kc * 128:pslot * 2048 + (kc + 1) * 128],
                                              t_sb[:, kc * 1024 + half * 512:kc * 1024 + (half + 1) * 512],
                                              start=(kc == 0), stop=(kc == NKC - 1))
                        pe.inc(ii)
                        if idx >= 1:
                            t.wait_ge(ve.h, ve_hh_h[idx - 1])
                        for kc in range(NKC):
                            for half in range(2):
                                ii = t.matmul(ps_u[:, half * 512:(half + 1) * 512],
                                              euw[:, pslot * 2048 + kc * 128:pslot * 2048 + (kc + 1) * 128],
                                              t_sb[:, kc * 1024 + half * 512:kc * 1024 + (half + 1) * 512],
                                              start=(kc == 0), stop=(kc == NKC - 1))
                        pe.inc(ii)
                    pe_up_h[idx] = pe.n

                    @block.scalar
                    def _(s):
                        s.wait_ge(pe.h, pe_up_h[idx] - 1)
                        if idx >= 1:
                            s.wait_ge(ve.h, ve_t1_h[idx - 1])
                        se.inc(s.activation(out=silu_sb[:], in_=ps_g[:], func=AF.Silu))
                    se_sil_h[idx] = se.n

                    @block.vector
                    def _(v):
                        v.wait_ge(se.h, se_sil_h[idx])
                        v.wait_ge(ve.h, ve_cwb_h[gidx])
                        if idx >= 1:
                            v.wait_ge(ve.h, ve_hh_h[idx - 1])
                        ve.inc(v.tensor_mul(out=tmp_sb[:],
                                            in0=silu_sb[:],
                                            in1=cwb[:, slot * 1024:(slot + 1) * 1024]))
                        ve_t1_h[idx] = ve.n
                        v.wait_ge(ve.h, ve.n)
                        v.wait_ge(pe.h, pe_up_h[idx])
                        hsl = slice((ei * NIC + ic) * 1024, (ei * NIC + ic + 1) * 1024)
                        ve.inc(v.tensor_mul(out=hh[:, hsl], in0=tmp_sb[:], in1=ps_u[:]))
                    ve_hh_h[idx] = ve.n

            ve_hh_done = ve.n

            for dc in range(NDC):
                dslot = dc % 2

                @block.gpsimd
                def _(g):
                    if dc >= 2:
                        g.wait_ge(pe.h, pe_dn_h[dc - 2])
                    g.wait_ge(el.h, el.n)
                    for ei, e in enumerate(es):
                        g.dma_start(out=edw[:, (dslot * 2 + ei) * 1408:(dslot * 2 + ei + 1) * 1408],
                                    in_=edT[e, dc]).then_inc(el.h, 16)
                el.n += 32
                el_dc = el.n

                dps = ps_d[dc % 2]

                @block.tensor
                def _(t):
                    t.wait_ge(el.h, el_dc)
                    t.wait_ge(ve.h, ve_hh_done)
                    if dc >= 2:
                        t.wait_ge(ve.h, ve_racc_h[dc - 2])
                    elif grp > 0 and dc == 0:
                        t.wait_ge(ve.h, ve_cwb_h[2 * grp])
                    for ei in range(2):
                        for ic in range(NIC):
                            for half in range(2):
                                ii = t.matmul(
                                    dps[:, half * 512:(half + 1) * 512],
                                    edw[:, (dslot * 2 + ei) * 1408 + ic * 128:(dslot * 2 + ei) * 1408 + (ic + 1) * 128],
                                    hh[:, (ei * NIC + ic) * 1024 + half * 512:(ei * NIC + ic) * 1024 + (half + 1) * 512],
                                    start=(ei == 0 and ic == 0), stop=(ei == 1 and ic == NIC - 1))
                    pe.inc(ii)
                pe_dn_h[dc] = pe.n

                @block.vector
                def _(v):
                    v.wait_ge(pe.h, pe_dn_h[dc])
                    if grp == 0:
                        ii = v.tensor_copy(out=racc[:, dc * 1024:(dc + 1) * 1024], in_=dps[:])
                    else:
                        ii = v.tensor_add(out=racc[:, dc * 1024:(dc + 1) * 1024],
                                          in0=racc[:, dc * 1024:(dc + 1) * 1024], in1=dps[:])
                    ve.inc(ii)
                ve_racc_h[dc] = ve.n
                last_racc_ve[0] = ve.n

        x2s = env.t[x2s_name]
        s_f = env.t[s_name]
        xo = env.t[xown_name]
        x1 = env.t[x1_name]
        ve_moe_done = ve.n
        ve_ep_prev = [ve.n]
        for dc in range(NDC):
            for half in range(2):
                colg = th * 1024 + half * 512
                eslot = 0

                @block.sync
                def _(sy):
                    sy.wait_ge(ve.h, ve_ep_prev[0])
                    sy.wait_ge(tl.h, tl.n)
                    tl.dma(sy.dma_start(out=ep_x2s[:, eslot * 512:(eslot + 1) * 512],
                                        in_=x2s[dc * 128:(dc + 1) * 128, colg:colg + 512]))
                    tl.dma(sy.dma_start(out=ep_s[:, eslot * 512:(eslot + 1) * 512],
                                        in_=s_f[dc * 128:(dc + 1) * 128, colg:colg + 512]))
                    tl.dma(sy.dma_start(out=ep_x[:, eslot * 512:(eslot + 1) * 512],
                                        in_=xo[dc * 128:(dc + 1) * 128, colg:colg + 512]))
                tl_ep = tl.n

                @block.vector
                def _(v):
                    v.wait_ge(tl.h, tl_ep)
                    v.wait_ge(ve.h, ve_moe_done)
                    ve.inc(v.tensor_add(out=ep_f[:, eslot * 512:(eslot + 1) * 512],
                                        in0=racc[:, dc * 1024 + half * 512:dc * 1024 + (half + 1) * 512],
                                        in1=ep_x2s[:, eslot * 512:(eslot + 1) * 512]))
                ve_f = ve.n

                @block.scalar
                def _(s):
                    s.wait_ge(ve.h, ve_f)
                    se.inc(s.activation(out=ep_f2[:, eslot * 512:(eslot + 1) * 512],
                                        in_=ep_f[:, eslot * 512:(eslot + 1) * 512],
                                        func=AF.Silu))
                se_sil2 = se.n

                @block.vector
                def _(v):
                    v.wait_ge(se.h, se_sil2)
                    v.wait_ge(st.h, st.n)
                    ve.inc(v.tensor_mul(out=ep_o[:, eslot * 512:(eslot + 1) * 512],
                                        in0=ep_f2[:, eslot * 512:(eslot + 1) * 512],
                                        in1=ep_s[:, eslot * 512:(eslot + 1) * 512]))
                    v.wait_ge(ve.h, ve.n)
                    ve.inc(v.tensor_add(out=ep_o[:, eslot * 512:(eslot + 1) * 512],
                                        in0=ep_o[:, eslot * 512:(eslot + 1) * 512],
                                        in1=ep_x[:, eslot * 512:(eslot + 1) * 512]))
                ve_o = ve.n
                ve_ep_prev[0] = ve_o

                @block.sync
                def _(sy):
                    sy.wait_ge(ve.h, ve_o)
                    st.dma(sy.dma_start(out=x1[dc * 128:(dc + 1) * 128, colg:colg + 512],
                                        in_=ep_o[:, eslot * 512:(eslot + 1) * 512]))

    @block.sync
    def _(sy):
        sy.wait_ge(st.h, st.n)
        if done_sem is not None:
            done_sem.inc(sy.nop())
    return st



def emit_gla_gk(env, block, pre, x_name, out_name, wait_sems=(), done_sem=None):
    nc = env.nc
    x = env.t[x_name]
    w1 = env.t["l1_wkg1"]
    w2 = env.t["l1_wkg2"]
    bk = env.t["l1_bkg2"]
    gk_out = env.t[out_name]
    w1_sb = env.sb(pre + "_w1", [128, NKC * 16], BF16)
    w2_sb = env.sb(pre + "_w2", [16, 512], BF16)
    bk_sb = env.sb(pre + "_bk", [128, 4], F32)
    xsb = env.sb(pre + "_x", [128, NKC * 512], BF16)
    z1_sb = env.sb(pre + "_z1", [16, 512], BF16)
    z2_sb = env.sb(pre + "_z2", [128, 4 * 512], F32)
    ab_sb = env.sb(pre + "_ab", [128, 512], F32)
    ln_sb = env.sb(pre + "_ln", [128, 512], F32)
    mn_sb = env.sb(pre + "_mn", [128, 512], F32)
    gk_sb = env.sb(pre + "_gk", [128, 4 * 512], F32)
    ps_z1 = env.ps(pre + "_psz1", [16, 512])
    ps_z2 = env.ps(pre + "_psz2", [128, 512])

    wl = env.sem(pre + "wl")
    ld = env.sem(pre + "ld")
    pe = env.sem(pre + "pe")
    ve = env.sem(pre + "ve")
    se = env.sem(pre + "se")
    st = env.sem(pre + "st")

    @block.gpsimd
    def _(g):
        for _ws in wait_sems:
            _ws.wait(g)
        for kc in range(NKC):
            wl.dma(g.dma_start(out=w1_sb[:, kc * 16:(kc + 1) * 16],
                               in_=w1[kc * 128:(kc + 1) * 128, :]))
        wl.dma(g.dma_start(out=w2_sb[:], in_=w2[:, :]))

    @block.sync
    def _(sy):
        for _ws in wait_sems:
            _ws.wait(sy)
        ld.dma(sy.dma_start(out=bk_sb[:], in_=bk[:, :]))

    for tb in range(NTB):
        se_prev = se.n
        ve_prev = ve.n
        st_prev = st.n

        @block.sync
        def _(sy):
            if tb > 0:
                sy.wait_ge(pe.h, pe.n)
            for kc in range(NKC):
                ld.dma(sy.dma_start(out=xsb[:, kc * 512:(kc + 1) * 512],
                                    in_=x[kc * 128:(kc + 1) * 128, tb * 512:(tb + 1) * 512]))
        ld_tb = ld.n

        @block.tensor
        def _(t):
            if tb == 0:
                wl.wait(t)
                t.wait_ge(ld.h, ld_tb)
            else:
                t.wait_ge(ld.h, ld_tb)
                t.wait_ge(ve.h, ve_prev)
            for kc in range(NKC):
                ii = t.matmul(ps_z1[:], w1_sb[:, kc * 16:(kc + 1) * 16],
                              xsb[:, kc * 512:(kc + 1) * 512],
                              start=(kc == 0), stop=(kc == NKC - 1))
            pe.inc(ii)
        pe_z1 = pe.n

        @block.vector
        def _(v):
            v.wait_ge(pe.h, pe_z1)
            ve.inc(v.tensor_copy(out=z1_sb[:], in_=ps_z1[:]))
        ve_z1 = ve.n

        for mo in range(4):
            @block.tensor
            def _(t):
                t.wait_ge(ve.h, ve_z1)
                if tb > 0 or mo > 0:
                    t.wait_ge(ve.h, ve.n)
                pe.inc(t.matmul(ps_z2[:], w2_sb[:, mo * 128:(mo + 1) * 128], z1_sb[:],
                                start=True, stop=True))
            pe_z2 = pe.n

            msl = slice(mo * 512, (mo + 1) * 512)

            @block.vector
            def _(v):
                v.wait_ge(pe.h, pe_z2)
                v.wait_ge(ld.h, ld_tb)
                if tb > 0:
                    v.wait_ge(st.h, st_prev)
                ve.inc(v.tensor_scalar(out=z2_sb[:, msl], in0=ps_z2[:],
                                       scalar1=bk_sb[:, mo:mo + 1], scalar2=None,
                                       op0=ALU.add))
            ve_z2 = ve.n

            @block.scalar
            def _(s):
                s.wait_ge(ve.h, ve_z2)
                se.inc(s.activation(out=ab_sb[:], in_=z2_sb[:, msl], func=AF.Abs))
                s.wait_ge(se.h, se.n)
                se.inc(s.activation(out=ab_sb[:], in_=ab_sb[:], func=AF.Exp, scale=-1.0))
                s.wait_ge(se.h, se.n)
                se.inc(s.activation(out=ln_sb[:], in_=ab_sb[:], func=AF.Ln, bias=1.0))
            se_ln = se.n

            @block.vector
            def _(v):
                v.wait_ge(se.h, se_ln)
                msl = slice(mo * 512, (mo + 1) * 512)
                n0 = [ve.n]

                def syn(ii):
                    ve.inc(ii)
                    v.wait_ge(ve.h, ve.n)
                syn(v.tensor_scalar_min(out=mn_sb[:], in0=z2_sb[:, msl], scalar1=0.0))
                syn(v.tensor_sub(out=mn_sb[:], in0=mn_sb[:], in1=ln_sb[:]))
                syn(v.tensor_scalar(out=gk_sb[:, msl], in0=mn_sb[:], scalar1=1.0 / 16.0,
                                    scalar2=None, op0=ALU.mult))
            ve_gk = ve.n

            @block.sync
            def _(sy):
                sy.wait_ge(ve.h, ve_gk)
                st.dma(sy.dma_start(out=gk_out[mo * 128:(mo + 1) * 128,
                                               tb * 512:(tb + 1) * 512],
                                    in_=gk_sb[:, mo * 512:(mo + 1) * 512]))
        if tb == NTB - 1:
            @block.sync
            def _(sy):
                sy.wait_ge(st.h, st.n)
                if done_sem is not None:
                    done_sem.inc(sy.nop())
    return st


def emit_gla_vg(env, block, pre, x_name, v_out, g_out, wait_sems=(), done_sem=None):
    nc = env.nc
    x = env.t[x_name]
    wv, wg = env.t["l1_wv"], env.t["l1_wg"]
    bg = env.t["l1_bg"]
    vt, gt = env.t[v_out], env.t[g_out]
    wv_sb = env.sb(pre + "_wv", [128, NKC * 1024], BF16)
    wg_sb = env.sb(pre + "_wg", [128, NKC * 1024], BF16)
    bg_sb = env.sb(pre + "_bg", [1, 1024], BF16)
    onesr = env.sb(pre + "_or", [1, 128], BF16)
    xsb = env.sb(pre + "_x", [128, 2 * NKC * 512], BF16)
    vsb = env.sb(pre + "_v", [128, 8 * 512], BF16)
    gsb = env.sb(pre + "_g", [128, 8 * 512], F32)
    psv = [env.ps(pre + f"_psv{i}", [128, 512]) for i in range(2)]
    psg = [env.ps(pre + f"_psg{i}", [128, 512]) for i in range(2)]

    wl = env.sem(pre + "wl")
    ld = env.sem(pre + "ld")
    pe = env.sem(pre + "pe")
    ve = env.sem(pre + "ve")
    se = env.sem(pre + "se")
    st = env.sem(pre + "st")

    ms = env.sem(pre + "ms")

    @block.gpsimd
    def _(g):
        for _ws in wait_sems:
            _ws.wait(g)
        ms.inc(g.memset(onesr[:], 1.0))
        for kc in range(NKC):
            wl.dma(g.dma_start(out=wv_sb[:, kc * 1024:(kc + 1) * 1024],
                               in_=wv[kc * 128:(kc + 1) * 128, :]))
            wl.dma(g.dma_start(out=wg_sb[:, kc * 1024:(kc + 1) * 1024],
                               in_=wg[kc * 128:(kc + 1) * 128, :]))
        wl.dma(g.dma_start(out=bg_sb[:], in_=bg[:, :]))

    first = [True]
    for tb in range(NTB):
        buf = tb % 2
        xoff = buf * NKC * 512
        se_prev = se.n
        ve_prev = ve.n
        st_prev = st.n
        pe_now = pe.n

        @block.sync
        def _(sy):
            if first[0]:
                for _ws in wait_sems:
                    _ws.wait(sy)
                first[0] = False
            if tb >= 2:
                sy.wait_ge(pe.h, pe_now - 16)
            if tb >= 1:
                sy.wait_ge(ld.h, ld.n)
            for kc in range(NKC):
                ld.dma(sy.dma_start(out=xsb[:, xoff + kc * 512:xoff + (kc + 1) * 512],
                                    in_=x[kc * 128:(kc + 1) * 128, tb * 512:(tb + 1) * 512]))
        ld_tb = ld.n
        pe_base = pe.n

        @block.tensor
        def _(t):
            if tb == 0:
                wl.wait(t)
                ms.wait(t)
            t.wait_ge(ld.h, ld_tb)
            gidx = 0
            for tt in range(4):
                for dvb in range(2):
                    if gidx >= 2:
                        t.wait_ge(se.h, se_prev + (gidx - 2) + 1)
                    else:
                        t.wait_ge(se.h, se_prev)
                    for kc in range(NKC):
                        ii = t.matmul(psv[gidx % 2][:],
                                      xsb[:, xoff + kc * 512 + tt * 128:xoff + kc * 512 + (tt + 1) * 128],
                                      wv_sb[:, kc * 1024 + dvb * 512:kc * 1024 + (dvb + 1) * 512],
                                      start=(kc == 0), stop=(kc == NKC - 1))
                    pe.inc(ii)
                    if gidx >= 2:
                        t.wait_ge(ve.h, ve_prev + (gidx - 2) + 1)
                    else:
                        t.wait_ge(ve.h, ve_prev)
                    for kc in range(NKC):
                        t.matmul(psg[gidx % 2][:],
                                 xsb[:, xoff + kc * 512 + tt * 128:xoff + kc * 512 + (tt + 1) * 128],
                                 wg_sb[:, kc * 1024 + dvb * 512:kc * 1024 + (dvb + 1) * 512],
                                 start=(kc == 0), stop=False)
                    ii = t.matmul(psg[gidx % 2][:], onesr[:],
                                  bg_sb[:, dvb * 512:(dvb + 1) * 512],
                                  start=False, stop=True)
                    pe.inc(ii)
                    gidx += 1

        @block.scalar
        def _(s):
            gidx = 0
            for tt in range(4):
                for dvb in range(2):
                    s.wait_ge(pe.h, pe_base + 2 * gidx + 1)
                    if tb > 0 and gidx == 0:
                        s.wait_ge(st.h, st_prev)
                    ii = s.activation(out=vsb[:, gidx * 512:(gidx + 1) * 512],
                                      in_=psv[gidx % 2][:], func=AF.Copy)
                    se.inc(ii)
                    gidx += 1

        @block.vector
        def _(v):
            gidx = 0
            for tt in range(4):
                for dvb in range(2):
                    v.wait_ge(pe.h, pe_base + 2 * gidx + 2)
                    if tb > 0 and gidx == 0:
                        v.wait_ge(st.h, st_prev)
                    ve.inc(v.tensor_copy(out=gsb[:, gidx * 512:(gidx + 1) * 512],
                                         in_=psg[gidx % 2][:]))
                    gidx += 1

        se_tb = se.n
        ve_tb = ve.n

        @block.sync
        def _(sy):
            sy.wait_ge(se.h, se_tb)
            sy.wait_ge(ve.h, ve_tb)
            gidx = 0
            for tt in range(4):
                for dvb in range(2):
                    rows = slice(tb * 512 + tt * 128, tb * 512 + (tt + 1) * 128)
                    cols = slice(dvb * 512, (dvb + 1) * 512)
                    st.dma(sy.dma_start(out=vt[rows, cols],
                                        in_=vsb[:, gidx * 512:(gidx + 1) * 512]))
                    st.dma(sy.dma_start(out=gt[rows, cols],
                                        in_=gsb[:, gidx * 512:(gidx + 1) * 512]))
                    gidx += 1
            if tb == NTB - 1:
                sy.wait_ge(st.h, st.n)
                if done_sem is not None:
                    done_sem.inc(sy.nop())
    return st



def emit_gla_core(env, block, pre, q_name, k_name, gk_name, v_name, g_name,
                  o_name, nheads=4, wait_sems=(), done_sem=None):
    nc = env.nc
    NCH = T // 128
    H = nheads
    qf, kf, gkf = env.t[q_name], env.t[k_name], env.t[gk_name]
    vt, gt, of = env.t[v_name], env.t[g_name], env.t[o_name]
    HB = H * 128
    HV = H * 256
    qs = env.sb(pre + "_q", [128, HB], BF16)
    ks = env.sb(pre + "_k", [128, HB], BF16)
    gks = env.sb(pre + "_gk", [128, HB], F32)
    vs = env.sb(pre + "_v", [128, HV], BF16)
    gs = env.sb(pre + "_g", [128, HV], F32)
    qs2 = env.sb(pre + "_q2", [128, HB], BF16)
    ks2 = env.sb(pre + "_k2", [128, HB], BF16)
    gks2 = env.sb(pre + "_gk2", [128, HB], F32)
    vs2 = env.sb(pre + "_v2", [128, HV], BF16)
    gs2 = env.sb(pre + "_g2", [128, HV], F32)
    gkc = env.sb(pre + "_gkc", [128, HB], F32)
    e1 = env.sb(pre + "_e1", [128, HB], F32)
    e2 = env.sb(pre + "_e2", [128, HB], F32)
    e3 = env.sb(pre + "_e3", [128, HB], F32)
    qd = env.sb(pre + "_qd", [128, HB], BF16)
    kk = env.sb(pre + "_kk", [128, HB], BF16)
    kd = env.sb(pre + "_kd", [128, HB], BF16)
    kdT = env.sb(pre + "_kdT", [128, HB], BF16)
    amask = env.sb(pre + "_am", [128, HB], BF16)
    tril = env.sb(pre + "_tril", [128, 128], BF16)
    ident = env.sb(pre + "_id", [128, 128], BF16)
    S_f32 = env.sb(pre + "_S", [128, HV], F32)
    S_bf = env.sb(pre + "_Sb", [128, HV], BF16)
    sqs = env.sb(pre + "_sq", [128, HV], F32)
    stat = env.sb(pre + "_st", [128, H * 8], F32)
    on_sb = env.sb(pre + "_on", [128, HV], F32)
    gsil = env.sb(pre + "_gs", [128, HV], F32)
    ofin = env.sb(pre + "_ofin", [128, HV], BF16)
    oT = env.sb(pre + "_oT", [128, HV], BF16)
    ps_a = env.ps(pre + "_psa", [128, HB])
    ps_o = env.ps(pre + "_pso", [128, HV])
    ps_s2 = env.ps(pre + "_pss2", [128, HV])
    ps_t = env.ps(pre + "_pst", [128, HB], BF16)
    ps_t2 = env.ps(pre + "_pst2", [128, HV], BF16)

    ld = env.sem(pre + "ld")
    pe = env.sem(pre + "pe")
    ve = env.sem(pre + "ve")
    se = env.sem(pre + "se")
    st = env.sem(pre + "st")
    ge = env.sem(pre + "ge")

    @block.gpsimd
    def _(g):
        for _ws in wait_sems:
            _ws.wait(g)
        g.memset(S_f32[:], 0.0)
        ge.inc(g.memset(S_bf[:], 0.0))
    ge_init = ge.n

    @block.sync
    def _(sy):
        for _ws in wait_sems:
            _ws.wait(sy)
        ld.dma(sy.dma_start(out=tril[:], in_=env.t["mask4"][:, 0:128]))
        ld.dma(sy.dma_start(out=ident[:], in_=env.t["ident"][:, :]))

    P = {}
    for n in range(NCH):
        bufs = (qs, ks, gks, vs, gs) if n % 2 == 0 else (qs2, ks2, gks2, vs2, gs2)
        qb, kb, gkb, vb, gb = bufs

        @block.sync
        def _(sy):
            if n >= 2:
                sy.wait_ge(pe.h, P[("pe_all", n - 2)])
                sy.wait_ge(ve.h, P[("ve_all", n - 2)])
                sy.wait_ge(se.h, P[("se_all", n - 2)])
            if n >= 1:
                sy.wait_ge(ld.h, ld.n)
            for h in range(H):
                ld.dma(sy.dma_start(out=qb[:, h * 128:(h + 1) * 128],
                                    in_=qf[h * 128:(h + 1) * 128, n * 128:(n + 1) * 128]))
                ld.dma(sy.dma_start(out=kb[:, h * 128:(h + 1) * 128],
                                    in_=kf[h * 128:(h + 1) * 128, n * 128:(n + 1) * 128]))
                ld.dma(sy.dma_start(out=gkb[:, h * 128:(h + 1) * 128],
                                    in_=gkf[h * 128:(h + 1) * 128, n * 128:(n + 1) * 128]))
            ld.dma(sy.dma_start(out=vb[:], in_=vt[n * 128:(n + 1) * 128, :]))
            ld.dma(sy.dma_start(out=gb[:], in_=gt[n * 128:(n + 1) * 128, :]))
        ld_n = ld.n

        def hq(h):
            return slice(h * 128, (h + 1) * 128)

        def hv_(h):
            return slice(h * 256, (h + 1) * 256)

        @block.vector
        def _(v):
            v.wait_ge(ld.h, ld_n)
            if n > 0:
                v.wait_ge(se.h, P[("se_e", n - 1)])
            for h in range(H):
                ve.inc(v.tensor_tensor_scan(out=gkc[:, hq(h)], data0=gkb[:, hq(h)],
                                            data1=gkb[:, hq(h)], initial=0.0,
                                            op0=ALU.add, op1=ALU.bypass))
        P[("ve_scan", n)] = ve.n

        @block.scalar
        def _(s):
            if n == 0:
                s.wait_ge(ge.h, ge_init)
            s.wait_ge(ve.h, P[("ve_scan", n)])
            if n > 0:
                s.wait_ge(ve.h, P[("ve_S", n - 1)])
                s.wait_ge(ve.h, P[("ve_dx", n - 1)])
            for h in range(H):
                s.activation(out=e1[:, hq(h)], in_=gkc[:, hq(h)], func=AF.Exp)
                s.activation(out=e2[:, hq(h)], in_=gkc[:, hq(h)], func=AF.Exp, scale=-1.0)
                ii = s.activation(out=e3[:, hq(h)], in_=gkc[:, hq(h)], func=AF.Exp,
                                  scale=-1.0, bias=gkc[:, h * 128 + 127:h * 128 + 128])
            se.inc(ii)
        P[("se_e", n)] = se.n

        @block.vector
        def _(v):
            v.wait_ge(se.h, P[("se_e", n)])
            if n > 0:
                v.wait_ge(pe.h, P[("pe_o", n - 1)])
            for h in range(H):
                v.tensor_mul(out=qd[:, hq(h)], in0=qb[:, hq(h)], in1=e1[:, hq(h)])
                v.tensor_mul(out=kk[:, hq(h)], in0=kb[:, hq(h)], in1=e2[:, hq(h)])
                ii = v.tensor_mul(out=kd[:, hq(h)], in0=kb[:, hq(h)], in1=e3[:, hq(h)])
            ve.inc(ii)
        P[("ve_dx", n)] = ve.n

        @block.tensor
        def _(t):
            t.wait_ge(ve.h, P[("ve_dx", n)])
            if n > 0:
                t.wait_ge(ve.h, P[("ve_kdT", n - 1)])
            for h in range(H):
                ii = t.transpose(ps_t[:, hq(h)], kd[:, hq(h)], ident[:])
            pe.inc(ii)
        P[("pe_t", n)] = pe.n

        @block.vector
        def _(v):
            v.wait_ge(pe.h, P[("pe_t", n)])
            ii = v.tensor_copy(out=kdT[:], in_=ps_t[:])
            ve.inc(ii)
        P[("ve_kdT", n)] = ve.n

        @block.tensor
        def _(t):
            if n > 0:
                t.wait_ge(ve.h, P[("ve_am", n - 1)])
            for h in range(H):
                ii = t.matmul(ps_a[:, hq(h)], kk[:, hq(h)], qd[:, hq(h)],
                              start=True, stop=True)
            pe.inc(ii)
        P[("pe_a", n)] = pe.n

        @block.vector
        def _(v):
            v.wait_ge(pe.h, P[("pe_a", n)])
            for h in range(H):
                ii = v.tensor_mul(out=amask[:, hq(h)], in0=ps_a[:, hq(h)], in1=tril[:])
            ve.inc(ii)
        P[("ve_am", n)] = ve.n

        @block.tensor
        def _(t):
            t.wait_ge(ve.h, P[("ve_am", n)])
            if n > 0:
                t.wait_ge(ve.h, P[("ve_on", n - 1)])
                t.wait_ge(se.h, P[("se_Sb", n - 1)])
                t.wait_ge(ve.h, P[("ve_S", n - 1)])
            for h in range(H):
                t.matmul(ps_o[:, hv_(h)], qd[:, hq(h)], S_bf[:, hv_(h)],
                         start=True, stop=False)
                t.matmul(ps_o[:, hv_(h)], amask[:, hq(h)], vb[:, hv_(h)],
                         start=False, stop=True)
                ii = t.matmul(ps_s2[:, hv_(h)], kdT[:, hq(h)], vb[:, hv_(h)],
                              start=True, stop=True)
            pe.inc(ii)
        P[("pe_o", n)] = pe.n

        @block.vector
        def _(v):
            v.wait_ge(pe.h, P[("pe_o", n)])
            for h in range(H):
                ii = v.scalar_tensor_tensor(out=S_f32[:, hv_(h)], in0=S_f32[:, hv_(h)],
                                            scalar=e1[:, h * 128 + 127:h * 128 + 128],
                                            in1=ps_s2[:, hv_(h)],
                                            op0=ALU.mult, op1=ALU.add)
            ve.inc(ii)
        P[("ve_S", n)] = ve.n

        @block.scalar
        def _(s):
            s.wait_ge(ve.h, P[("ve_S", n)])
            ii = s.activation(out=S_bf[:], in_=S_f32[:], func=AF.Copy)
            se.inc(ii)
        P[("se_Sb", n)] = se.n

        @block.vector
        def _(v):
            v.wait_ge(pe.h, P[("pe_o", n)])
            for h in range(H):
                ii = v.reduce_sum(out=stat[:, h * 8:h * 8 + 1], in_=ps_o[:, hv_(h)], axis=AX)
            ve.inc(ii)
        P[("ve_s1", n)] = ve.n

        @block.scalar
        def _(s):
            s.wait_ge(pe.h, P[("pe_o", n)])
            if n > 0:
                s.wait_ge(ve.h, P[("ve_var", n - 1)])
            ii = s.activation(out=sqs[:], in_=ps_o[:], func=AF.Square)
            se.inc(ii)
        P[("se_sq", n)] = se.n

        @block.vector
        def _(v):
            def syn(ii):
                ve.inc(ii)
                v.wait_ge(ve.h, ve.n)
            v.wait_ge(se.h, P[("se_sq", n)])
            v.wait_ge(ve.h, P[("ve_s1", n)])
            for h in range(H):
                v.reduce_sum(out=stat[:, h * 8 + 1:h * 8 + 2], in_=sqs[:, hv_(h)], axis=AX)
            syn(v.engine_nop())
            for h in range(H):
                v.tensor_scalar(out=stat[:, h * 8:h * 8 + 1], in0=stat[:, h * 8:h * 8 + 1],
                                scalar1=1.0 / 256, scalar2=None, op0=ALU.mult)
                v.tensor_scalar(out=stat[:, h * 8 + 1:h * 8 + 2],
                                in0=stat[:, h * 8 + 1:h * 8 + 2],
                                scalar1=1.0 / 256, scalar2=1e-5, op0=ALU.mult, op1=ALU.add)
            syn(v.engine_nop())
            for h in range(H):
                v.tensor_mul(out=stat[:, h * 8 + 2:h * 8 + 3], in0=stat[:, h * 8:h * 8 + 1],
                             in1=stat[:, h * 8:h * 8 + 1])
            syn(v.engine_nop())
            for h in range(H):
                v.tensor_sub(out=stat[:, h * 8 + 1:h * 8 + 2],
                             in0=stat[:, h * 8 + 1:h * 8 + 2],
                             in1=stat[:, h * 8 + 2:h * 8 + 3])
            syn(v.engine_nop())
        P[("ve_var", n)] = ve.n

        @block.scalar
        def _(s):
            s.wait_ge(ve.h, P[("ve_var", n)])
            for h in range(H):
                ii = s.activation(out=stat[:, h * 8 + 3:h * 8 + 4],
                                  in_=stat[:, h * 8 + 1:h * 8 + 2], func=AF.Ln)
            se.inc(ii)
        P[("se_lnv", n)] = se.n

        @block.vector
        def _(v):
            v.wait_ge(se.h, P[("se_lnv", n)])
            for h in range(H):
                ii = v.tensor_scalar(out=stat[:, h * 8 + 3:h * 8 + 4],
                                     in0=stat[:, h * 8 + 3:h * 8 + 4],
                                     scalar1=-0.5, scalar2=None, op0=ALU.mult)
            ve.inc(ii)
        P[("ve_l2", n)] = ve.n

        @block.scalar
        def _(s):
            s.wait_ge(ve.h, P[("ve_l2", n)])
            for h in range(H):
                s.activation(out=stat[:, h * 8 + 4:h * 8 + 5],
                             in_=stat[:, h * 8 + 3:h * 8 + 4], func=AF.Exp)
            if n > 0:
                s.wait_ge(ve.h, P[("ve_on", n - 1)])
            ii = s.activation(out=gsil[:], in_=gb[:], func=AF.Silu)
            se.inc(ii)
        P[("se_rstd", n)] = se.n

        @block.vector
        def _(v):
            v.wait_ge(se.h, P[("se_rstd", n)])
            if n > 0:
                v.wait_ge(pe.h, P[("pe_oT", n - 1)])
            for h in range(H):
                v.tensor_scalar(out=on_sb[:, hv_(h)], in0=ps_o[:, hv_(h)],
                                scalar1=stat[:, h * 8:h * 8 + 1],
                                scalar2=stat[:, h * 8 + 4:h * 8 + 5],
                                op0=ALU.subtract, op1=ALU.mult)
            ve.inc(v.engine_nop())
            v.wait_ge(ve.h, ve.n)
            ii = v.tensor_mul(out=ofin[:], in0=on_sb[:], in1=gsil[:])
            ve.inc(ii)
        P[("ve_on", n)] = ve.n

        @block.tensor
        def _(t):
            t.wait_ge(ve.h, P[("ve_on", n)])
            if n > 0:
                t.wait_ge(ve.h, P[("ve_oT", n - 1)])
            for h in range(H):
                t.transpose(ps_t2[:, h * 256:h * 256 + 128], ofin[:, h * 256:h * 256 + 128],
                            ident[:])
                ii = t.transpose(ps_t2[:, h * 256 + 128:h * 256 + 256],
                                 ofin[:, h * 256 + 128:h * 256 + 256], ident[:])
            pe.inc(ii)
        P[("pe_oT", n)] = pe.n

        @block.vector
        def _(v):
            v.wait_ge(pe.h, P[("pe_oT", n)])
            v.wait_ge(st.h, st.n)
            ii = v.tensor_copy(out=oT[:], in_=ps_t2[:])
            ve.inc(ii)
        P[("ve_oT", n)] = ve.n

        @block.sync
        def _(sy):
            sy.wait_ge(ve.h, P[("ve_oT", n)])
            for h in range(H):
                st.dma(sy.dma_start(out=of[h * 256:h * 256 + 128, n * 128:(n + 1) * 128],
                                    in_=oT[:, h * 256:h * 256 + 128]))
                st.dma(sy.dma_start(out=of[h * 256 + 128:h * 256 + 256,
                                           n * 128:(n + 1) * 128],
                                    in_=oT[:, h * 256 + 128:h * 256 + 256]))
        P[("pe_all", n)] = pe.n
        P[("ve_all", n)] = ve.n
        P[("se_all", n)] = se.n

    @block.sync
    def _(sy):
        sy.wait_ge(st.h, st.n)
        if done_sem is not None:
            done_sem.inc(sy.nop())
    return st



def emit_l1ffn_wr(env, block, pre, wait_sems=(), done_sem=None):
    nc = env.nc
    wT, rT = env.t["l1f_wT"], env.t["l1f_rT"]
    wrd = env.t["wr_bf"]
    wld = env.sb(pre + "_wld", [128, 2 * 2048], F32)
    rld = env.sb(pre + "_rld", [128, 2 * 2048], F32)
    tq1 = env.sb(pre + "_tq1", [128, 2 * 2048], F32)
    wrt = env.sb(pre + "_wrt", [128, 2 * 2048], BF16)
    tq2 = env.sb(pre + "_tq2", [128, 2048], F32)
    wl = env.sem(pre + "wl")
    ve = env.sem(pre + "ve")
    st = env.sem(pre + "st")
    ve_wr_h = {}
    st_h = {}
    for kc in range(NKC):
        slot = kc % 2

        @block.gpsimd
        def _(g):
            for _ws in wait_sems:
                if kc == 0:
                    _ws.wait(g)
            if kc >= 2:
                g.wait_ge(ve.h, ve_wr_h[kc - 2])
            g.wait_ge(wl.h, wl.n)
            wl.dma(g.dma_start(out=wld[:, slot * 2048:(slot + 1) * 2048],
                               in_=wT[kc * 128:(kc + 1) * 128, :]))
            wl.dma(g.dma_start(out=rld[:, slot * 2048:(slot + 1) * 2048],
                               in_=rT[kc * 128:(kc + 1) * 128, :]))
        wl_kc = wl.n

        @block.vector
        def _(v):
            v.wait_ge(wl.h, wl_kc)
            if kc >= 2:
                v.wait_ge(st.h, st_h[kc - 2])
            wsl = slice(slot * 2048, (slot + 1) * 2048)

            def syn(ii):
                ve.inc(ii)
                v.wait_ge(ve.h, ve.n)
            v.tensor_scalar(out=tq1[:, wsl], in0=wld[:, wsl], scalar1=0.2, scalar2=0.6,
                            op0=ALU.is_gt, op1=ALU.mult)
            syn(v.tensor_scalar(out=tq2[:], in0=wld[:, wsl], scalar1=-0.2, scalar2=0.6,
                                op0=ALU.is_lt, op1=ALU.mult))
            syn(v.tensor_sub(out=tq1[:, wsl], in0=tq1[:, wsl], in1=tq2[:]))
            syn(v.tensor_add(out=wrt[:, wsl], in0=tq1[:, wsl], in1=rld[:, wsl]))
        ve_wr_h[kc] = ve.n

        @block.sync
        def _(sy):
            sy.wait_ge(ve.h, ve_wr_h[kc])
            sy.wait_ge(st.h, st.n)
            st.dma(sy.dma_start(out=wrd[kc * 128:(kc + 1) * 128, :],
                                in_=wrt[:, slot * 2048:(slot + 1) * 2048]))
        st_h[kc] = st.n

    @block.sync
    def _(sy):
        sy.wait_ge(st.h, st.n)
        if done_sem is not None:
            done_sem.inc(sy.nop())
    return None


def emit_l1ffn(env, block, pre, xn1f_name, s1_name, x1_name, out_name,
               n=TH, wait_sems=(), done_sem=None):
    nc = env.nc
    xn1f = env.t[xn1f_name]
    s1d, x1d = env.t[s1_name], env.t[x1_name]
    outd = env.t[out_name]
    wrd = env.t["wr_bf"]
    wr_sb = env.sb(pre + "_wr", [128, NKC * D], BF16)
    ve_wr_done = 0
    xldf = env.sb(pre + "_xld", [128, NKC * 512], F32)
    xq = env.sb(pre + "_xq", [128, NKC * 512], BF16)
    m1 = env.sb(pre + "_m1", [128, 512], F32)
    m2 = env.sb(pre + "_m2", [128, 512], F32)
    g1 = env.sb(pre + "_g1", [128, 2 * 512], F32)
    es = env.sb(pre + "_es", [128, 2 * 512], F32)
    ex = env.sb(pre + "_ex", [128, 2 * 512], F32)
    eo = env.sb(pre + "_eo", [128, NKC * 512], F32)
    ps = [env.ps(pre + f"_ps{i}", [128, 512]) for i in range(4)]
    wl2 = env.sem(pre + "ld")
    ld = env.sem(pre + "ld")
    rl = env.sem(pre + "rl")
    pe = env.sem(pre + "pe")
    ve = env.sem(pre + "ve")
    se = env.sem(pre + "se")
    st = env.sem(pre + "st")

    @block.sync
    def _(sy):
        for _ws in wait_sems:
            _ws.wait(sy)
        for kc in range(NKC):
            wl2.dma(sy.dma_start(out=wr_sb[:, kc * D:(kc + 1) * D],
                                 in_=wrd[kc * 128:(kc + 1) * 128, :]))

    for tb in range(n // 512):
        se_prev = se.n
        ve_prev = ve.n
        st_base = st.n
        rl_base = rl.n

        @block.sync
        def _(sy):
            if tb == 0:
                for _ws in wait_sems:
                    _ws.wait(sy)
                sy.wait_ge(ve.h, ve_wr_done)
            else:
                sy.wait_ge(pe.h, pe.n)
            for kc in range(NKC):
                ld.dma(sy.dma_start(out=xldf[:, kc * 512:(kc + 1) * 512],
                                    in_=xn1f[kc * 128:(kc + 1) * 128,
                                             tb * 512:(tb + 1) * 512]))
        ld_tb = ld.n

        @block.vector
        def _(v):
            v.wait_ge(ld.h, ld_tb)
            v.wait_ge(ve.h, ve_wr_done)
            for kc in range(NKC):
                xsl = slice(kc * 512, (kc + 1) * 512)
                v.wait_ge(ve.h, ve.n)
                v.tensor_scalar(out=m1[:], in0=xldf[:, xsl], scalar1=0.33, scalar2=0.5,
                                op0=ALU.is_gt, op1=ALU.mult)
                ve.inc(v.tensor_scalar(out=m2[:], in0=xldf[:, xsl], scalar1=-0.33,
                                       scalar2=0.5, op0=ALU.is_lt, op1=ALU.mult))
                v.wait_ge(ve.h, ve.n)
                ve.inc(v.tensor_sub(out=xq[:, xsl], in0=m1[:], in1=m2[:]))
        ve_xq = ve.n
        ve_ep_base = ve.n
        pe_base = pe.n

        @block.tensor
        def _(t):
            if tb == 0:
                wl2.wait(t)
            t.wait_ge(ve.h, ve_xq)
            for mo in range(NKC):
                bank = ps[mo % 4]
                if mo >= 4:
                    t.wait_ge(se.h, se_prev + (mo - 4) + 1)
                else:
                    t.wait_ge(se.h, se_prev)
                for kc in range(NKC):
                    ii = t.matmul(bank[:], wr_sb[:, kc * D + mo * 128:kc * D + (mo + 1) * 128],
                                  xq[:, kc * 512:(kc + 1) * 512],
                                  start=(kc == 0), stop=(kc == NKC - 1))
                pe.inc(ii)

        @block.scalar
        def _(s):
            for mo in range(NKC):
                s.wait_ge(pe.h, pe_base + mo + 1)
                if mo >= 2:
                    s.wait_ge(ve.h, ve_ep_base + 2 * (mo - 1))
                se.inc(s.activation(out=g1[:, (mo % 2) * 512:(mo % 2 + 1) * 512],
                                    in_=ps[mo % 4][:], func=AF.Silu))

        @block.gpsimd
        def _(g):
            for mo in range(0, NKC, 2):
                if tb == 0 and mo == 0:
                    for _ws in wait_sems:
                        _ws.wait(g)
                if mo >= 2:
                    g.wait_ge(ve.h, ve_ep_base + 2 * mo)
                elif tb > 0:
                    g.wait_ge(ve.h, ve_prev)
                g.wait_ge(rl.h, rl.n)
                for m2 in (mo, mo + 1):
                    rl.dma(g.dma_start(out=es[:, (m2 % 2) * 512:(m2 % 2 + 1) * 512],
                                       in_=s1d[m2 * 128:(m2 + 1) * 128, tb * 512:(tb + 1) * 512]))
                    rl.dma(g.dma_start(out=ex[:, (m2 % 2) * 512:(m2 % 2 + 1) * 512],
                                       in_=x1d[m2 * 128:(m2 + 1) * 128, tb * 512:(tb + 1) * 512]))
        rl_tb = rl.n

        @block.vector
        def _(v):
            v.wait_ge(st.h, st_base)
            for mo in range(NKC):
                v.wait_ge(se.h, se_prev + mo + 1)
                v.wait_ge(rl.h, rl_base + 64 * (mo // 2 + 1))
                sl2 = slice((mo % 2) * 512, (mo % 2 + 1) * 512)
                ve.inc(v.tensor_mul(out=eo[:, mo * 512:(mo + 1) * 512],
                                    in0=g1[:, sl2], in1=es[:, sl2]))
                v.wait_ge(ve.h, ve.n)
                ve.inc(v.tensor_add(out=eo[:, mo * 512:(mo + 1) * 512],
                                    in0=eo[:, mo * 512:(mo + 1) * 512], in1=ex[:, sl2]))

        @block.sync
        def _(sy):
            for mo in range(NKC):
                sy.wait_ge(ve.h, ve_ep_base + 2 * (mo + 1))
                st.dma(sy.dma_start(out=outd[mo * 128:(mo + 1) * 128,
                                             tb * 512:(tb + 1) * 512],
                                    in_=eo[:, mo * 512:(mo + 1) * 512]))
            if tb == n // 512 - 1:
                sy.wait_ge(st.h, st.n)
                if done_sem is not None:
                    done_sem.inc(sy.nop())
    return st



def _inp(env, name, shape, dt):
    env.dram(name, shape, dt, kind="ExternalInput")


def _out(env, name, shape, dt):
    env.dram(name, shape, dt, kind="ExternalOutput")


def build_L1():
    nc = bass.Bass("TRN2", target_bir_lowering=False)
    env = Env(nc)
    with env.ctx:
        _inp(env, "xT", [D, T], F32)
        _inp(env, "ident", [128, 128], BF16)
        _inp(env, "cosT", [64, T], F32)
        _inp(env, "sinT", [64, T], F32)
        _inp(env, "mask4", [128, 2048], BF16)
        for w in ("l0_wq", "l0_wk", "l0_wv", "l0f_wq", "l0f_wk", "l0f_wv"):
            _inp(env, w, [D, 1024], F32)
        _inp(env, "l0f_ln1", [128, NKC], F32)
        _out(env, "xn_f32", [D, T], F32)
        _out(env, "y", [1024, T], BF16)
        _out(env, "y2", [1024, T], BF16)
        env.dram("xn_bf", [D, T], BF16)
        env.dram("h_bf", [D, T], BF16)
        for nm in ("q", "k", "q2", "k2"):
            env.dram(nm, [1024, T], BF16)
        for nm in ("v", "v2"):
            env.dram(nm, [T, 1024], BF16)
        with nc.Block() as block:
            d0 = env.sem("d0"); d1 = env.sem("d1"); d2 = env.sem("d2")
            d3 = env.sem("d3"); d4 = env.sem("d4"); d5 = env.sem("d5")
            with env.phase():
                emit_ln(env, block, "ln0", "xT", dst_bf="xn_bf", dst_f32="xn_f32",
                        done_sem=d0)
            with env.phase():
                emit_qkv_rope(env, block, "qkv0", "xn_bf", "l0_wq", "l0_wk", "l0_wv",
                              "q", "k", "v", rope=True, wait_sems=(d0,), done_sem=d1)
            with env.phase():
                emit_attn(env, block, "at0", "q", "k", "v", "y", 128.0 ** -0.5,
                          extra_recip_scale=1.0 / 3.0, wait_sems=(d1,), done_sem=d2)
            with env.phase():
                emit_ln(env, block, "rms1", "xn_f32", dst_bf="h_bf", eps=1e-6,
                        weight="l0f_ln1", rms=True, wait_sems=(d2,), done_sem=d3)
            with env.phase():
                emit_qkv_rope(env, block, "qkv1", "h_bf", "l0f_wq", "l0f_wk", "l0f_wv",
                              "q2", "k2", "v2", rope=False, wait_sems=(d3,), done_sem=d4)
            with env.phase():
                emit_attn(env, block, "at1", "q2", "k2", "v2", "y2", 128.0 ** -0.5,
                          extra_recip_scale=1.0, wait_sems=(d4,), done_sem=d5)
    return nc


def build_L2():
    nc = bass.Bass("TRN2", target_bir_lowering=False)
    env = Env(nc)
    with env.ctx:
        _inp(env, "xn_own", [D, TH], F32)
        _inp(env, "y_own", [D, TH], BF16)
        _inp(env, "y2_own", [D, TH], BF16)
        _inp(env, "xT_own", [D, TH], F32)
        _inp(env, "l0_wo", [D, D], F32)
        _inp(env, "l0f_wo", [D, D], F32)
        _inp(env, "l0f_ln2", [128, NKC], F32)
        _inp(env, "l0f_gate", [D, 16], F32)
        _inp(env, "l0f_egT", [16, 11, 128, 2048], F32)
        _inp(env, "l0f_euT", [16, 11, 128, 2048], F32)
        _inp(env, "l0f_edT", [16, 16, 128, 1408], F32)
        _inp(env, "l0f_sgT", [22, 128, D], F32)
        _inp(env, "l0f_suT", [22, 128, D], F32)
        _inp(env, "l0f_sdT", [2816, D], F32)
        _inp(env, "ident", [128, 128], BF16)
        _inp(env, "ident32", [128, 128], F32)
        _out(env, "x1_f32", [D, TH], F32)
        _out(env, "xn1_bf", [D, TH], BF16)
        _out(env, "xn1_f32", [D, TH], F32)
        env.dram("s_f32", [D, TH], F32)
        env.dram("x2_f32", [D, TH], F32)
        env.dram("t_bf", [D, TH], BF16)
        env.dram("t_f32", [D, TH], F32)
        env.dram("cwT", [16, TH], F32)
        env.dram("shs_bf", [2816, TH], BF16)
        env.dram("x2s_f32", [D, TH], F32)
        with nc.Block() as block:
            ds = [env.sem(f"d{i}") for i in range(8)]
            with env.phase():
                emit_mm_fm(env, block, "wo0", "y_own", "l0_wo", TH, out_f32="s_f32",
                           done_sem=ds[0])
            with env.phase():
                emit_mm_fm(env, block, "wo1", "y2_own", "l0f_wo", TH,
                           residual="xn_own", out_f32="x2_f32",
                           wait_sems=(ds[0],), done_sem=ds[1])
            with env.phase():
                emit_ln(env, block, "rms2", "x2_f32", dst_bf="t_bf", dst_f32="t_f32",
                        eps=1e-6, weight="l0f_ln2", rms=True, n_tb=NTBH,
                        wait_sems=(ds[1],), done_sem=ds[2])
            with env.phase():
                emit_gate(env, block, "gate", "t_f32", "l0f_gate", "cwT", TH,
                          wait_sems=(ds[2],), done_sem=ds[3])
            with env.phase():
                emit_glu(env, block, "glu", "t_bf", "l0f_sgT", "l0f_suT", "shs_bf", TH,
                         wait_sems=(ds[3],), done_sem=ds[4])
            with env.phase():
                emit_mm_fm(env, block, "sdn", "shs_bf", "l0f_sdT", TH, din=2816,
                           residual="x2_f32", out_f32="x2s_f32",
                           wait_sems=(ds[4],), done_sem=ds[5])
            with env.phase():
                emit_moe(env, block, "moe", "t_bf", "cwT", "x2s_f32", "s_f32",
                         "xT_own", "x1_f32", wait_sems=(ds[5],), done_sem=ds[6])
            with env.phase():
                emit_ln(env, block, "ln1", "x1_f32", dst_bf="xn1_bf", dst_f32="xn1_f32",
                        n_tb=NTBH, wait_sems=(ds[6],), done_sem=ds[7])
    return nc


def build_L3():
    nc = bass.Bass("TRN2", target_bir_lowering=False)
    env = Env(nc)
    with env.ctx:
        _inp(env, "xn1_full", [D, T], BF16)
        _inp(env, "l1_wq", [D, 512], F32)
        _inp(env, "l1_wk", [D, 512], F32)
        _inp(env, "l1_wkg1", [D, 16], F32)
        _inp(env, "l1_wkg2", [16, 512], F32)
        _inp(env, "l1_bkg2", [128, 4], F32)
        _inp(env, "l1_wv", [D, 1024], F32)
        _inp(env, "l1_wg", [D, 1024], F32)
        _inp(env, "l1_bg", [1, 1024], F32)
        _inp(env, "ident", [128, 128], BF16)
        _inp(env, "mask4", [128, 2048], BF16)
        _out(env, "o", [1024, T], BF16)
        env.dram("gq", [512, T], BF16)
        env.dram("gkk", [512, T], BF16)
        env.dram("gk", [512, T], F32)
        env.dram("gv", [T, 1024], BF16)
        env.dram("gg", [T, 1024], F32)
        with nc.Block() as block:
            ds = [env.sem(f"d{i}") for i in range(5)]
            with env.phase():
                emit_mm_fm(env, block, "glq", "xn1_full", "l1_wq", T, dout=512,
                           out_bf="gq", done_sem=ds[0])
            with env.phase():
                emit_mm_fm(env, block, "glk", "xn1_full", "l1_wk", T, dout=512,
                           out_bf="gkk", evac_scale=1.0 / 16.0,
                           wait_sems=(ds[0],), done_sem=ds[1])
            with env.phase():
                emit_gla_gk(env, block, "ggk", "xn1_full", "gk",
                            wait_sems=(ds[1],), done_sem=ds[2])
            with env.phase():
                emit_gla_vg(env, block, "gvg", "xn1_full", "gv", "gg",
                            wait_sems=(ds[2],), done_sem=ds[3])
            with env.phase():
                emit_gla_core(env, block, "gc", "gq", "gkk", "gk", "gv", "gg", "o",
                              wait_sems=(ds[3],), done_sem=ds[4])
    return nc


def build_L4():
    nc = bass.Bass("TRN2", target_bir_lowering=False)
    env = Env(nc)
    with env.ctx:
        _inp(env, "o_own", [D, TH], BF16)
        _inp(env, "l1_wo", [D, D], F32)
        _inp(env, "xn1_f32", [D, TH], F32)
        _inp(env, "x1_f32", [D, TH], F32)
        _inp(env, "l1f_wT", [D, D], F32)
        _inp(env, "l1f_rT", [D, D], F32)
        _out(env, "out", [D, TH], F32)
        env.dram("s1_f32", [D, TH], F32)
        env.dram("wr_bf", [D, D], BF16)
        with nc.Block() as block:
            ds = [env.sem(f"d{i}") for i in range(3)]
            with env.phase():
                emit_l1ffn_wr(env, block, "ffw", done_sem=ds[2])
            with env.phase():
                emit_mm_fm(env, block, "s1", "o_own", "l1_wo", TH, out_f32="s1_f32",
                           wait_sems=(ds[2],), done_sem=ds[0])
            with env.phase():
                emit_l1ffn(env, block, "ff", "xn1_f32", "s1_f32", "x1_f32", "out",
                           wait_sems=(ds[0],), done_sem=ds[1])
    return nc


def kernel(**inputs):
    if FUSED:
        return kernel_fused(inputs)
    cores = host_prep(inputs)
    f32 = np.float32

    nc1 = build_L1()
    keys1 = ["xT", "ident", "cosT", "sinT", "mask4", "l0_wq", "l0_wk", "l0_wv",
             "l0f_wq", "l0f_wk", "l0f_wv", "l0f_ln1"]
    maps1 = [{k: cores[c][k] for k in keys1} for c in range(8)]
    r1 = run_bass_kernel_spmd(nc1, maps1, core_ids=list(range(8))).results

    maps2 = []
    for c in range(8):
        b, j = c // 2, c % 2
        own = slice(j * TH, (j + 1) * TH)
        y_own = np.concatenate([r1[2 * b]["y"][:, own], r1[2 * b + 1]["y"][:, own]], axis=0)
        y2_own = np.concatenate([r1[2 * b]["y2"][:, own], r1[2 * b + 1]["y2"][:, own]], axis=0)
        m = {"xn_own": np.ascontiguousarray(r1[c]["xn_f32"][:, own]),
             "y_own": np.ascontiguousarray(y_own),
             "y2_own": np.ascontiguousarray(y2_own),
             "xT_own": np.ascontiguousarray(cores[c]["xT"][:, own])}
        for k in ("l0_wo", "l0f_wo", "l0f_ln2", "l0f_gate", "l0f_egT", "l0f_euT",
                  "l0f_edT", "l0f_sgT", "l0f_suT", "l0f_sdT", "ident", "ident32"):
            m[k] = cores[c][k]
        maps2.append(m)
    nc2 = build_L2()
    r2 = run_bass_kernel_spmd(nc2, maps2, core_ids=list(range(8))).results

    maps3 = []
    for c in range(8):
        b = c // 2
        xn1_full = np.concatenate([r2[2 * b]["xn1_bf"], r2[2 * b + 1]["xn1_bf"]], axis=1)
        m = {"xn1_full": np.ascontiguousarray(xn1_full)}
        for k in ("l1_wq", "l1_wk", "l1_wkg1", "l1_wkg2", "l1_bkg2", "l1_wv",
                  "l1_wg", "l1_bg", "ident", "mask4"):
            m[k] = cores[c][k]
        maps3.append(m)
    nc3 = build_L3()
    r3 = run_bass_kernel_spmd(nc3, maps3, core_ids=list(range(8))).results

    maps4 = []
    for c in range(8):
        b, j = c // 2, c % 2
        own = slice(j * TH, (j + 1) * TH)
        o_own = np.concatenate([r3[2 * b]["o"][:, own], r3[2 * b + 1]["o"][:, own]], axis=0)
        m = {"o_own": np.ascontiguousarray(o_own),
             "xn1_f32": r2[c]["xn1_f32"], "x1_f32": r2[c]["x1_f32"]}
        for k in ("l1_wo", "l1f_wT", "l1f_rT"):
            m[k] = cores[c][k]
        maps4.append(m)
    nc4 = build_L4()
    r4 = run_bass_kernel_spmd(nc4, maps4, core_ids=list(range(8))).results

    out = np.empty((4, T, D), f32)
    for c in range(8):
        b, j = c // 2, c % 2
        out[b, j * TH:(j + 1) * TH, :] = r4[c]["out"].T
    return out



def _gvals(env, g):
    if not hasattr(env, "_gv"):
        pid = g.partition_id()
        j = g.compute_val(pid % 2)
        pj = g.compute_val(pid + 1 - 2 * (pid % 2))
        env._gv = {
            "0": 0,
            "j2048": g.compute_val(j * 2048),
            "oj2048": g.compute_val(2048 - j * 2048),
            "nj2048": g.compute_val(2048 - j * 2048),
            "j1024TH": g.compute_val(j * (1024 * TH)),
            "nj1024TH": g.compute_val((1 - j) * (1024 * TH)),
            "p1024TH": g.compute_val(pj * (1024 * TH)),
            "pDTH": g.compute_val(pj * (D * TH)),
            "nj2048c": g.compute_val(2048 - j * 2048),
        }
    return env._gv


def emit_copy_dyn(env, block, pre, copies, wait_sems=(), done_sem=None):
    nc = env.nc
    cp = env.sem(pre + "cp")

    @block.gpsimd
    def _(g):
        for _ws in wait_sems:
            _ws.wait(g)
        gv = _gvals(env, g)
        for (dst, dk_, src, sk_, rows, cols, srs, drs) in copies:
            cp.dma(g.dma_start(
                out=bass.AP(env.t[dst].tensor, gv[dk_], [[drs, rows], [1, cols]]),
                in_=bass.AP(env.t[src].tensor, gv[sk_], [[srs, rows], [1, cols]])))
        cp.wait(g)
        if done_sem is not None:
            done_sem.inc(g.nop())
    return cp


def emit_allgather(env, block, pre, pairs, wait_sems=(), done_sem=None,
                   block_until_done=True):
    nc = env.nc
    cc = env.sem(pre + "cc")

    @block.gpsimd
    def _(g):
        for _ws in wait_sems:
            _ws.wait(g)
        for contrib, gath in pairs:
            cc.inc(g.collective_compute(
                "AllGather", mybir.AluOpType.bypass,
                replica_groups=[[0, 1, 2, 3, 4, 5, 6, 7]],
                ins=[env.t[contrib][:, :].opt()], outs=[env.t[gath][:, :].opt()]))
        if block_until_done:
            cc.wait(g)
        if done_sem is not None:
            done_sem.inc(g.nop())
    return cc


def build_fused():
    nc = bass.Bass("TRN2", target_bir_lowering=False)
    env = Env(nc)
    with env.ctx:
        _inp(env, "xT", [D, T], F32)
        _inp(env, "xT_own", [D, TH], F32)
        _inp(env, "ident", [128, 128], BF16)
        _inp(env, "ident32", [128, 128], F32)
        _inp(env, "cosT", [64, T], F32)
        _inp(env, "sinT", [64, T], F32)
        _inp(env, "mask4", [128, 2048], BF16)
        for w in ("l0_wq", "l0_wk", "l0_wv", "l0f_wq", "l0f_wk", "l0f_wv"):
            _inp(env, w, [D, 1024], F32)
        _inp(env, "l0f_ln1", [128, NKC], F32)
        _inp(env, "l0f_ln2", [128, NKC], F32)
        _inp(env, "l0_wo", [D, D], F32)
        _inp(env, "l0f_wo", [D, D], F32)
        _inp(env, "l0f_gate", [D, 16], F32)
        _inp(env, "l0f_egT", [16, 11, 128, 2048], F32)
        _inp(env, "l0f_euT", [16, 11, 128, 2048], F32)
        _inp(env, "l0f_edT", [16, 16, 128, 1408], F32)
        _inp(env, "l0f_sgT", [22, 128, D], F32)
        _inp(env, "l0f_suT", [22, 128, D], F32)
        _inp(env, "l0f_sdT", [2816, D], F32)
        _inp(env, "l1_wq", [D, 512], F32)
        _inp(env, "l1_wk", [D, 512], F32)
        _inp(env, "l1_wkg1", [D, 16], F32)
        _inp(env, "l1_wkg2", [16, 512], F32)
        _inp(env, "l1_bkg2", [128, 4], F32)
        _inp(env, "l1_wv", [D, 1024], F32)
        _inp(env, "l1_wg", [D, 1024], F32)
        _inp(env, "l1_bg", [1, 1024], F32)
        _inp(env, "l1_wo", [D, D], F32)
        _inp(env, "l1f_wT", [D, D], F32)
        _inp(env, "l1f_rT", [D, D], F32)
        _out(env, "out", [D, TH], F32)
        env.dram("xn_bf", [D, T], BF16)
        env.dram("xn_f32", [D, T], F32)
        env.dram("h_bf", [D, T], BF16)
        for nm in ("q", "k", "q2", "k2"):
            env.dram(nm, [1024, T], BF16)
        for nm in ("v", "v2"):
            env.dram(nm, [T, 1024], BF16)
        env.dram("y", [1024, T], BF16)
        env.dram("y2", [1024, T], BF16)
        env.dram("nb0", [128, 8], F32)
        env.dram("nb1", [128, 8], F32)
        env.dram("cy", [1024, TH], BF16)
        env.dram("cy2", [1024, TH], BF16)
        env.dram("gy", [8 * 1024, TH], BF16)
        env.dram("gy2", [8 * 1024, TH], BF16)
        env.dram("y_own", [D, TH], BF16)
        env.dram("y2_own", [D, TH], BF16)
        env.dram("xn_own", [D, TH], F32)
        env.dram("s_f32", [D, TH], F32)
        env.dram("x2_f32", [D, TH], F32)
        env.dram("t_bf", [D, TH], BF16)
        env.dram("t_f32", [D, TH], F32)
        env.dram("cwT", [16, TH], F32)
        env.dram("shs_bf", [2816, TH], BF16)
        env.dram("x2s_f32", [D, TH], F32)
        env.dram("x1_f32", [D, TH], F32)
        env.dram("xn1_bf", [D, TH], BF16)
        env.dram("xn1_f32", [D, TH], F32)
        env.dram("gxn1", [8 * D, TH], BF16)
        env.dram("xn1_full", [D, T], BF16)
        env.dram("gq", [512, T], BF16)
        env.dram("gkk", [512, T], BF16)
        env.dram("gk", [512, T], F32)
        env.dram("gv", [T, 1024], BF16)
        env.dram("gg", [T, 1024], F32)
        env.dram("o", [1024, T], BF16)
        env.dram("co", [1024, TH], BF16)
        env.dram("go", [8 * 1024, TH], BF16)
        env.dram("o_own", [D, TH], BF16)
        env.dram("s1_f32", [D, TH], F32)
        env.dram("wr_bf", [D, D], BF16)

        with nc.Block() as block:
            ds = [env.sem(f"fd{i}") for i in range(26)]
            with env.phase():
                emit_ln(env, block, "ln0", "xT", dst_bf="xn_bf", dst_f32="xn_f32",
                        rms2_bf="h_bf", rms2_weight="l0f_ln1", done_sem=ds[0])
            with env.phase():
                emit_qkv_rope(env, block, "qkv0", "xn_bf", "l0_wq", "l0_wk", "l0_wv",
                              "q", "k", "v", rope=True, wait_sems=(ds[0],), done_sem=ds[1])
            dstab0 = env.sem("dstab0")
            with env.phase():
                emit_stab(env, block, "sb0", "q", "k", "nb0", 128.0 ** -0.5,
                          wait_sems=(ds[1],), done_sem=dstab0)
            with env.phase():
                emit_attn(env, block, "at0", "q", "k", "v", "y", 128.0 ** -0.5,
                          "nb0", extra_recip_scale=1.0 / 3.0,
                          wait_sems=(dstab0,), done_sem=ds[2])
            with env.phase():
                emit_qkv_rope(env, block, "qkv1", "h_bf", "l0f_wq", "l0f_wk", "l0f_wv",
                              "q2", "k2", "v2", rope=False, wait_sems=(ds[2],), done_sem=ds[4])
            dstab1 = env.sem("dstab1")
            with env.phase():
                emit_stab(env, block, "sb1", "q2", "k2", "nb1", 128.0 ** -0.5,
                          wait_sems=(ds[4],), done_sem=dstab1)
            with env.phase():
                emit_attn(env, block, "at1", "q2", "k2", "v2", "y2", 128.0 ** -0.5,
                          "nb1", extra_recip_scale=1.0,
                          wait_sems=(dstab1,), done_sem=ds[5])
            with env.phase():
                emit_copy_dyn(env, block, "exA", [
                    ("cy", "0", "y", "oj2048", 1024, TH, T, TH),
                    ("xn_own", "0", "xn_f32", "j2048", D, TH, T, TH),
                ], wait_sems=(ds[2],), done_sem=ds[6])
                emit_allgather(env, block, "agA", [("cy", "gy")],
                               wait_sems=(ds[6],), done_sem=ds[7],
                               block_until_done=False)
            with env.phase():
                emit_copy_dyn(env, block, "asA", [
                    ("y_own", "j1024TH", "y", "j2048", 1024, TH, T, TH),
                    ("y_own", "nj1024TH", "gy", "p1024TH", 1024, TH, TH, TH),
                ], wait_sems=(env.sem("agAcc"),), done_sem=ds[8])
                emit_copy_dyn(env, block, "exA2", [
                    ("cy2", "0", "y2", "oj2048", 1024, TH, T, TH),
                ], wait_sems=(ds[5],), done_sem=env.sem("fe2"))
                emit_allgather(env, block, "agA2", [("cy2", "gy2")],
                               wait_sems=(env.sem("fe2"),), done_sem=None,
                               block_until_done=False)
            with env.phase():
                emit_mm_fm(env, block, "wo0", "y_own", "l0_wo", TH, out_f32="s_f32",
                           wait_sems=(ds[8], ds[5]), done_sem=ds[9])
            with env.phase():
                emit_copy_dyn(env, block, "asA2", [
                    ("y2_own", "j1024TH", "y2", "j2048", 1024, TH, T, TH),
                    ("y2_own", "nj1024TH", "gy2", "p1024TH", 1024, TH, TH, TH),
                ], wait_sems=(env.sem("agAcc"),), done_sem=env.sem("fa2"))
            with env.phase():
                emit_mm_fm(env, block, "wo1", "y2_own", "l0f_wo", TH,
                           residual="xn_own", out_f32="x2_f32",
                           wait_sems=(ds[9], env.sem("fa2")), done_sem=ds[10])
            with env.phase():
                emit_ln(env, block, "rms2", "x2_f32", dst_bf="t_bf", dst_f32="t_f32",
                        eps=1e-6, weight="l0f_ln2", rms=True, n_tb=NTBH,
                        wait_sems=(ds[10],), done_sem=ds[11])
            with env.phase():
                emit_gate(env, block, "gate", "t_f32", "l0f_gate", "cwT", TH,
                          wait_sems=(ds[11],), done_sem=ds[12])
            with env.phase():
                emit_glu(env, block, "glu", "t_bf", "l0f_sgT", "l0f_suT", "shs_bf", TH,
                         wait_sems=(ds[12],), done_sem=ds[13])
            with env.phase():
                emit_mm_fm(env, block, "sdn", "shs_bf", "l0f_sdT", TH, din=2816,
                           residual="x2_f32", out_f32="x2s_f32",
                           wait_sems=(ds[13],), done_sem=ds[14])
            with env.phase():
                emit_moe(env, block, "moe", "t_bf", "cwT", "x2s_f32", "s_f32",
                         "xT_own", "x1_f32", wait_sems=(ds[14],), done_sem=ds[15])
            with env.phase():
                emit_ln(env, block, "ln1", "x1_f32", dst_bf="xn1_bf", dst_f32="xn1_f32",
                        n_tb=NTBH, wait_sems=(ds[15],), done_sem=ds[16])
            with env.phase():
                emit_allgather(env, block, "agB", [("xn1_bf", "gxn1")],
                               wait_sems=(ds[16],), done_sem=ds[17])
            with env.phase():
                emit_copy_dyn(env, block, "asB", [
                    ("xn1_full", "j2048", "xn1_bf", "0", D, TH, TH, T),
                    ("xn1_full", "nj2048c", "gxn1", "pDTH", D, TH, TH, T),
                ], wait_sems=(ds[17],), done_sem=ds[18])
            dwr = env.sem("dwr")
            outer_scope = env.phase()
            outer_scope.__enter__()
            emit_l1ffn_wr(env, block, "ffw", wait_sems=(ds[18],), done_sem=dwr)
            with env.phase():
                emit_mm_fm(env, block, "glq", "xn1_full", "l1_wq", T, dout=512,
                           out_bf="gq", wait_sems=(ds[18],), done_sem=ds[19])
            with env.phase():
                emit_mm_fm(env, block, "glk", "xn1_full", "l1_wk", T, dout=512,
                           out_bf="gkk", evac_scale=1.0 / 16.0,
                           wait_sems=(ds[19],), done_sem=ds[20])
            with env.phase():
                emit_gla_gk(env, block, "ggk", "xn1_full", "gk",
                            wait_sems=(ds[20],), done_sem=ds[21])
            with env.phase():
                emit_gla_vg(env, block, "gvg", "xn1_full", "gv", "gg",
                            wait_sems=(ds[21],), done_sem=ds[22])
            with env.phase():
                emit_gla_core(env, block, "gc", "gq", "gkk", "gk", "gv", "gg", "o",
                              wait_sems=(ds[22],), done_sem=ds[23])
            outer_scope.__exit__(None, None, None)
            with env.phase():
                emit_copy_dyn(env, block, "exC", [
                    ("co", "0", "o", "oj2048", 1024, TH, T, TH),
                ], wait_sems=(ds[23],), done_sem=ds[24])
            with env.phase():
                emit_allgather(env, block, "agC", [("co", "go")],
                               wait_sems=(ds[24],), done_sem=ds[25])
            dsx = env.sem("fdx")
            dsy = env.sem("fdy")
            with env.phase():
                emit_copy_dyn(env, block, "asC", [
                    ("o_own", "j1024TH", "o", "j2048", 1024, TH, T, TH),
                    ("o_own", "nj1024TH", "go", "p1024TH", 1024, TH, TH, TH),
                ], wait_sems=(ds[25],), done_sem=dsx)
            with env.phase():
                emit_mm_fm(env, block, "s1", "o_own", "l1_wo", TH, out_f32="s1_f32",
                           wait_sems=(dsx, dwr), done_sem=dsy)
            with env.phase():
                emit_l1ffn(env, block, "ff", "xn1_f32", "s1_f32", "x1_f32", "out",
                           wait_sems=(dsy, dwr))
    return nc


FUSED = True


def kernel_fused(inputs):
    cores = host_prep(inputs)
    for c in range(8):
        j = c % 2
        cores[c]["xT_own"] = np.ascontiguousarray(cores[c]["xT"][:, j * TH:(j + 1) * TH])
    nc = build_fused()
    keys = ["xT", "xT_own", "ident", "ident32", "cosT", "sinT", "mask4",
            "l0_wq", "l0_wk", "l0_wv", "l0f_wq", "l0f_wk", "l0f_wv",
            "l0f_ln1", "l0f_ln2", "l0_wo", "l0f_wo", "l0f_gate",
            "l0f_egT", "l0f_euT", "l0f_edT", "l0f_sgT", "l0f_suT", "l0f_sdT",
            "l1_wq", "l1_wk", "l1_wkg1", "l1_wkg2", "l1_bkg2", "l1_wv", "l1_wg",
            "l1_bg", "l1_wo", "l1f_wT", "l1f_rT"]
    maps = [{k: cores[c][k] for k in keys} for c in range(8)]
    r = run_bass_kernel_spmd(nc, maps, core_ids=list(range(8))).results
    out = np.empty((4, T, D), np.float32)
    for c in range(8):
        b, j = c // 2, c % 2
        out[b, j * TH:(j + 1) * TH, :] = r[c]["out"].T
    return out
